# version 13
# baseline (speedup 1.0000x reference)
import numpy as np

import concourse.bass as bass
import concourse.mybir as mybir
from concourse.bass_utils import run_bass_kernel_spmd

F32 = mybir.dt.float32
BF16 = mybir.dt.bfloat16
I32 = mybir.dt.int32
AF = mybir.ActivationFunctionType
ALU = mybir.AluOpType
AX = mybir.AxisListType.X

H = 1024
HC = 8
ID = 2816
IC = 22
M = 256
DH = 64
HEADS = 16
NPAIR = 8
QMAX = 127.0
CSH = 16384.0
LN_EPS = 1e-5
KEPS = 1e-3
CDAT = float(DH) ** -0.25
MIS = float(M) ** -0.5


class Sem:
    def __init__(self, h):
        self.h = h
        self.n = 0


class B:
    def __init__(self, nc, block):
        self.nc = nc
        self.block = block

    def t(self, f):
        self.block.tensor(f)

    def v(self, f):
        cv = self.CV.n

        def w(e):
            e.wait_ge(self.CV.h, cv)
            f(e)
        self.block.vector(w)

    def v_raw(self, f):
        self.block.vector(f)

    def s(self, f):
        cs = self.CS.n

        def w(e):
            e.wait_ge(self.CS.h, cs)
            f(e)
        self.block.scalar(w)

    def g(self, f):
        self.block.gpsimd(f)

    def y(self, f):
        self.block.sync(f)

    def barrier(self):
        ph = self.PH
        sems = [self.D, self.DS, self.DG, self.DXA, self.DXB,
                self.GXA, self.GXB, self.SSA, self.SSB, self.DH, self.DHB, self.DW, self.DQ0, self.DQ1, self.GSA, self.GSB]
        vals = [s.n for s in sems]
        bar = self.sb["ebar"]

        def gdrain(e):
            for sm, v in zip(sems, vals):
                e.wait_ge(sm.h, v)
            e.memset(bar[0:1, 2:3], 0.0).then_inc(ph.h, 1)
        self.block.gpsimd(gdrain)
        self.block.sync(lambda e: e.nop().then_inc(ph.h, 1))
        self.block.scalar(lambda e: e.memzero(bar[0:1, 1:2]).then_inc(ph.h, 1))
        self.block.vector(lambda e: e.memset(bar[0:1, 0:1], 0.0).then_inc(ph.h, 1))
        self.block.tensor(lambda e: e.nop().then_inc(ph.h, 1))
        ph.n += 5
        tgt = ph.n
        for name in ("sync", "scalar", "vector", "tensor", "gpsimd"):
            def mk2():
                def f(e):
                    e.wait_ge(ph.h, tgt)
                return f
            getattr(self.block, name)(mk2())


def vstep(b, fn):
    def f(e):
        fn(e).then_inc(b.CV.h, 1)
    b.v(f)
    b.CV.n += 1
    return b.CV.n


def sstep(b, fn):
    def f(e):
        fn(e).then_inc(b.CS.h, 1)
    b.s(f)
    b.CS.n += 1
    return b.CS.n


def wait(b, eng, sem, val):
    def f(e):
        e.wait_ge(sem.h, val)
    getattr(b, eng)(f)


def build(R, dbg=False):
    assert R % 1024 == 0
    NCH = R // 128
    NQ = R // 1024

    nc = bass.Bass("TRN2", target_bir_lowering=False)

    io = {}
    io["x_in"] = nc.dram_tensor("x_in", [R, H], F32, kind="ExternalInput")
    for nm, shp in (("wq", [H, H]), ("wk", [H, H]), ("wv", [H, H]), ("wo", [H, H]),
                    ("bq", [1, H]), ("bk", [1, H]), ("bv", [1, H]), ("bo", [1, H]),
                    ("proj", [M, DH]), ("clip_x", [1, 1]), ("clip_down", [1, 1]),
                    ("w_gate", [ID, H]), ("w_up", [ID, H]), ("w_down", [H, ID])):
        io[nm] = nc.dram_tensor(nm, shp, F32, kind="ExternalInput")
    y_out = nc.dram_tensor("y", [R, H], F32, kind="ExternalOutput")
    dbg_t = {}
    if dbg:
        for nm, shp, dt in (("d_x2", [R, H], BF16), ("d_x3", [R, H], BF16),
                            ("d_ln2", [R, H], BF16), ("d_xq", [R, H], BF16),
                            ("d_h", [IC * 128, R], BF16),
                            ("d_trow", [1, 16], F32),
                            ("d_ctx", [65, HEADS * 256], F32),
                            ("d_qrow", [1, 1040], F32),
                            ("d_att", [R, H], BF16)):
            dbg_t[nm] = nc.dram_tensor(nm, shp, dt, kind="ExternalOutput")

    sd = {}
    for nm, shp, dt in (("s_x2", [R, H], BF16), ("s_x3", [R, H], BF16),
                        ("s_xq", [R, H], BF16), ("s_ln2", [R, H], BF16),
                        ("s_att", [R, H], BF16), ("s_h", [IC * 128, R], BF16),
                        ("s_wgn", [ID, H], BF16), ("s_wun", [ID, H], BF16),
                        ("s_wdn", [H, ID], BF16), ("s_wqn", [H, H], BF16),
                        ("s_wkn", [H, H], BF16), ("s_wvn", [H, H], BF16),
                        ("s_won", [H, H], BF16),
                        ("arm_i", [1, 1], F32), ("arm_o", [1, 1], F32),
                        ("arc_i", [1, 65 * HEADS * 256 + 1040], F32),
                        ("arc_o", [1, 65 * HEADS * 256 + 1040], F32),
                        ("s_pj", [64, 257], BF16)):
        sd[nm] = nc.dram_tensor(nm, shp, dt)

    A = nc.alloc_sbuf_tensor
    sb = {}
    sb["biga"] = A("biga", [128, IC * 1024], BF16)
    sb["bigb"] = A("bigb", [128, IC * 1024], BF16)
    sb["bigc"] = A("bigc", [128, IC * 1024], BF16)
    sb["bigd"] = A("bigd", [128, 11264], BF16)
    sb["xstg"] = A("xstg", [128, 2 * H], F32)
    sb["ostg"] = A("ostg", [128, 2 * H], F32)
    sb["ctx_sb"] = A("ctx_sb", [65, HEADS * 256], F32)
    sb["ctxT"] = A("ctxT", [128, HEADS * 130], BF16)
    sb["qrow"] = A("qrow", [1, 1040], F32)
    sb["eps_row"] = A("eps_row", [1, HEADS * 65], BF16)
    sb["identf"] = A("identf", [128, 128], F32)
    sb["identb"] = A("identb", [128, 128], BF16)
    sb["projc2"] = A("projc2", [128, 257], BF16)
    sb["nhalf2"] = A("nhalf2", [128, 257], BF16)
    sb["projsb"] = A("projsb", [128, 256], F32)
    sb["onecb"] = A("onecb", [128, 1], BF16)
    sb["onerb"] = A("onerb", [1, 128], BF16)
    sb["brow"] = A("brow", [1, 2 * H], BF16)
    sb["b_tab"] = A("b_tab", [128, 16], F32)
    sb["swg"] = A("swg", [128, IC], F32)
    sb["swu"] = A("swu", [128, IC], F32)
    sb["sgx"] = A("sgx", [128, IC], F32)
    sb["sux"] = A("sux", [128, IC], F32)
    sb["m_tab"] = A("m_tab", [128, NCH], F32)
    sb["v_tab"] = A("v_tab", [128, NCH], F32)
    sb["r_tab"] = A("r_tab", [128, NCH], F32)
    sb["xmx"] = A("xmx", [128, NCH], F32)
    sb["xmn"] = A("xmn", [128, NCH], F32)
    sb["rr_tab"] = A("rr_tab", [128, NCH], F32)
    sb["bb_tab"] = A("bb_tab", [128, NCH], F32)
    sb["bns"] = A("bns", [128, 12], F32)
    sb["agg"] = A("agg", [128, 2], F32)
    sb["red1"] = A("red1", [128, 1], F32)
    sb["dmax"] = A("dmax", [128, 1], F32)
    sb["dcol"] = A("dcol", [128, 4], F32)
    sb["pad"] = A("pad", [128, 8], F32)
    sb["trow"] = A("trow", [1, 16], F32)
    sb["ebar"] = A("ebar", [1, 4], F32)
    sb["ti1"] = A("ti1", [1, 1], I32)
    sb["sqt"] = A("sqt", [128, 1024], BF16)

    PSn = nc.alloc_psum_tensor
    psf = [PSn(f"psf{i}", [128, 512], F32) for i in range(7)]
    psb = PSn("psb", [128, 1024], BF16)

    semg = []

    def mksem(name):
        gd = nc.semaphore(name)
        h = gd.__enter__()
        semg.append(gd)
        return Sem(h)

    with nc.Block() as block:
        b = B(nc, block)
        b.io, b.sd, b.sb, b.psf, b.psb = io, sd, sb, psf, psb
        b.y_out = y_out
        b.R, b.NCH, b.NQ = R, NCH, NQ
        b.D = mksem("d_sync")
        b.DS = mksem("d_scal")
        b.DG = mksem("d_gps")
        b.DXA = mksem("d_xa")
        b.DXB = mksem("d_xb")
        b.GXA = mksem("d_ga")
        b.GXB = mksem("d_gb")
        b.SSA = mksem("d_sa")
        b.SSB = mksem("d_sb")
        b.DH = mksem("d_hl")
        b.DHB = mksem("d_hlb")
        b.DW = mksem("d_wc")
        b.DQ0 = mksem("d_q0")
        b.DQ1 = mksem("d_q1")
        b.GSA = mksem("d_gsa")
        b.GSB = mksem("d_gsb")
        b.dbg_t = dbg_t
        b.CV = mksem("c_vec")
        b.CS = mksem("c_sca")
        b.CT = mksem("c_ten")
        b.CG = mksem("c_gps")
        b.CC = mksem("c_cc")
        b.PH = mksem("c_ph")
        emit(b)

    for gd in reversed(semg):
        gd.__exit__(None, None, None)
    return nc



def colmax_to_trow(b, slot):
    cvn = b.CV.n
    ps = b.psf[6]

    def t(e):
        e.wait_ge(b.CV.h, cvn)
        e.transpose(ps.ap()[0:1, 0:128], b.sb["red1"][:, :], b.sb["identf"][:, :]) \
            .then_inc(b.CT.h, 1)
    b.t(t)
    b.CT.n += 1
    ctn = b.CT.n

    def v(e):
        e.wait_ge(b.CT.h, ctn)
        e.tensor_reduce(b.sb["trow"][0:1, slot:slot + 1], ps.ap()[0:1, 0:128],
                        axis=AX, op=ALU.max).then_inc(b.CV.h, 1)
    b.v(v)
    b.CV.n += 1


def armax(b, slot_in, slot_out):
    cvn = b.CV.n
    dg = b.DG

    def g1(e):
        e.wait_ge(b.CV.h, cvn)
        e.dma_start(out=b.sd["arm_i"].ap(),
                    in_=b.sb["trow"][0:1, slot_in:slot_in + 1]).then_inc(dg.h, 16)
    b.g(g1)
    dg.n += 16
    dgn = dg.n

    def g2(e):
        e.wait_ge(dg.h, dgn)
        e.collective_compute("AllReduce", ALU.max,
                             replica_groups=[[0, 1, 2, 3, 4, 5, 6, 7]],
                             ins=[b.sd["arm_i"].ap()], outs=[b.sd["arm_o"].ap()]) \
            .then_inc(b.CC.h, 1)
    b.g(g2)
    b.CC.n += 1
    ccn = b.CC.n

    def g3(e):
        e.wait_ge(b.CC.h, ccn)
        e.dma_start(out=b.sb["trow"][0:1, slot_out:slot_out + 1],
                    in_=b.sd["arm_o"].ap()).then_inc(dg.h, 16)
    b.g(g3)
    dg.n += 16
    dgn2 = dg.n

    def g4(e):
        e.wait_ge(dg.h, dgn2)
        e.nop().then_inc(b.CG.h, 1)
    b.g(g4)
    b.CG.n += 1


def bcast_exact(b, src_col, dst, dst_col, rows=128):
    dg = b.DG
    cvn = b.CV.n

    def g1(e):
        e.wait_ge(b.CV.h, cvn)
        e.dma_start(out=b.sd["arm_i"].ap(),
                    in_=b.sb["trow"][0:1, src_col:src_col + 1]).then_inc(dg.h, 16)
    b.g(g1)
    dg.n += 16
    dgn = dg.n

    def g2(e):
        e.wait_ge(dg.h, dgn)
        src = b.sd["arm_i"].ap()
        srcb = bass.AP(src.tensor, src.offset, [[0, rows], [1, 1]])
        e.dma_start(out=dst[0:rows, dst_col:dst_col + 1], in_=srcb).then_inc(dg.h, 16)
    b.g(g2)
    dg.n += 16
    dgn2 = dg.n

    def g3(e):
        e.wait_ge(dg.h, dgn2)
        e.nop().then_inc(b.CG.h, 1)
    b.g(g3)
    b.CG.n += 1



def emit(b):
    p_consts(b)
    p_startup(b)
    p_scales_x(b)
    p_xq(b, b.io["x_in"], "gpsimd")
    b.barrier()
    p_gateup(b)
    colmax_to_trow(b, 4)
    armax(b, 4, 5)
    p_scales_h(b)
    b.barrier()
    p_down(b, b.io["x_in"], b.sd["s_x2"], final=False, do_stats=True)
    p_ln_fin(b, amax=False)
    p_ln_apply(b, b.sd["s_x2"], b.sd["s_ln2"])
    b.barrier()
    p_sweep1(b)
    colmax_to_trow(b, 4)
    p_ar_ctx(b)
    b.barrier()
    p_ctx_fix(b)
    b.barrier()
    p_sweep2(b)
    b.barrier()
    p_wo(b, do_stats=True)
    p_ln_fin(b, amax=True)
    colmax_to_trow(b, 4)
    armax(b, 4, 5)
    p_scales_x(b)
    p_xq(b, b.sd["s_x3"], "gpsimd")
    b.barrier()
    p_gateup(b)
    colmax_to_trow(b, 4)
    armax(b, 4, 5)
    p_scales_h(b)
    b.barrier()
    p_down(b, b.sd["s_x3"], None, final=True)
    p_finish(b)


def p_consts(b):
    io, sb, sd = b.io, b.sb, b.sd

    def g0(e):
        e.memset(sb["identf"][:, :], 0.0)
        e.memset(sb["identb"][:, :], 0.0)
        e.memset(sb["onecb"][:, :], 1.0)
        e.memset(sb["onerb"][:, :], 1.0)
        e.memset(sb["nhalf2"][:, :], -0.5).then_inc(b.CG.h, 1)
    b.g(g0)
    b.CG.n += 1
    cg0 = b.CG.n

    def g(e):
        e.wait_ge(b.CG.h, cg0)
        e.affine_select(out=sb["identf"][:, :], in_=sb["identf"][:, :],
                        compare_op=ALU.not_equal, fill=1.0, base=0,
                        pattern=[[-1, 128]], channel_multiplier=1)
        e.affine_select(out=sb["identb"][:, :], in_=sb["identb"][:, :],
                        compare_op=ALU.not_equal, fill=1.0, base=0,
                        pattern=[[-1, 128]], channel_multiplier=1) \
            .then_inc(b.CG.h, 1)
    b.g(g)
    b.CG.n += 1
    cgn = b.CG.n

    def g2(e):
        e.dma_start(out=sd["s_wqn"].ap(), in_=io["wq"].ap()).then_inc(b.DW.h, 16)
        e.dma_start(out=sd["s_wkn"].ap(), in_=io["wk"].ap()).then_inc(b.DW.h, 16)
        e.dma_start(out=sd["s_wvn"].ap(), in_=io["wv"].ap()).then_inc(b.DW.h, 16)
        e.dma_start(out=sd["s_won"].ap(), in_=io["wo"].ap()).then_inc(b.DW.h, 16)
        e.dma_start(out=sb["brow"][0:1, 0:H], in_=io["bo"].ap()).then_inc(b.DW.h, 16)
        e.dma_start(out=sb["brow"][0:1, H:2 * H], in_=io["bv"].ap()).then_inc(b.DW.h, 16)
    b.g(g2)
    b.DW.n += 96
    b.dg_wqkv = b.DW.n

    def y1(e):
        e.dma_start(out=sb["projsb"][:, 0:64], in_=io["proj"][0:128, :]).then_inc(b.D.h, 16)
        e.dma_start(out=sb["projsb"][:, 64:128], in_=io["proj"][128:256, :]).then_inc(b.D.h, 16)
        e.dma_start(out=sb["b_tab"][:, 0:8],
                    in_=io["bq"].ap().rearrange("o (c p) -> (o p) c", p=128),
                    allow_slow_non_contiguous=True).then_inc(b.D.h, 16)
        e.dma_start(out=sb["b_tab"][:, 8:16],
                    in_=io["bk"].ap().rearrange("o (c p) -> (o p) c", p=128),
                    allow_slow_non_contiguous=True).then_inc(b.D.h, 16)
        e.dma_start(out=sb["trow"][0:1, 0:1], in_=io["clip_x"].ap()).then_inc(b.D.h, 16)
        e.dma_start(out=sb["trow"][0:1, 1:2], in_=io["clip_down"].ap()).then_inc(b.D.h, 16)
    b.y(y1)
    b.D.n += 96
    dn = b.D.n

    def t1(e):
        e.wait_ge(b.D.h, dn)
        e.wait_ge(b.CG.h, cgn)
        e.transpose(b.psf[6].ap()[0:64, 0:128], sb["projsb"][:, 0:64], sb["identf"][:, :])
        e.transpose(b.psf[6].ap()[0:64, 128:256], sb["projsb"][:, 64:128], sb["identf"][:, :]) \
            .then_inc(b.CT.h, 1)
    b.t(t1)
    b.CT.n += 1
    ctn = b.CT.n

    def v1(e):
        e.wait_ge(b.CT.h, ctn)
        e.tensor_scalar(sb["projc2"][0:64, 0:256], b.psf[6].ap()[0:64, 0:256],
                        CDAT, None, ALU.mult)
        e.memset(sb["projc2"][0:64, 256:257], 0.0).then_inc(b.CV.h, 1)
    b.v(v1)
    b.CV.n += 1
    cvp = b.CV.n

    def y2(e):
        e.wait_ge(b.CV.h, cvp)
        e.dma_start(out=sd["s_pj"].ap(), in_=sb["projc2"][0:64, 0:257]).then_inc(b.D.h, 16)
        e.wait_ge(b.D.h, dn + 16)
        e.dma_start(out=sb["projc2"][64:128, 0:257], in_=sd["s_pj"].ap()).then_inc(b.D.h, 16)
    b.y(y2)
    b.D.n += 32

    def s1(e):
        e.wait_ge(b.D.h, dn)
        e.activation(sb["trow"][0:1, 2:3], sb["trow"][0:1, 0:1], AF.Sigmoid)
        e.activation(sb["trow"][0:1, 3:4], sb["trow"][0:1, 1:2], AF.Sigmoid) \
            .then_inc(b.CS.h, 1)
    b.s(s1)
    b.CS.n += 1

    vstep(b, lambda e: e.memset(sb["trow"][0:1, 12:13], 0.7))
    vstep(b, lambda e: e.tensor_copy(sb["ti1"][0:1, 0:1], sb["trow"][0:1, 12:13]))
    vstep(b, lambda e: e.tensor_copy(sb["trow"][0:1, 13:14], sb["ti1"][0:1, 0:1]))
    vstep(b, lambda e: e.tensor_scalar(sb["trow"][0:1, 14:15], sb["trow"][0:1, 13:14],
                                       -0.5, CSH + 0.5, ALU.mult, ALU.add))
    bcast_exact(b, 14, sb["pad"], 5)


def p_wquant(b):
    io, sb, sd = b.io, b.sb, b.sd
    wait(b, "v", b.CG, b.CG.n)
    stgA = sb["bigd"].ap().bitcast(F32)
    stgB = sb["bigc"].ap().bitcast(F32)
    tmp = sb["biga"].ap().bitcast(F32)
    ti = sb["biga"].ap().bitcast(I32)
    out = sb["bigb"]

    jobs = []
    for w_in, w_out, wd, deq, tab in (
            (io["w_gate"], sd["s_wgn"], H, False, sb["swg"]),
            (io["w_up"], sd["s_wun"], H, False, sb["swu"]),
            (io["w_down"], sd["s_wdn"], ID, True, None)):
        for r0 in range(0, w_in.shape[0], 256):
            jobs.append((w_in, w_out, wd, deq, tab, r0))

    cons = {}
    stores = {}
    cv0 = b.CV.n
    for j, (w_in, w_out, wd, deq, tab, r0) in enumerate(jobs):
        stg = stgA if j % 2 == 0 else stgB
        dx = b.DXA if j % 2 == 0 else b.DXB

        def ld(e, j=j, w_in=w_in, wd=wd, r0=r0, stg=stg, dx=dx):
            e.wait_ge(b.CV.h, cons[j - 2] if j >= 2 else cv0)
            e.dma_start(out=stg[:, 0:2 * wd].rearrange("p (c h) -> p c h", c=2),
                        in_=w_in[r0:r0 + 256, 0:wd].rearrange(
                            "(c p) h -> p c h", p=128)).then_inc(dx.h, 16)
        b.y(ld)
        dx.n += 16
        dj = dx.n
        ci = r0 // 128

        def v1(e, dj=dj, stg=stg, wd=wd, dx=dx):
            e.wait_ge(dx.h, dj)
            e.tensor_reduce(sb["agg"][:, 0:2],
                            stg[:, 0:2 * wd].rearrange("p (c h) -> p c h", c=2),
                            axis=AX, op=ALU.max, apply_absolute_value=True) \
                .then_inc(b.CV.h, 1)
        b.v(v1)
        b.CV.n += 1
        if tab is not None:
            vstep(b, lambda e, tab=tab, ci=ci: e.tensor_scalar(
                tab[:, ci:ci + 2], sb["agg"][:, 0:2], 1.0 / QMAX, None, ALU.mult))
        else:
            vstep(b, lambda e: e.tensor_scalar(
                sb["bns"][:, 0:2], sb["agg"][:, 0:2], 1.0 / QMAX, None, ALU.mult))
        vstep(b, lambda e: e.reciprocal(sb["bns"][:, 2:4], sb["agg"][:, 0:2]))
        vstep(b, lambda e: e.tensor_scalar(sb["bns"][:, 2:4], sb["bns"][:, 2:4],
                                           QMAX, None, ALU.mult))
        cvw = b.CV.n

        def saff(e, stg=stg, wd=wd, cvw=cvw):
            e.wait_ge(b.CV.h, cvw)
            for c in range(2):
                ins = e.activation(tmp[:, c * wd:(c + 1) * wd],
                                   stg[:, c * wd:(c + 1) * wd], AF.Identity,
                                   bias=sb["pad"][:, 5:6], scale=sb["bns"][:, 2 + c:3 + c])
            ins.then_inc(b.CS.h, 1)
        b.s(saff)
        b.CS.n += 1
        wait(b, "v", b.CS, b.CS.n)
        vstep(b, lambda e, wd=wd: e.tensor_scalar(
            tmp[:, 0:2 * wd], tmp[:, 0:2 * wd], CSH - 128.0, CSH + 127.49,
            ALU.max, ALU.min))
        vstep(b, lambda e, wd=wd: e.tensor_copy(ti[:, 5632:5632 + 2 * wd],
                                                tmp[:, 0:2 * wd]))
        if j >= 2:
            ss2, sv2 = stores[j - 2]
            wait(b, "v", ss2, sv2)
        ob = (j % 2) * 11264
        if not deq:
            vstep(b, lambda e, ob=ob, wd=wd: e.tensor_scalar(
                out[:, ob: ob + 2 * wd], ti[:, 5632:5632 + 2 * wd],
                -CSH, None, ALU.add))
        else:
            vstep(b, lambda e, ob=ob, wd=wd: e.tensor_scalar(
                out[:, ob: ob + wd], ti[:, 5632:5632 + wd],
                -CSH, sb["bns"][:, 0:1], ALU.add, ALU.mult))
            vstep(b, lambda e, ob=ob, wd=wd: e.tensor_scalar(
                out[:, ob + wd: ob + 2 * wd], ti[:, 5632 + wd:5632 + 2 * wd],
                -CSH, sb["bns"][:, 1:2], ALU.add, ALU.mult))
        cons[j] = b.CV.n
        cvj = b.CV.n
        ssm = b.SSA if j % 2 == 0 else b.SSB

        def st(e, cvj=cvj, w_out=w_out, r0=r0, wd=wd, ob=ob, ssm=ssm):
            e.wait_ge(b.CV.h, cvj)
            e.dma_start(out=w_out[r0:r0 + 256, 0:wd].rearrange(
                            "(c p) h -> p c h", p=128),
                        in_=out[:, ob: ob + 2 * wd].rearrange(
                            "p (c h) -> p c h", c=2)).then_inc(ssm.h, 16)
        b.s(st)
        ssm.n += 16
        stores[j] = (ssm, ssm.n)
    b.wq_store_done = (b.SSA.n, b.SSB.n)


def p_ln_fin(b, amax):
    sb = b.sb
    NCH = b.NCH
    vstep(b, lambda e: e.tensor_scalar(sb["m_tab"][:, 0:NCH], sb["m_tab"][:, 0:NCH],
                                       1.0 / H, None, ALU.mult))
    vstep(b, lambda e: e.tensor_tensor(sb["rr_tab"][:, 0:NCH], sb["m_tab"][:, 0:NCH],
                                       sb["m_tab"][:, 0:NCH], ALU.mult))
    vstep(b, lambda e: e.tensor_scalar(sb["v_tab"][:, 0:NCH], sb["v_tab"][:, 0:NCH],
                                       1.0 / H, LN_EPS, ALU.mult, ALU.add))
    vstep(b, lambda e: e.tensor_tensor(sb["v_tab"][:, 0:NCH], sb["v_tab"][:, 0:NCH],
                                       sb["rr_tab"][:, 0:NCH], ALU.subtract))
    cvn = b.CV.n

    def s1(e):
        e.wait_ge(b.CV.h, cvn)
        e.activation(sb["r_tab"][:, 0:NCH], sb["v_tab"][:, 0:NCH], AF.Sqrt) \
            .then_inc(b.CS.h, 1)
    b.s(s1)
    b.CS.n += 1
    wait(b, "v", b.CS, b.CS.n)
    vstep(b, lambda e: e.reciprocal(sb["r_tab"][:, 0:NCH], sb["r_tab"][:, 0:NCH]))
    if amax:
        vstep(b, lambda e: e.tensor_tensor(sb["xmx"][:, 0:NCH], sb["xmx"][:, 0:NCH],
                                           sb["m_tab"][:, 0:NCH], ALU.subtract))
        vstep(b, lambda e: e.tensor_tensor(sb["xmn"][:, 0:NCH], sb["m_tab"][:, 0:NCH],
                                           sb["xmn"][:, 0:NCH], ALU.subtract))
        vstep(b, lambda e: e.tensor_tensor(sb["xmx"][:, 0:NCH], sb["xmx"][:, 0:NCH],
                                           sb["xmn"][:, 0:NCH], ALU.max))
        vstep(b, lambda e: e.tensor_tensor(sb["xmx"][:, 0:NCH], sb["xmx"][:, 0:NCH],
                                           sb["r_tab"][:, 0:NCH], ALU.mult))
        vstep(b, lambda e: e.tensor_reduce(sb["red1"][:, :], sb["xmx"][:, 0:NCH],
                                           axis=AX, op=ALU.max))


def p_startup(b):
    io, sb, sd = b.io, b.sb, b.sd
    xsrc = io["x_in"]
    NCH = b.NCH
    NB = NCH // 2
    wait(b, "v", b.CG, b.CG.n)

    sq = sb["bigb"].ap().bitcast(F32)[:, 2816:4864]
    ln_cons = {}
    ln_cv0 = b.CV.n

    def emit_ln_batch(t):
        buf = sb["xstg"] if t % 2 == 0 else sb["ostg"]
        dx = (b.GXA, b.GXB)[t % 2]

        def ld(e, t=t, buf=buf, dx=dx):
            e.wait_ge(b.CV.h, ln_cons[t - 2] if t >= 2 else ln_cv0)
            e.dma_start(out=buf[:, 0:2048].rearrange("p (c h) -> p c h", c=2),
                        in_=xsrc[t * 256:(t + 1) * 256, :].rearrange(
                            "(c p) h -> p c h", p=128)).then_inc(dx.h, 16)
        b.g(ld)
        dx.n += 16
        di = dx.n

        def v1(e, di=di, buf=buf, dx=dx, t=t):
            e.wait_ge(dx.h, di)
            e.tensor_reduce(sb["m_tab"][:, 2 * t:2 * t + 2],
                            buf[:, 0:2048].rearrange("p (c h) -> p c h", c=2),
                            axis=AX, op=ALU.add).then_inc(b.CV.h, 1)
        b.v(v1)
        b.CV.n += 1
        vstep(b, lambda e, buf=buf: e.tensor_tensor(sq[:, 0:2048], buf[:, 0:2048],
                                                    buf[:, 0:2048], ALU.mult))
        vstep(b, lambda e, t=t: e.tensor_reduce(
            sb["v_tab"][:, 2 * t:2 * t + 2],
            sq[:, 0:2048].rearrange("p (c h) -> p c h", c=2), axis=AX, op=ALU.add))
        vstep(b, lambda e, t=t, buf=buf: e.tensor_reduce(
            sb["xmx"][:, 2 * t:2 * t + 2],
            buf[:, 0:2048].rearrange("p (c h) -> p c h", c=2), axis=AX, op=ALU.max))
        vstep(b, lambda e, t=t, buf=buf: e.tensor_reduce(
            sb["xmn"][:, 2 * t:2 * t + 2],
            buf[:, 0:2048].rearrange("p (c h) -> p c h", c=2), axis=AX, op=ALU.min))
        ln_cons[t] = b.CV.n

    stgA = sb["bigd"].ap().bitcast(F32)
    stgB = sb["bigc"].ap().bitcast(F32)
    tmp = sb["biga"].ap().bitcast(F32)
    ti = sb["biga"].ap().bitcast(I32)
    out = sb["bigb"]
    jobs = []
    for w_in, w_out, wd, deq, tab in (
            (io["w_gate"], sd["s_wgn"], H, False, sb["swg"]),
            (io["w_up"], sd["s_wun"], H, False, sb["swu"]),
            (io["w_down"], sd["s_wdn"], ID, True, None)):
        for r0 in range(0, w_in.shape[0], 256):
            jobs.append((w_in, w_out, wd, deq, tab, r0))
    wq_cons = {}
    wq_stores = {}
    wq_cv0 = b.CV.n

    def emit_wq_job(j):
        (w_in, w_out, wd, deq, tab, r0) = jobs[j]
        stg = stgA if j % 2 == 0 else stgB
        dx = (b.DXA, b.DXB)[j % 2]

        def ld(e, j=j, w_in=w_in, wd=wd, r0=r0, stg=stg, dx=dx):
            e.wait_ge(b.CV.h, wq_cons[j - 2] if j >= 2 else wq_cv0)
            e.dma_start(out=stg[:, 0:2 * wd].rearrange("p (c h) -> p c h", c=2),
                        in_=w_in[r0:r0 + 256, 0:wd].rearrange(
                            "(c p) h -> p c h", p=128)).then_inc(dx.h, 16)
        b.y(ld)
        dx.n += 16
        dj = dx.n
        ci = r0 // 128

        def v1(e, dj=dj, stg=stg, wd=wd, dx=dx):
            e.wait_ge(dx.h, dj)
            e.tensor_reduce(sb["agg"][:, 0:2],
                            stg[:, 0:2 * wd].rearrange("p (c h) -> p c h", c=2),
                            axis=AX, op=ALU.max, apply_absolute_value=True) \
                .then_inc(b.CV.h, 1)
        b.v(v1)
        b.CV.n += 1
        if tab is not None:
            vstep(b, lambda e, tab=tab, ci=ci: e.tensor_scalar(
                tab[:, ci:ci + 2], sb["agg"][:, 0:2], 1.0 / QMAX, None, ALU.mult))
        else:
            vstep(b, lambda e: e.tensor_scalar(
                sb["bns"][:, 0:2], sb["agg"][:, 0:2], 1.0 / QMAX, None, ALU.mult))
        vstep(b, lambda e: e.reciprocal(sb["bns"][:, 2:4], sb["agg"][:, 0:2]))
        vstep(b, lambda e: e.tensor_scalar(sb["bns"][:, 2:4], sb["bns"][:, 2:4],
                                           QMAX, None, ALU.mult))
        cvw = b.CV.n

        def saff(e, stg=stg, wd=wd, cvw=cvw):
            e.wait_ge(b.CV.h, cvw)
            for c in range(2):
                ins = e.activation(tmp[:, c * wd:(c + 1) * wd],
                                   stg[:, c * wd:(c + 1) * wd], AF.Identity,
                                   bias=sb["pad"][:, 5:6], scale=sb["bns"][:, 2 + c:3 + c])
            ins.then_inc(b.CS.h, 1)
        b.s(saff)
        b.CS.n += 1
        wait(b, "v", b.CS, b.CS.n)
        vstep(b, lambda e, wd=wd: e.tensor_scalar(
            tmp[:, 0:2 * wd], tmp[:, 0:2 * wd], CSH - 128.0, CSH + 127.49,
            ALU.max, ALU.min))
        vstep(b, lambda e, wd=wd: e.tensor_copy(ti[:, 5632:5632 + 2 * wd],
                                                tmp[:, 0:2 * wd]))
        if j >= 2:
            ss2, sv2 = wq_stores[j - 2]
            wait(b, "v", ss2, sv2)
        ob = (j % 2) * 11264
        if not deq:
            vstep(b, lambda e, ob=ob, wd=wd: e.tensor_scalar(
                out[:, ob: ob + 2 * wd], ti[:, 5632:5632 + 2 * wd],
                -CSH, None, ALU.add))
        else:
            vstep(b, lambda e, ob=ob, wd=wd: e.tensor_scalar(
                out[:, ob: ob + wd], ti[:, 5632:5632 + wd],
                -CSH, sb["bns"][:, 0:1], ALU.add, ALU.mult))
            vstep(b, lambda e, ob=ob, wd=wd: e.tensor_scalar(
                out[:, ob + wd: ob + 2 * wd], ti[:, 5632 + wd:5632 + 2 * wd],
                -CSH, sb["bns"][:, 1:2], ALU.add, ALU.mult))
        wq_cons[j] = b.CV.n
        cvj = b.CV.n
        ssm = b.SSA if j % 2 == 0 else b.SSB

        def st(e, cvj=cvj, w_out=w_out, r0=r0, wd=wd, ob=ob, ssm=ssm):
            e.wait_ge(b.CV.h, cvj)
            e.dma_start(out=w_out[r0:r0 + 256, 0:wd].rearrange(
                            "(c p) h -> p c h", p=128),
                        in_=out[:, ob: ob + 2 * wd].rearrange(
                            "p (c h) -> p c h", c=2)).then_inc(ssm.h, 16)
        b.s(st)
        ssm.n += 16
        wq_stores[j] = (ssm, ssm.n)

    NJ = len(jobs)
    for k in range(NB):
        emit_ln_batch(k)
        if k < NJ:
            emit_wq_job(k)

    vstep(b, lambda e: e.tensor_scalar(sb["m_tab"][:, 0:NCH], sb["m_tab"][:, 0:NCH],
                                       1.0 / H, None, ALU.mult))
    vstep(b, lambda e: e.tensor_tensor(sb["rr_tab"][:, 0:NCH], sb["m_tab"][:, 0:NCH],
                                       sb["m_tab"][:, 0:NCH], ALU.mult))
    vstep(b, lambda e: e.tensor_scalar(sb["v_tab"][:, 0:NCH], sb["v_tab"][:, 0:NCH],
                                       1.0 / H, LN_EPS, ALU.mult, ALU.add))
    vstep(b, lambda e: e.tensor_tensor(sb["v_tab"][:, 0:NCH], sb["v_tab"][:, 0:NCH],
                                       sb["rr_tab"][:, 0:NCH], ALU.subtract))
    cvn = b.CV.n

    def s1(e):
        e.wait_ge(b.CV.h, cvn)
        e.activation(sb["r_tab"][:, 0:NCH], sb["v_tab"][:, 0:NCH], AF.Sqrt) \
            .then_inc(b.CS.h, 1)
    b.s(s1)
    b.CS.n += 1
    wait(b, "v", b.CS, b.CS.n)
    vstep(b, lambda e: e.reciprocal(sb["r_tab"][:, 0:NCH], sb["r_tab"][:, 0:NCH]))
    vstep(b, lambda e: e.tensor_tensor(sb["xmx"][:, 0:NCH], sb["xmx"][:, 0:NCH],
                                       sb["m_tab"][:, 0:NCH], ALU.subtract))
    vstep(b, lambda e: e.tensor_tensor(sb["xmn"][:, 0:NCH], sb["m_tab"][:, 0:NCH],
                                       sb["xmn"][:, 0:NCH], ALU.subtract))
    vstep(b, lambda e: e.tensor_tensor(sb["xmx"][:, 0:NCH], sb["xmx"][:, 0:NCH],
                                       sb["xmn"][:, 0:NCH], ALU.max))
    vstep(b, lambda e: e.tensor_tensor(sb["xmx"][:, 0:NCH], sb["xmx"][:, 0:NCH],
                                       sb["r_tab"][:, 0:NCH], ALU.mult))
    vstep(b, lambda e: e.tensor_reduce(sb["red1"][:, :], sb["xmx"][:, 0:NCH],
                                       axis=AX, op=ALU.max))
    colmax_to_trow(b, 4)
    armax(b, 4, 5)

    for j in range(NB, NJ):
        emit_wq_job(j)
    b.wq_store_done = (b.SSA.n, b.SSB.n)


def p_ln_stats(b, xsrc, queue, amax):
    sb = b.sb
    NCH = b.NCH
    NB = NCH // 2
    emitq = b.y if queue == "sync" else b.g
    slots = (b.DXA, b.DXB) if queue == "sync" else (b.GXA, b.GXB)
    sq = sb["bigd"].ap().bitcast(F32)
    cons = {}
    cv0 = b.CV.n
    for t in range(NB):
        buf = sb["xstg"] if t % 2 == 0 else sb["ostg"]
        dx = slots[t % 2]

        def ld(e, t=t, buf=buf, dx=dx):
            e.wait_ge(b.CV.h, cons[t - 2] if t >= 2 else cv0)
            if t < 2:
                e.wait_ge(b.SSA.h, ssa0)
                e.wait_ge(b.SSB.h, ssb0)
            e.dma_start(out=buf[:, 0:2048].rearrange("p (c h) -> p c h", c=2),
                        in_=xsrc[t * 256:(t + 1) * 256, :].rearrange(
                            "(c p) h -> p c h", p=128)).then_inc(dx.h, 16)
        emitq(ld)
        dx.n += 16
        di = dx.n

        def v1(e, di=di, buf=buf, dx=dx, t=t):
            e.wait_ge(dx.h, di)
            e.tensor_reduce(sb["m_tab"][:, 2 * t:2 * t + 2],
                            buf[:, 0:2048].rearrange("p (c h) -> p c h", c=2),
                            axis=AX, op=ALU.add).then_inc(b.CV.h, 1)
        b.v(v1)
        b.CV.n += 1
        vstep(b, lambda e, buf=buf: e.tensor_tensor(sq[:, 0:2048], buf[:, 0:2048],
                                                    buf[:, 0:2048], ALU.mult))
        vstep(b, lambda e, t=t: e.tensor_reduce(
            sb["v_tab"][:, 2 * t:2 * t + 2],
            sq[:, 0:2048].rearrange("p (c h) -> p c h", c=2), axis=AX, op=ALU.add))
        if amax:
            vstep(b, lambda e, t=t, buf=buf: e.tensor_reduce(
                sb["xmx"][:, 2 * t:2 * t + 2],
                buf[:, 0:2048].rearrange("p (c h) -> p c h", c=2), axis=AX, op=ALU.max))
            vstep(b, lambda e, t=t, buf=buf: e.tensor_reduce(
                sb["xmn"][:, 2 * t:2 * t + 2],
                buf[:, 0:2048].rearrange("p (c h) -> p c h", c=2), axis=AX, op=ALU.min))
        cons[t] = b.CV.n

    vstep(b, lambda e: e.tensor_scalar(sb["m_tab"][:, 0:NCH], sb["m_tab"][:, 0:NCH],
                                       1.0 / H, None, ALU.mult))
    vstep(b, lambda e: e.tensor_tensor(sb["rr_tab"][:, 0:NCH], sb["m_tab"][:, 0:NCH],
                                       sb["m_tab"][:, 0:NCH], ALU.mult))
    vstep(b, lambda e: e.tensor_scalar(sb["v_tab"][:, 0:NCH], sb["v_tab"][:, 0:NCH],
                                       1.0 / H, LN_EPS, ALU.mult, ALU.add))
    vstep(b, lambda e: e.tensor_tensor(sb["v_tab"][:, 0:NCH], sb["v_tab"][:, 0:NCH],
                                       sb["rr_tab"][:, 0:NCH], ALU.subtract))
    cvn = b.CV.n

    def s1(e):
        e.wait_ge(b.CV.h, cvn)
        e.activation(sb["r_tab"][:, 0:NCH], sb["v_tab"][:, 0:NCH], AF.Sqrt) \
            .then_inc(b.CS.h, 1)
    b.s(s1)
    b.CS.n += 1
    csn = b.CS.n
    wait(b, "v", b.CS, csn)
    vstep(b, lambda e: e.reciprocal(sb["r_tab"][:, 0:NCH], sb["r_tab"][:, 0:NCH]))

    if amax:
        vstep(b, lambda e: e.tensor_tensor(sb["xmx"][:, 0:NCH], sb["xmx"][:, 0:NCH],
                                           sb["m_tab"][:, 0:NCH], ALU.subtract))
        vstep(b, lambda e: e.tensor_tensor(sb["xmn"][:, 0:NCH], sb["m_tab"][:, 0:NCH],
                                           sb["xmn"][:, 0:NCH], ALU.subtract))
        vstep(b, lambda e: e.tensor_tensor(sb["xmx"][:, 0:NCH], sb["xmx"][:, 0:NCH],
                                           sb["xmn"][:, 0:NCH], ALU.max))
        vstep(b, lambda e: e.tensor_tensor(sb["xmx"][:, 0:NCH], sb["xmx"][:, 0:NCH],
                                           sb["r_tab"][:, 0:NCH], ALU.mult))
        vstep(b, lambda e: e.tensor_reduce(sb["red1"][:, :], sb["xmx"][:, 0:NCH],
                                           axis=AX, op=ALU.max))


def p_scales_x(b):
    sb = b.sb
    NCH = b.NCH
    wait(b, "v", b.CG, b.CG.n)
    wait(b, "v", b.CS, b.CS.n)
    vstep(b, lambda e: e.tensor_tensor(sb["trow"][0:1, 6:7], sb["trow"][0:1, 5:6],
                                       sb["trow"][0:1, 2:3], ALU.mult))
    vstep(b, lambda e: e.tensor_scalar(sb["trow"][0:1, 6:7], sb["trow"][0:1, 6:7],
                                       1.0 / QMAX, None, ALU.mult))
    vstep(b, lambda e: e.reciprocal(sb["trow"][0:1, 7:8], sb["trow"][0:1, 6:7]))
    bcast_exact(b, 7, sb["pad"], 0)
    bcast_exact(b, 6, sb["pad"], 1)
    wait(b, "v", b.CG, b.CG.n)
    vstep(b, lambda e: e.tensor_scalar(sb["rr_tab"][:, 0:NCH], sb["r_tab"][:, 0:NCH],
                                       sb["pad"][:, 0:1], None, ALU.mult))
    vstep(b, lambda e: e.tensor_tensor(sb["bb_tab"][:, 0:NCH], sb["m_tab"][:, 0:NCH],
                                       sb["rr_tab"][:, 0:NCH], ALU.mult))
    vstep(b, lambda e: e.tensor_scalar(sb["bb_tab"][:, 0:NCH], sb["bb_tab"][:, 0:NCH],
                                       -1.0, sb["pad"][:, 5:6], ALU.mult, ALU.add))
    vstep(b, lambda e: e.tensor_scalar(sb["sgx"][:, 0:IC], sb["swg"][:, 0:IC],
                                       sb["pad"][:, 1:2], None, ALU.mult))
    vstep(b, lambda e: e.tensor_scalar(sb["sux"][:, 0:IC], sb["swu"][:, 0:IC],
                                       sb["pad"][:, 1:2], None, ALU.mult))


def p_scales_h(b):
    sb = b.sb
    wait(b, "v", b.CG, b.CG.n)
    vstep(b, lambda e: e.tensor_tensor(sb["trow"][0:1, 8:9], sb["trow"][0:1, 5:6],
                                       sb["trow"][0:1, 3:4], ALU.mult))
    vstep(b, lambda e: e.tensor_scalar(sb["trow"][0:1, 8:9], sb["trow"][0:1, 8:9],
                                       1.0 / QMAX, None, ALU.mult))
    vstep(b, lambda e: e.reciprocal(sb["trow"][0:1, 9:10], sb["trow"][0:1, 8:9]))
    vstep(b, lambda e: e.tensor_scalar(sb["trow"][0:1, 10:11], sb["trow"][0:1, 8:9],
                                       0.5, None, ALU.mult))
    bcast_exact(b, 9, sb["pad"], 2)
    bcast_exact(b, 10, sb["pad"], 3)


def p_xq(b, xsrc, queue):
    sb = b.sb
    NCH = b.NCH
    NB = NCH // 2
    emitq = b.y if queue == "sync" else b.g
    slots = (b.DXA, b.DXB) if queue == "sync" else (b.GXA, b.GXB)
    tmp = sb["biga"].ap().bitcast(F32)[:, 2048:4096]
    ti = sb["biga"].ap().bitcast(I32)[:, 4096:6144]
    obf = sb["biga"]
    cons = {}
    stores = {}
    cv0 = b.CV.n
    ssa0, ssb0 = b.SSA.n, b.SSB.n
    for t in range(NB):
        buf = sb["xstg"] if t % 2 == 0 else sb["ostg"]
        dx = slots[t % 2]

        def ld(e, t=t, buf=buf, dx=dx):
            e.wait_ge(b.CV.h, cons[t - 2] if t >= 2 else cv0)
            if t < 2:
                e.wait_ge(b.SSA.h, ssa0)
                e.wait_ge(b.SSB.h, ssb0)
            e.dma_start(out=buf[:, 0:2048].rearrange("p (c h) -> p c h", c=2),
                        in_=xsrc[t * 256:(t + 1) * 256, :].rearrange(
                            "(c p) h -> p c h", p=128)).then_inc(dx.h, 16)
        emitq(ld)
        dx.n += 16
        di = dx.n
        cvw = b.CV.n

        def s1a(e, di=di, buf=buf, t=t, dx=dx, cvw=cvw):
            e.wait_ge(dx.h, di)
            e.wait_ge(b.CV.h, cvw)
            for c in range(2):
                i = 2 * t + c
                ins = e.activation(tmp[:, c * 1024:(c + 1) * 1024],
                                   buf[:, c * 1024:(c + 1) * 1024],
                                   AF.Identity, bias=sb["bb_tab"][:, i:i + 1],
                                   scale=sb["rr_tab"][:, i:i + 1])
            ins.then_inc(b.CS.h, 1)
        b.s(s1a)
        b.CS.n += 1

        def v1(e, t=t, csw=b.CS.n):
            e.wait_ge(b.CS.h, csw)
            if t >= 2:
                ss2, sv2 = stores[t - 2]
                e.wait_ge(ss2.h, sv2)
            e.tensor_scalar(tmp[:, 0:2048], tmp[:, 0:2048],
                            CSH - 128.0, CSH + 127.49, ALU.max, ALU.min) \
                .then_inc(b.CV.h, 1)
        b.v(v1)
        b.CV.n += 1
        vstep(b, lambda e: e.tensor_copy(ti[:, 0:2048], tmp[:, 0:2048]))
        ob = t % 2
        vstep(b, lambda e, ob=ob: e.tensor_scalar(obf[:, ob * 2048:(ob + 1) * 2048],
                                                  ti[:, 0:2048], -CSH, None, ALU.add))
        cons[t] = b.CV.n
        cvi = b.CV.n
        ssm = b.SSA if ob == 0 else b.SSB

        def st(e, cvi=cvi, t=t, ob=ob, ssm=ssm, ssa0=ssa0, ssb0=ssb0):
            e.wait_ge(b.CV.h, cvi)
            if t < 2:
                e.wait_ge(ssm.h, ssa0 if t % 2 == 0 else ssb0)
            e.dma_start(out=b.sd["s_xq"][t * 256:(t + 1) * 256, :].rearrange(
                            "(c p) h -> p c h", p=128),
                        in_=obf[:, ob * 2048:(ob + 1) * 2048].rearrange(
                            "p (c h) -> p c h", c=2)).then_inc(ssm.h, 16)
        b.s(st)
        ssm.n += 16
        stores[t] = (ssm, ssm.n)
    b.xq_stores = stores


def p_gateup(b):
    sb, sd = b.sb, b.sd
    NQ = b.NQ

    wq_ssa, wq_ssb = getattr(b, "wq_store_done", (b.SSA.n, b.SSB.n))
    ssa_n, ssb_n = b.SSA.n, b.SSB.n
    for hc in range(HC):
        def ld(e, hc=hc, wq_ssa=wq_ssa, wq_ssb=wq_ssb):
            if hc == 0:
                e.wait_ge(b.SSA.h, wq_ssa)
                e.wait_ge(b.SSB.h, wq_ssb)
            e.dma_start(out=sb["biga"][:, hc * ID:(hc + 1) * ID],
                        in_=sd["s_wgn"][0:ID, hc * 128:(hc + 1) * 128],
                        transpose=True).then_inc(b.D.h, 16)
        b.y(ld)
        b.D.n += 16
    for hc in range(HC):
        def ld2(e, hc=hc):
            e.dma_start(out=sb["bigb"][:, hc * ID:(hc + 1) * ID],
                        in_=sd["s_wun"][0:ID, hc * 128:(hc + 1) * 128],
                        transpose=True).then_inc(b.D.h, 16)
        b.y(ld2)
        b.D.n += 16
    d_w = b.D.n

    vstep(b, lambda e: e.memset(sb["dmax"][:, :], 0.0))
    cv_init = b.CV.n

    ct_q = {}
    for q in range(NQ):
        qb = sb["bigd"] if q % 2 == 0 else sb["bigc"]
        dq = b.DQ0 if q % 2 == 0 else b.DQ1
        for hc in range(HC):
            qs1 = b.xq_stores[4 * q + 2]
            qs2 = b.xq_stores[4 * q + 3]

            def xl(e, hc=hc, q=q, qb=qb, dq=dq, qs1=qs1, qs2=qs2):
                if hc == 0:
                    e.wait_ge(qs1[0].h, qs1[1])
                    e.wait_ge(qs2[0].h, qs2[1])
                if hc == 0 and q >= 2:
                    e.wait_ge(b.CT.h, ct_q[q - 2])
                e.dma_start(out=qb[:, hc * 1024:(hc + 1) * 1024],
                            in_=sd["s_xq"][q * 1024:(q + 1) * 1024,
                                           hc * 128:(hc + 1) * 128],
                            transpose=True).then_inc(dq.h, 16)
            b.y(xl)
            dq.n += 16
        d_x = dq.n

        evacs = {}
        spills = {}
        cv_qprev = cv_init if q == 0 else gu_prev_last
        for g in range(2):
            ctj = {}
            cv_g0 = b.CV.n

            def epred(j):
                return evacs[j] if j in evacs else None
            for ic in range(IC):
                j = g * IC + ic
                pg = b.psf[0 + 2 * (j % 3)]
                pu = b.psf[1 + 2 * (j % 3)]
                evacs[j] = cv_g0 + 3 * (ic + 1) if False else None

                def tm(e, j=j, g=g, ic=ic, pg=pg, pu=pu, d_x=d_x, cv_qprev=cv_qprev,
                       qb=qb, dq=dq, ep=evacs.get(j - 3)):
                    if j == 0:
                        e.wait_ge(b.D.h, d_w)
                        e.wait_ge(dq.h, d_x)
                        e.wait_ge(b.CV.h, cv_qprev)
                    if j >= 3:
                        e.wait_ge(b.CV.h, ep)
                    col = g * 512
                    for hc in range(HC):
                        e.matmul(pg.ap()[:, 0:512],
                                 sb["biga"][:, hc * ID + ic * 128: hc * ID + (ic + 1) * 128],
                                 qb[:, hc * 1024 + col: hc * 1024 + col + 512],
                                 start=(hc == 0), stop=(hc == HC - 1))
                    for hc in range(HC):
                        ins = e.matmul(pu.ap()[:, 0:512],
                                       sb["bigb"][:, hc * ID + ic * 128: hc * ID + (ic + 1) * 128],
                                       qb[:, hc * 1024 + col: hc * 1024 + col + 512],
                                       start=(hc == 0), stop=(hc == HC - 1))
                    ins.then_inc(b.CT.h, 1)
                b.t(tm)
                b.CT.n += 1
                ctj[ic] = b.CT.n
                evacs[j] = cv_g0 + 3 * (ic + 1)

            def sg_sec(e, g=g, ctj=dict(ctj), cv_g0=cv_g0):
                for ic in range(IC):
                    j = g * IC + ic
                    e.wait_ge(b.CT.h, ctj[ic])
                    if ic >= 3:
                        e.wait_ge(b.CV.h, cv_g0 + 3 * (ic - 2))
                    else:
                        e.wait_ge(b.CV.h, cv_g0)
                    pg = b.psf[0 + 2 * (j % 3)]
                    e.activation(sb["xstg"][:, (j % 3) * 512:(j % 3 + 1) * 512],
                                 pg.ap()[:, 0:512], AF.Silu, bias=0.0,
                                 scale=sb["sgx"][:, ic:ic + 1]).then_inc(b.CS.h, 1)
            b.s(sg_sec)
            cs_g0 = b.CS.n
            b.CS.n += IC

            for ic in range(IC):
                j = g * IC + ic
                pu = b.psf[1 + 2 * (j % 3)]
                csj = cs_g0 + ic + 1

                def vv(e, csj=csj, ic=ic, j=j, pu=pu, q=q):
                    e.wait_ge(b.CS.h, csj)
                    if j >= 2:
                        ss2, sv2 = spills[j - 2]
                        e.wait_ge(ss2.h, sv2)
                    elif q > 0:
                        ss2, sv2 = gu_prev_spills[j % 2]
                        e.wait_ge(ss2.h, sv2)
                    e.scalar_tensor_tensor(
                        sb["ostg"].ap().bitcast(BF16)[:, (j % 2) * 512:(j % 2 + 1) * 512],
                        pu.ap()[:, 0:512], sb["sux"][:, ic:ic + 1],
                        sb["xstg"][:, (j % 3) * 512:(j % 3 + 1) * 512],
                        ALU.mult, ALU.mult).then_inc(b.CV.h, 1)
                b.v(vv)
                b.CV.n += 1
                vstep(b, lambda e, j=j: e.tensor_reduce(
                    sb["red1"][:, :],
                    sb["ostg"].ap().bitcast(BF16)[:, (j % 2) * 512:(j % 2 + 1) * 512],
                    axis=AX, op=ALU.max, apply_absolute_value=True))
                vstep(b, lambda e: e.tensor_tensor(sb["dmax"][:, :], sb["dmax"][:, :],
                                                   sb["red1"][:, :], ALU.max))
                assert b.CV.n == evacs[j], (b.CV.n, evacs[j])
                cvj = b.CV.n
                ssm = b.GSA if j % 2 == 0 else b.GSB

                def sp(e, cvj=cvj, ic=ic, q=q, g=g, j=j, ssm=ssm):
                    e.wait_ge(b.CV.h, cvj)
                    e.dma_start(out=sd["s_h"][ic * 128:(ic + 1) * 128,
                                              q * 1024 + g * 512: q * 1024 + g * 512 + 512],
                                in_=sb["ostg"].ap().bitcast(BF16)[:, (j % 2) * 512:(j % 2 + 1) * 512]) \
                        .then_inc(ssm.h, 16)
                b.g(sp)
                ssm.n += 16
                spills[j] = (ssm, ssm.n)
        ct_q[q] = b.CT.n
        gu_prev_last = b.CV.n
        gu_prev_spills = (spills[2 * IC - 2], spills[2 * IC - 1])

    vstep(b, lambda e: e.tensor_copy(sb["red1"][:, :], sb["dmax"][:, :]))


def p_down(b, xold, xdst, final, do_stats=False):
    sb, sd = b.sb, b.sd
    R = b.R
    NG = R // 512
    NCH = b.NCH
    tmp = sb["biga"].ap().bitcast(F32)
    ti = sb["biga"].ap().bitcast(I32)
    ostg = sb["ostg"]
    ob_bf = ostg.ap().bitcast(BF16)

    ssa_n, ssb_n = b.GSA.n, b.GSB.n
    for icc in range(IC):
        def ld(e, icc=icc, ssa_n=ssa_n, ssb_n=ssb_n):
            if icc == 0:
                e.wait_ge(b.GSA.h, ssa_n)
                e.wait_ge(b.GSB.h, ssb_n)
            e.dma_start(out=sb["bigc"][:, icc * H:(icc + 1) * H],
                        in_=sd["s_wdn"][0:H, icc * 128:(icc + 1) * 128],
                        transpose=True).then_inc(b.D.h, 16)
        b.y(ld)
        b.D.n += 16
    d_w = b.D.n

    dg0 = b.DG.n
    xevac = {}

    xdn = {}

    def xload(i, wait_cv):
        gx = b.GXA if i % 2 == 0 else b.GXB

        def xle(e, i=i, wait_cv=wait_cv, gx=gx):
            if wait_cv:
                e.wait_ge(b.CV.h, wait_cv)
            e.dma_start(out=sb["xstg"][:, (i % 2) * H:(i % 2 + 1) * H],
                        in_=xold[i * 128:(i + 1) * 128, :]).then_inc(gx.h, 16)
        b.g(xle)
        gx.n += 16
        xdn[i] = (gx, gx.n)

    xload(0, 0)
    xload(1, 0)

    hq = {}
    ct_g = {}
    ctA = {}
    ctB = {}
    dn_store = {}
    stats_cv = {}
    qcv_all = {}
    cv_entry = b.CV.n
    ctd0 = [None]

    def emit_loads_quant(g, ctA_pred, ctB_pred, prev_qcvB):
        def hlA(e, g=g, ctA_pred=ctA_pred):
            if g >= 1:
                e.wait_ge(b.CT.h, ctA_pred)
            for icc in range(11):
                e.dma_start(out=sb["bigd"][:, icc * 512:(icc + 1) * 512],
                            in_=sd["s_h"][icc * 128:(icc + 1) * 128,
                                          g * 512:(g + 1) * 512]).then_inc(b.DH.h, 16)
        b.y(hlA)
        b.DH.n += 16 * 11
        d_ha = b.DH.n

        def hlB(e, g=g, ctB_pred=ctB_pred):
            if g >= 1:
                e.wait_ge(b.CT.h, ctB_pred)
            for icc in range(11, IC):
                e.dma_start(out=sb["bigd"][:, icc * 512:(icc + 1) * 512],
                            in_=sd["s_h"][icc * 128:(icc + 1) * 128,
                                          g * 512:(g + 1) * 512]).then_inc(b.DHB.h, 16)
        b.y(hlB)
        b.DHB.n += 16 * 11
        d_hb = b.DHB.n

        qcv = {}
        for hf, (dsm, dvl) in ((0, (b.DH, d_ha)), (1, (b.DHB, d_hb))):
            w = 11 * 512
            base = hf * w
            tmp_free = prev_qcvB if hf == 0 else qcv[0]

            def sq(e, dsm=dsm, dvl=dvl, base=base, w=w, tmp_free=tmp_free):
                e.wait_ge(dsm.h, dvl)
                e.wait_ge(b.CV.h, tmp_free)
                e.activation(tmp[:, 0:w], sb["bigd"][:, base:base + w], AF.Identity,
                             bias=sb["pad"][:, 5:6], scale=sb["pad"][:, 2:3]) \
                    .then_inc(b.CS.h, 1)
            b.s(sq)
            b.CS.n += 1
            wait(b, "v", b.CS, b.CS.n)
            vstep(b, lambda e, w=w: e.tensor_scalar(
                tmp[:, 0:w], tmp[:, 0:w], CSH - 128.0, CSH + 127.49, ALU.max, ALU.min))
            vstep(b, lambda e, w=w: e.tensor_copy(ti[:, 5632:5632 + w], tmp[:, 0:w]))
            vstep(b, lambda e, base=base, w=w: e.tensor_scalar(
                sb["bigd"][:, base:base + w], ti[:, 5632:5632 + w], -CSH, None, ALU.add))
            qcv[hf] = b.CV.n
        qcv_all[g] = qcv
        hq[g] = b.CV.n

    def emit_group_mms(g, parts=(0, 1)):
        qcv = qcv_all[g]
        if ctd0[0] is None:
            ctd0[0] = b.CT.n
        for pair in parts:
            for hf in range(2):
                for ii in range(2):
                    sub = 2 * pair + ii
                    nch = g * 4 + sub
                    par = sub % 2
                    psA = b.psf[0 + 2 * par]
                    psB = b.psf[1 + 2 * par]

                    def tdm(e, hf=hf, sub=sub, nch=nch, psA=psA, psB=psB, g=g,
                            qv=qcv[hf], d_w=d_w, pair=pair, ii=ii):
                        if pair == 0 and ii == 0:
                            e.wait_ge(b.CV.h, qv)
                        if hf == 0:
                            if g == 0 and sub == 0:
                                e.wait_ge(b.D.h, d_w)
                            if nch >= 2:
                                e.wait_ge(b.CV.h, xevac[nch - 2])
                        r0, r1 = (0, 11) if hf == 0 else (11, IC)
                        for icc in range(r0, r1):
                            e.matmul(psA.ap()[:, 0:512],
                                     sb["bigd"][:, icc * 512 + sub * 128: icc * 512 + sub * 128 + 128],
                                     sb["bigc"][:, icc * H: icc * H + 512],
                                     start=(icc == 0), stop=(icc == IC - 1))
                        for icc in range(r0, r1):
                            ins = e.matmul(psB.ap()[:, 0:512],
                                           sb["bigd"][:, icc * 512 + sub * 128: icc * 512 + sub * 128 + 128],
                                           sb["bigc"][:, icc * H + 512: icc * H + 1024],
                                           start=(icc == 0), stop=(icc == IC - 1))
                        ins.then_inc(b.CT.h, 1)
                    b.t(tdm)
                    b.CT.n += 1
                    if hf == 0 and pair == 1:
                        ctA[g] = b.CT.n
                    if hf == 1:
                        if pair == 1:
                            ctB[g] = b.CT.n
                        ctn = b.CT.n

                        def vev(e, ctn=ctn, nch=nch, psA=psA, psB=psB):
                            e.wait_ge(b.CT.h, ctn)
                            if nch >= 1 and (nch - 1) in xevac:
                                e.wait_ge(b.CV.h, stats_cv.get(nch - 1, xevac[nch - 1]))
                            else:
                                e.wait_ge(b.CV.h, cv_entry)
                            gx2, gv2 = xdn[nch]
                            e.wait_ge(gx2.h, gv2)
                            if nch >= 2:
                                ss2, sv2 = dn_store[nch - 2]
                                e.wait_ge(ss2.h, sv2)
                                if do_stats:
                                    e.wait_ge(b.CV.h, stats_cv[nch - 2])
                            xb = (nch % 2) * H
                            ob = (nch % 2)
                            if final:
                                dA = ostg[:, ob * H: ob * H + 512]
                                dB = ostg[:, ob * H + 512: ob * H + 1024]
                            else:
                                dA = ob_bf[:, ob * 2 * H: ob * 2 * H + 512]
                                dB = ob_bf[:, ob * 2 * H + 512: ob * 2 * H + 1024]
                            e.scalar_tensor_tensor(dA, psA.ap()[:, 0:512], sb["pad"][:, 3:4],
                                                   sb["xstg"][:, xb: xb + 512], ALU.mult, ALU.add)
                            e.scalar_tensor_tensor(dB, psB.ap()[:, 0:512], sb["pad"][:, 3:4],
                                                   sb["xstg"][:, xb + 512: xb + 1024],
                                                   ALU.mult, ALU.add).then_inc(b.CV.h, 1)
                        b.v_raw(vev)
                        b.CV.n += 1
                        xevac[nch] = b.CV.n
                        if do_stats:
                            obq = (nch % 2)
                            ch = ob_bf[:, obq * 2 * H: obq * 2 * H + 1024]
                            xe = xevac[nch]

                            def vstats(e, ch=ch, nch=nch, xe=xe):
                                e.wait_ge(b.CV.h, xe)
                                e.tensor_reduce(sb["m_tab"][:, nch:nch + 1], ch,
                                                axis=AX, op=ALU.add).then_inc(b.CV.h, 1)
                                e.wait_ge(b.CV.h, xe + 1)
                                e.tensor_tensor(sb["sqt"][:, :], ch, ch, ALU.mult) \
                                    .then_inc(b.CV.h, 1)
                                e.wait_ge(b.CV.h, xe + 2)
                                e.tensor_reduce(sb["v_tab"][:, nch:nch + 1],
                                                sb["sqt"][:, :], axis=AX,
                                                op=ALU.add).then_inc(b.CV.h, 1)
                            b.v_raw(vstats)
                            b.CV.n += 3
                            stats_cv[nch] = b.CV.n
                        cvn = stats_cv[nch] if do_stats else xevac[nch]
                        ssm = b.SSA if nch % 2 == 0 else b.SSB

                        def stv(e, cvn=cvn, nch=nch, ssm=ssm):
                            e.wait_ge(b.CV.h, cvn)
                            ob = (nch % 2)
                            if final:
                                e.dma_start(out=b.y_out[nch * 128:(nch + 1) * 128, :],
                                            in_=ostg[:, ob * H:(ob + 1) * H]).then_inc(ssm.h, 16)
                            else:
                                e.dma_start(out=xdst[nch * 128:(nch + 1) * 128, :],
                                            in_=ob_bf[:, ob * 2 * H: ob * 2 * H + 1024]) \
                                    .then_inc(ssm.h, 16)
                        b.s(stv)
                        ssm.n += 16
                        dn_store[nch] = (ssm, ssm.n)
                        if nch + 2 < NCH:
                            xload(nch + 2, xevac[nch])
        ct_g[g] = b.CT.n

    ct0_pred = b.CT.n
    emit_loads_quant(0, 0, 0, b.CV.n)
    for g in range(NG):
        emit_group_mms(g, parts=(0,))
        if g + 1 < NG:
            emit_loads_quant(g + 1, ct0_pred + 8 * g + 6, ct0_pred + 8 * g + 8,
                             qcv_all[g][1])
        emit_group_mms(g, parts=(1,))
        assert ctA[g] == (ctd0[0] + 8 * g + 6), (g, ctA[g], ctd0[0])
        assert ctB[g] == (ctd0[0] + 8 * g + 8), (g, ctB[g], ctd0[0])


def p_ln_apply(b, xsrc, dst):
    sb = b.sb
    NCH = b.NCH
    cons = {}
    stores = {}
    cv0 = b.CV.n
    ssa0, ssb0 = b.SSA.n, b.SSB.n
    ob_bf = sb["ostg"].ap().bitcast(BF16)
    for i in range(NCH):
        gx = b.GXA if i % 2 == 0 else b.GXB

        def ld(e, i=i, gx=gx):
            e.wait_ge(b.CV.h, cons[i - 2] if i >= 2 else cv0)
            e.dma_start(out=sb["xstg"][:, (i % 2) * H:(i % 2 + 1) * H],
                        in_=xsrc[i * 128:(i + 1) * 128, :]).then_inc(gx.h, 16)
        b.g(ld)
        gx.n += 16
        dgi = gx.n

        def v1(e, dgi=dgi, i=i, gx=gx):
            e.wait_ge(gx.h, dgi)
            if i >= 2:
                ss2, sv2 = stores[i - 2]
                e.wait_ge(ss2.h, sv2)
            else:
                e.wait_ge(b.SSA.h, ssa0)
                e.wait_ge(b.SSB.h, ssb0)
            e.tensor_scalar(ob_bf[:, (i % 2) * 2 * H: (i % 2) * 2 * H + H],
                            sb["xstg"][:, (i % 2) * H:(i % 2 + 1) * H],
                            sb["m_tab"][:, i:i + 1], sb["r_tab"][:, i:i + 1],
                            ALU.subtract, ALU.mult).then_inc(b.CV.h, 1)
        b.v(v1)
        b.CV.n += 1
        cons[i] = b.CV.n
        cvi = b.CV.n
        ssm = b.SSA if i % 2 == 0 else b.SSB

        def st(e, cvi=cvi, i=i, ssm=ssm):
            e.wait_ge(b.CV.h, cvi)
            e.dma_start(out=dst[i * 128:(i + 1) * 128, :],
                        in_=ob_bf[:, (i % 2) * 2 * H: (i % 2) * 2 * H + H]) \
                .then_inc(ssm.h, 16)
        b.s(st)
        ssm.n += 16
        stores[i] = (ssm, ssm.n)



VE = 0
KP = 8 * 1040
KQ = KP + 2 * 1024
KF = KQ + 2 * 1024


def p_sweep1(b):
    sb, sd = b.sb, b.sd
    NQ = b.NQ

    dgn = b.dg_wqkv
    for hc in range(HC):
        def ld(e, hc=hc, dgn=dgn):
            if hc == 0:
                e.wait_ge(b.DW.h, dgn)
            e.dma_start(out=sb["biga"][:, hc * 1024:(hc + 1) * 1024],
                        in_=sd["s_wkn"][0:H, hc * 128:(hc + 1) * 128],
                        transpose=True).then_inc(b.D.h, 16)
        b.y(ld)
        b.D.n += 16
    for hc in range(HC):
        def ld2(e, hc=hc):
            e.dma_start(out=sb["biga"][:, 8192 + hc * 1024: 9216 + hc * 1024],
                        in_=sd["s_wvn"][0:H, hc * 128:(hc + 1) * 128],
                        transpose=True).then_inc(b.D.h, 16)
        b.y(ld2)
        b.D.n += 16
    d_w = b.D.n

    vstep(b, lambda e: e.memset(sb["ctx_sb"][:, :], 0.0))
    vstep(b, lambda e: e.memset(sb["dmax"][:, :], 0.0))
    vstep(b, lambda e: e.memset(sb["qrow"][:, :], 0.0))
    b.ctx_evac_cv = b.CV.n
    b.ps45_cs = b.CS.n
    b.ps45_cv = b.CV.n
    cv_init = b.CV.n
    ct_q = {}

    for q in range(NQ):
        qb = sb["bigd"] if q % 2 == 0 else sb["bigc"]
        dq = b.DQ0 if q % 2 == 0 else b.DQ1
        for hc in range(HC):
            def xl(e, hc=hc, q=q, qb=qb, dq=dq):
                if hc == 0 and q >= 2:
                    e.wait_ge(b.CT.h, ct_q[q - 2])
                e.dma_start(out=qb[:, hc * 1024:(hc + 1) * 1024],
                            in_=sd["s_ln2"][q * 1024:(q + 1) * 1024,
                                            hc * 128:(hc + 1) * 128],
                            transpose=True).then_inc(dq.h, 16)
            b.y(xl)
            dq.n += 16
        d_x = dq.n

        vproj = {}
        for i in range(8):
            ph = b.psf[0 + (i % 2)]
            pl = b.psf[2 + (i % 2)]

            def tv(e, i=i, ph=ph, pl=pl, d_x=d_x, q=q, qb=qb, dq=dq):
                if i == 0:
                    e.wait_ge(b.D.h, d_w)
                    e.wait_ge(dq.h, d_x)
                    e.wait_ge(b.CV.h, cv_init if q == 0 else b.s1_vext_free[q])
                if i >= 2:
                    e.wait_ge(b.CV.h, vproj[i - 2])
                for half, ps in ((0, ph), (1, pl)):
                    for hc in range(HC):
                        e.matmul(ps.ap()[:, 0:512],
                                 qb[:, hc * 1024 + i * 128: hc * 1024 + (i + 1) * 128],
                                 sb["biga"][:, 8192 + hc * 1024 + half * 512:
                                            8192 + hc * 1024 + half * 512 + 512],
                                 start=(hc == 0), stop=False)
                    ins = e.matmul(ps.ap()[:, 0:512], sb["onerb"][0:1, 0:128],
                                   sb["brow"][0:1, H + half * 512: H + half * 512 + 512],
                                   start=False, stop=True)
                ins.then_inc(b.CT.h, 1)
            b.t(tv)
            b.CT.n += 1
            ctn = b.CT.n

            def vv(e, ctn=ctn, i=i, ph=ph, pl=pl):
                e.wait_ge(b.CT.h, ctn)
                for half, ps in ((0, ph), (1, pl)):
                    dst = sb["bigb"][:, VE + i * 1040 + half * 520:
                                     VE + i * 1040 + half * 520 + 520]
                    d3 = dst.rearrange("p (h e) -> p h e", e=65)[:, :, 0:64]
                    s3 = ps.ap()[:, 0:512].rearrange("p (h e) -> p h e", e=64)
                    ins = e.tensor_copy(d3, s3)
                ins.then_inc(b.CV.h, 1)
            b.v(vv)
            b.CV.n += 1
            vproj[i] = b.CV.n

        def vo(e):
            vex = sb["bigb"][:, VE: VE + 8 * 1040].rearrange(
                "p (i h e) -> p i h e", i=8, e=65)
            e.memset(vex[:, :, :, 64:65], 1.0).then_inc(b.CV.h, 1)
        b.v(vo)
        b.CV.n += 1
        cv_vext = b.CV.n

        kdone = {}
        for pr in range(NPAIR):
            kb = pr % 2

            def tk(e, pr=pr, kb=kb, cv_vext=cv_vext, pcs=b.ps45_cs, pcv=b.ps45_cv,
                   qb=qb):
                if pr == 0:
                    e.wait_ge(b.CV.h, cv_vext)
                e.wait_ge(b.CS.h, pcs)
                e.wait_ge(b.CV.h, pcv)
                for g in range(2):
                    ps = b.psf[4 + g]
                    for hc in range(HC):
                        ins = e.matmul(ps.ap()[:, 0:512],
                                       sb["biga"][:, hc * 1024 + pr * 128: hc * 1024 + (pr + 1) * 128],
                                       qb[:, hc * 1024 + g * 512: hc * 1024 + g * 512 + 512],
                                       start=(hc == 0), stop=(hc == HC - 1))
                ins.then_inc(b.CT.h, 1)
            b.t(tk)
            b.CT.n += 1
            ctk = b.CT.n

            def sk(e, ctk=ctk, pr=pr, kb=kb):
                e.wait_ge(b.CT.h, ctk)
                for g in range(2):
                    ps = b.psf[4 + g]
                    e.tensor_scalar(sb["bigb"][:, KP + kb * 1024 + g * 512:
                                               KP + kb * 1024 + g * 512 + 512],
                                    ps.ap()[:, 0:512],
                                    sb["b_tab"][:, 8 + pr: 9 + pr], None, ALU.add) \
                        .then_inc(b.CV.h, 1)
                    e.wait_ge(b.CV.h, b.CV.n + 2 * g + 1)
                    e.tensor_tensor(sb["bigb"][:, KQ + kb * 1024 + g * 512:
                                               KQ + kb * 1024 + g * 512 + 512],
                                    sb["bigb"][:, KP + kb * 1024 + g * 512:
                                               KP + kb * 1024 + g * 512 + 512],
                                    sb["bigb"][:, KP + kb * 1024 + g * 512:
                                               KP + kb * 1024 + g * 512 + 512],
                                    ALU.mult).then_inc(b.CV.h, 1)
            b.v(sk)
            b.CV.n += 4
            kdone[pr] = b.CV.n

            for hh in range(2):
                head = 2 * pr + hh
                p_head_sweep1(b, head, kb, hh * 64, head % 2, kdone[pr])

        for colh, w in ((0, 512), (1, 512), (2, 16)):
            cvn = b.CV.n

            def tvs(e, colh=colh, w=w, cvn=cvn):
                e.wait_ge(b.CV.h, cvn)
                ps = b.psf[4]
                for i in range(8):
                    ins = e.matmul(ps.ap()[0:1, 0:w], sb["onecb"][:, 0:1],
                                   sb["bigb"][:, VE + i * 1040 + colh * 512:
                                              VE + i * 1040 + colh * 512 + w],
                                   start=(i == 0), stop=(i == 7))
                ins.then_inc(b.CT.h, 1)
            b.t(tvs)
            b.CT.n += 1
            ctn = b.CT.n

            def vvs(e, ctn=ctn, colh=colh, w=w):
                e.wait_ge(b.CT.h, ctn)
                e.tensor_tensor(sb["qrow"][0:1, colh * 512: colh * 512 + w],
                                sb["qrow"][0:1, colh * 512: colh * 512 + w],
                                b.psf[4].ap()[0:1, 0:w], ALU.add).then_inc(b.CV.h, 1)
            b.v(vvs)
            b.CV.n += 1
        if not hasattr(b, "s1_vext_free"):
            b.s1_vext_free = {}
        b.s1_vext_free[q + 1] = b.CV.n
        b.ps45_cv = b.CV.n
        ct_q[q] = b.CT.n

    vstep(b, lambda e: e.tensor_copy(sb["red1"][:, :], sb["dmax"][:, :]))


def p_head_sweep1(b, head, kb, base, kfb, cs_k):
    sb = b.sb
    KFW = 257
    ctd = {}
    cs0 = b.CS.n
    for i in range(8):
        ps = b.psf[4 + (i % 2)]

        def td(e, i=i, ps=ps, cs_k=cs_k, kb=kb, base=base,
               pcs=b.ps45_cs, pcv=b.ps45_cv, cs0=cs0):
            if i == 0:
                e.wait_ge(b.CV.h, cs_k)
            if i < 2:
                e.wait_ge(b.CS.h, pcs)
                e.wait_ge(b.CV.h, pcv)
            if i >= 2:
                e.wait_ge(b.CS.h, cs0 + i - 1)
            e.matmul(ps.ap()[:, 0:257],
                     sb["bigb"][base:base + 64,
                                KP + kb * 1024 + i * 128: KP + kb * 1024 + (i + 1) * 128],
                     sb["projc2"][base:base + 64, 0:257],
                     start=True, stop=False)
            e.matmul(ps.ap()[:, 0:257],
                     sb["bigb"][base:base + 64,
                                KQ + kb * 1024 + i * 128: KQ + kb * 1024 + (i + 1) * 128],
                     sb["nhalf2"][base:base + 64, 0:257],
                     start=False, stop=True).then_inc(b.CT.h, 1)
        b.t(td)
        b.CT.n += 1
        ctd[i] = b.CT.n

    def se(e, ctd=dict(ctd), kfb=kfb):
        for i in range(8):
            e.wait_ge(b.CT.h, ctd[i])
            ps = b.psf[4 + (i % 2)]
            e.activation(sb["bigb"][:, KF + kfb * 2056 + i * KFW:
                                    KF + kfb * 2056 + i * KFW + KFW],
                         ps.ap()[:, 0:257], AF.Exp).then_inc(b.CS.h, 1)
    b.s(se)
    b.CS.n += 8
    cs_last = b.CS.n
    b.ps45_cs = cs_last

    wait(b, "v", b.CS, cs_last)
    vstep(b, lambda e, kfb=kfb: e.tensor_reduce(
        sb["red1"][:, :],
        sb["bigb"][:, KF + kfb * 2056: KF + kfb * 2056 + 2056]
        .rearrange("p (i o) -> p i o", o=KFW)[:, :, 256:257].transpose([0, 2, 1]),
        axis=AX, op=ALU.max))
    vstep(b, lambda e: e.tensor_tensor(sb["dmax"][:, :], sb["dmax"][:, :],
                                       sb["red1"][:, :], ALU.max))

    cv_prev = b.ctx_evac_cv

    def tc(e, cs_last=cs_last, head=head, kfb=kfb, cv_prev=cv_prev):
        e.wait_ge(b.CS.h, cs_last)
        e.wait_ge(b.CV.h, cv_prev)
        for i in range(8):
            ins = e.matmul(b.psf[6].ap()[0:65, 0:256],
                           sb["bigb"][:, VE + i * 1040 + head * 65: VE + i * 1040 + head * 65 + 65],
                           sb["bigb"][:, KF + kfb * 2056 + i * KFW: KF + kfb * 2056 + i * KFW + 256],
                           start=(i == 0), stop=(i == 7))
        ins.then_inc(b.CT.h, 1)
    b.t(tc)
    b.CT.n += 1
    ctc = b.CT.n

    def vc(e, ctc=ctc, head=head):
        e.wait_ge(b.CT.h, ctc)
        e.tensor_tensor(sb["ctx_sb"][0:65, head * 256:(head + 1) * 256],
                        sb["ctx_sb"][0:65, head * 256:(head + 1) * 256],
                        b.psf[6].ap()[0:65, 0:256], ALU.add).then_inc(b.CV.h, 1)
    b.v(vc)
    b.CV.n += 1
    b.ctx_evac_cv = b.CV.n


def p_ar_ctx(b):
    sb, sd = b.sb, b.sd
    cvn = b.CV.n
    CW = 65 * HEADS * 256

    def g1(e, cvn=cvn):
        e.wait_ge(b.CV.h, cvn)
        e.dma_start(out=sd["arm_i"].ap(),
                    in_=sb["trow"][0:1, 4:5]).then_inc(b.DG.h, 16)
        e.dma_start(out=sd["arc_i"][0:1, 0:CW].rearrange("o (p w) -> (o p) w", p=65),
                    in_=sb["ctx_sb"][:, :]).then_inc(b.DG.h, 16)
        e.dma_start(out=sd["arc_i"][0:1, CW:CW + 1040],
                    in_=sb["qrow"][:, :]).then_inc(b.DG.h, 16)
    b.g(g1)
    b.DG.n += 48
    dgn = b.DG.n

    def g2(e, dgn=dgn):
        e.wait_ge(b.DG.h, dgn)
        e.collective_compute("AllReduce", ALU.max,
                             replica_groups=[[0, 1, 2, 3, 4, 5, 6, 7]],
                             ins=[sd["arm_i"].ap()], outs=[sd["arm_o"].ap()]) \
            .then_inc(b.CC.h, 1)
        e.collective_compute("AllReduce", ALU.add,
                             replica_groups=[[0, 1, 2, 3], [4, 5, 6, 7]],
                             ins=[sd["arc_i"].ap()],
                             outs=[sd["arc_o"].ap()]) \
            .then_inc(b.CC.h, 1)
    b.g(g2)
    b.CC.n += 2
    ccn = b.CC.n

    def g3(e, ccn=ccn):
        e.wait_ge(b.CC.h, ccn)
        e.dma_start(out=sb["trow"][0:1, 5:6], in_=sd["arm_o"].ap()).then_inc(b.DG.h, 16)
        e.dma_start(out=sb["ctx_sb"][:, :],
                    in_=sd["arc_o"][0:1, 0:CW].rearrange("o (p w) -> (o p) w", p=65)).then_inc(b.DG.h, 16)
        e.dma_start(out=sb["qrow"][:, :],
                    in_=sd["arc_o"][0:1, CW:CW + 1040]).then_inc(b.DG.h, 16)
    b.g(g3)
    b.DG.n += 48
    dgn2 = b.DG.n

    def g4(e, dgn2=dgn2):
        e.wait_ge(b.DG.h, dgn2)
        e.nop().then_inc(b.CG.h, 1)
    b.g(g4)
    b.CG.n += 1


def p_ctx_fix(b):
    sb = b.sb
    wait(b, "v", b.CG, b.CG.n)
    vstep(b, lambda e: e.reciprocal(sb["trow"][0:1, 11:12], sb["trow"][0:1, 5:6]))
    vstep(b, lambda e: e.tensor_scalar(sb["trow"][0:1, 11:12], sb["trow"][0:1, 11:12],
                                       MIS, None, ALU.mult))
    bcast_exact(b, 11, sb["pad"], 4, rows=65)
    wait(b, "v", b.CG, b.CG.n)

    for head in range(HEADS):
        cvn = b.CV.n

        def t1(e, head=head, cvn=cvn):
            e.wait_ge(b.CV.h, cvn)
            e.transpose(b.psf[5].ap()[0:65, 0:1],
                        sb["qrow"][0:1, head * 65:(head + 1) * 65],
                        sb["identf"][0:1, 0:1]).then_inc(b.CT.h, 1)
        b.t(t1)
        b.CT.n += 1
        ctn = b.CT.n

        def v1(e, ctn=ctn):
            e.wait_ge(b.CT.h, ctn)
            e.tensor_scalar(sb["dcol"][0:65, 0:1], b.psf[5].ap()[0:65, 0:1],
                            MIS * KEPS, None, ALU.mult).then_inc(b.CV.h, 1)
        b.v(v1)
        b.CV.n += 1
        vstep(b, lambda e, head=head: e.tensor_scalar(
            sb["ctx_sb"][0:65, head * 256:(head + 1) * 256],
            sb["ctx_sb"][0:65, head * 256:(head + 1) * 256],
            sb["pad"][0:65, 4:5], sb["dcol"][0:65, 0:1], ALU.mult, ALU.add))
        vstep(b, lambda e, head=head: e.tensor_copy(
            sb["bigd"][0:65, 0:256],
            sb["ctx_sb"][0:65, head * 256:(head + 1) * 256]))
        cvn2 = b.CV.n

        def t2(e, cvn2=cvn2):
            e.wait_ge(b.CV.h, cvn2)
            e.transpose(b.psb.ap()[0:128, 0:65], sb["bigd"][0:65, 0:128],
                        sb["identb"][0:65, 0:65])
            e.transpose(b.psb.ap()[0:128, 128:193], sb["bigd"][0:65, 128:256],
                        sb["identb"][0:65, 0:65]).then_inc(b.CT.h, 1)
        b.t(t2)
        b.CT.n += 1
        ctn2 = b.CT.n

        def v3(e, ctn2=ctn2, head=head):
            e.wait_ge(b.CT.h, ctn2)
            e.tensor_copy(sb["ctxT"][:, head * 130: head * 130 + 65],
                          b.psb.ap()[0:128, 0:65])
            e.tensor_copy(sb["ctxT"][:, head * 130 + 65: head * 130 + 130],
                          b.psb.ap()[0:128, 128:193]).then_inc(b.CV.h, 1)
        b.v(v3)
        b.CV.n += 1
        cvn3 = b.CV.n

        def t3(e, cvn3=cvn3, head=head):
            e.wait_ge(b.CV.h, cvn3)
            e.matmul(b.psf[5].ap()[0:1, 0:65], sb["onecb"][:, 0:1],
                     sb["ctxT"][:, head * 130: head * 130 + 65],
                     start=True, stop=False)
            e.matmul(b.psf[5].ap()[0:1, 0:65], sb["onecb"][:, 0:1],
                     sb["ctxT"][:, head * 130 + 65: head * 130 + 130],
                     start=False, stop=True).then_inc(b.CT.h, 1)
        b.t(t3)
        b.CT.n += 1
        ctn3 = b.CT.n

        def v4(e, ctn3=ctn3, head=head):
            e.wait_ge(b.CT.h, ctn3)
            e.tensor_scalar(sb["eps_row"][0:1, head * 65:(head + 1) * 65],
                            b.psf[5].ap()[0:1, 0:65], KEPS, None, ALU.mult) \
                .then_inc(b.CV.h, 1)
        b.v(v4)
        b.CV.n += 1


def p_sweep2(b):
    sb, sd = b.sb, b.sd
    NQ = b.NQ
    QP = 0
    QF = 2 * 1024
    AT = QF + 2 * 2048

    for hc in range(HC):
        def ld(e, hc=hc):
            e.dma_start(out=sb["biga"][:, hc * 1024:(hc + 1) * 1024],
                        in_=sd["s_wqn"][0:H, hc * 128:(hc + 1) * 128],
                        transpose=True).then_inc(b.D.h, 16)
        b.y(ld)
        b.D.n += 16
    for hc in range(HC):
        def ld2(e, hc=hc):
            e.dma_start(out=sb["biga"][:, 8192 + hc * 1024: 9216 + hc * 1024],
                        in_=sd["s_won"][0:H, hc * 128:(hc + 1) * 128],
                        transpose=True).then_inc(b.D.h, 16)
        b.y(ld2)
        b.D.n += 16
    d_w = b.D.n

    ct_q = {}
    at_store = {}
    b.s2_prev_csqf = b.CS.n
    b.s2_prev_atev = b.CV.n
    for q in range(NQ):
        qb = sb["bigd"] if q % 2 == 0 else sb["bigc"]
        dq = b.DQ0 if q % 2 == 0 else b.DQ1
        for hc in range(HC):
            def xl(e, hc=hc, q=q, qb=qb, dq=dq):
                if hc == 0 and q >= 2:
                    e.wait_ge(b.CT.h, ct_q[q - 2])
                e.dma_start(out=qb[:, hc * 1024:(hc + 1) * 1024],
                            in_=sd["s_ln2"][q * 1024:(q + 1) * 1024,
                                            hc * 128:(hc + 1) * 128],
                            transpose=True).then_inc(dq.h, 16)
            b.y(xl)
            dq.n += 16
        d_x = dq.n

        qdone = {}
        at_evac = {}
        for pr in range(NPAIR):
            kb = pr % 2

            def tq(e, pr=pr, kb=kb, d_x=d_x, q=q, patev=b.s2_prev_atev, dq=dq):
                if pr == 0:
                    e.wait_ge(b.D.h, d_w)
                    e.wait_ge(dq.h, d_x)
                    if q >= 1:
                        e.wait_ge(b.DS.h, at_store[q - 1])
                if pr >= 2:
                    e.wait_ge(b.CV.h, qdone[pr - 2])
                e.wait_ge(b.CV.h, patev)
                for g in range(2):
                    ps = b.psf[0 + g]
                    for hc in range(HC):
                        ins = e.matmul(ps.ap()[:, 0:512],
                                       sb["biga"][:, hc * 1024 + pr * 128: hc * 1024 + (pr + 1) * 128],
                                       qb[:, hc * 1024 + g * 512: hc * 1024 + g * 512 + 512],
                                       start=(hc == 0), stop=(hc == HC - 1))
                ins.then_inc(b.CT.h, 1)
            b.t(tq)
            b.CT.n += 1
            ctq = b.CT.n

            def sq(e, ctq=ctq, pr=pr, kb=kb):
                e.wait_ge(b.CT.h, ctq)
                for g in range(2):
                    ps = b.psf[0 + g]
                    ins = e.tensor_scalar(sb["bigb"][:, QP + kb * 1024 + g * 512:
                                                     QP + kb * 1024 + g * 512 + 512],
                                          ps.ap()[:, 0:512],
                                          sb["b_tab"][:, pr: pr + 1], None, ALU.add)
                ins.then_inc(b.CV.h, 1)
            b.v(sq)
            b.CV.n += 1
            qdone[pr] = b.CV.n

            for hh in range(2):
                head = 2 * pr + hh
                p_head_sweep2(b, q, head, kb, hh * 64, head % 2,
                              qdone[pr], QP, QF, AT, at_evac)

        cv_at = b.CV.n

        def sat(e, cv_at=cv_at, q=q):
            e.wait_ge(b.CV.h, cv_at)
            for i in range(8):
                e.dma_start(out=sd["s_att"][q * 1024 + i * 128: q * 1024 + (i + 1) * 128, :],
                            in_=sb["bigb"][:, AT + i * 1024: AT + (i + 1) * 1024]) \
                    .then_inc(b.DS.h, 16)
        b.s(sat)
        b.DS.n += 128
        at_store[q] = b.DS.n
        ct_q[q] = b.CT.n
    b.s2_at_store = at_store


def p_head_sweep2(b, q, head, kb, base, qfb, cs_q, QP, QF, AT, at_evac):
    sb = b.sb
    ctd = {}
    cs0 = b.CS.n
    for mc in range(2):
        for g in range(2):
            ps = b.psf[2 + 2 * mc + g]

            def td(e, mc=mc, g=g, ps=ps, cs_q=cs_q, kb=kb, base=base, head=head,
                   pcs=b.s2_prev_csqf):
                if mc == 0 and g == 0:
                    e.wait_ge(b.CV.h, cs_q)
                    e.wait_ge(b.CS.h, pcs)
                e.matmul(ps.ap()[:, 0:512],
                         sb["projc2"][base:base + 64, mc * 128:(mc + 1) * 128],
                         sb["bigb"][base:base + 64,
                                    QP + kb * 1024 + g * 512: QP + kb * 1024 + g * 512 + 512],
                         start=True, stop=True).then_inc(b.CT.h, 1)
            b.t(td)
            b.CT.n += 1
            ctd[(mc, g)] = b.CT.n

    def se(e, ctd=dict(ctd), qfb=qfb):
        for mc in range(2):
            for g in range(2):
                e.wait_ge(b.CT.h, ctd[(mc, g)])
                ps = b.psf[2 + 2 * mc + g]
                e.activation(sb["bigb"][:, QF + qfb * 2048 + mc * 1024 + g * 512:
                                        QF + qfb * 2048 + mc * 1024 + g * 512 + 512],
                             ps.ap()[:, 0:512], AF.Exp).then_inc(b.CS.h, 1)
    b.s(se)
    b.CS.n += 4
    cs_qf = b.CS.n
    b.s2_prev_csqf = cs_qf

    for i in range(8):
        bank = b.psf[6] if i < 4 else b.psf[1]
        off = (i % 4) * 128

        def to(e, i=i, bank=bank, off=off, cs_qf=cs_qf, qfb=qfb, head=head,
               patev=b.s2_prev_atev):
            if i == 0:
                e.wait_ge(b.CS.h, cs_qf)
                e.wait_ge(b.CV.h, patev)
            if i == 4:
                e.wait_ge(b.CV.h, at_evac.get((head, 7), patev))
            g = i // 4
            loc = (i % 4) * 128
            for mc in range(2):
                e.matmul(bank.ap()[:, off: off + 65],
                         sb["bigb"][:, QF + qfb * 2048 + mc * 1024 + g * 512 + loc:
                                    QF + qfb * 2048 + mc * 1024 + g * 512 + loc + 128],
                         sb["ctxT"][:, head * 130 + mc * 65: head * 130 + (mc + 1) * 65],
                         start=(mc == 0), stop=False)
            e.matmul(bank.ap()[:, off: off + 65], sb["onerb"][0:1, 0:128],
                     sb["eps_row"][0:1, head * 65:(head + 1) * 65],
                     start=False, stop=True).then_inc(b.CT.h, 1)
        b.t(to)
        b.CT.n += 1
        if i % 4 == 3:
            grp = i // 4
            bank2 = bank
            cto = b.CT.n

            def ve(e, cto=cto, bank2=bank2):
                e.wait_ge(b.CT.h, cto)
                e.reciprocal(sb["dcol"][:, 0:1], bank2.ap()[:, 64:65])
                e.reciprocal(sb["dcol"][:, 1:2], bank2.ap()[:, 192:193])
                e.reciprocal(sb["dcol"][:, 2:3], bank2.ap()[:, 320:321])
                e.reciprocal(sb["dcol"][:, 3:4], bank2.ap()[:, 448:449]) \
                    .then_inc(b.CV.h, 1)
            b.v(ve)
            b.CV.n += 1
            for k in range(4):
                i2 = grp * 4 + k
                vstep(b, lambda e, i2=i2, bank2=bank2, k=k, head=head: e.tensor_scalar(
                    sb["bigb"][:, AT + i2 * 1024 + head * 64: AT + i2 * 1024 + head * 64 + 64],
                    bank2.ap()[:, k * 128: k * 128 + 64], sb["dcol"][:, k:k + 1],
                    None, ALU.mult))
                at_evac[(head, i2)] = b.CV.n
    b.s2_prev_atev = b.CV.n


def p_wo(b, do_stats=True):
    sb, sd = b.sb, b.sd
    NQ = b.NQ
    NCH = b.NCH

    dg0 = b.DG.n
    xevac = {}
    stats_cv = {}

    xdn = {}

    def xload(i, wait_cv):
        gx = b.GXA if i % 2 == 0 else b.GXB

        def xle(e, i=i, wait_cv=wait_cv, gx=gx):
            if wait_cv:
                e.wait_ge(b.CV.h, wait_cv)
            e.dma_start(out=sb["xstg"][:, (i % 2) * H:(i % 2 + 1) * H],
                        in_=sd["s_x2"][i * 128:(i + 1) * 128, :]).then_inc(gx.h, 16)
        b.g(xle)
        gx.n += 16
        xdn[i] = (gx, gx.n)

    xload(0, 0)
    xload(1, 0)

    stores = {}
    ct_q = {}
    for q in range(NQ):
        qb = sb["bigd"] if q % 2 == 0 else sb["bigc"]
        dq = b.DQ0 if q % 2 == 0 else b.DQ1
        for hc in range(HC):
            def xl(e, hc=hc, q=q, qb=qb, dq=dq):
                if hc == 0:
                    if q == 0:
                        e.wait_ge(b.DS.h, b.s2_at_store[b.NQ - 1])
                    elif q >= 2:
                        e.wait_ge(b.CT.h, ct_q[q - 2])
                e.dma_start(out=qb[:, hc * 1024:(hc + 1) * 1024],
                            in_=sd["s_att"][q * 1024:(q + 1) * 1024,
                                            hc * 128:(hc + 1) * 128],
                            transpose=True).then_inc(dq.h, 16)
            b.y(xl)
            dq.n += 16
        d_x = dq.n

        for i in range(8):
            nch = q * 8 + i
            psA = b.psf[0 + 2 * (i % 2)]
            psB = b.psf[1 + 2 * (i % 2)]

            def tw(e, i=i, nch=nch, psA=psA, psB=psB, d_x=d_x, qb=qb, dq=dq):
                if i == 0:
                    e.wait_ge(dq.h, d_x)
                if nch >= 2:
                    e.wait_ge(b.CV.h, stats_cv.get(nch - 2, xevac[nch - 2]))
                for half, ps in ((0, psA), (1, psB)):
                    for oc in range(HC):
                        e.matmul(ps.ap()[:, 0:512],
                                 qb[:, oc * 1024 + i * 128: oc * 1024 + (i + 1) * 128],
                                 sb["biga"][:, 8192 + oc * 1024 + half * 512:
                                            8192 + oc * 1024 + half * 512 + 512],
                                 start=(oc == 0), stop=False)
                    ins = e.matmul(ps.ap()[:, 0:512], sb["onerb"][0:1, 0:128],
                                   sb["brow"][0:1, half * 512: half * 512 + 512],
                                   start=False, stop=True)
                ins.then_inc(b.CT.h, 1)
            b.t(tw)
            b.CT.n += 1
            ctn = b.CT.n
            ob_bf = sb["ostg"].ap().bitcast(BF16)

            def vv(e, ctn=ctn, nch=nch, psA=psA, psB=psB):
                e.wait_ge(b.CT.h, ctn)
                gx2, gv2 = xdn[nch]
                e.wait_ge(gx2.h, gv2)
                if nch >= 2:
                    ss2, sv2 = stores[nch - 2]
                    e.wait_ge(ss2.h, sv2)
                    if nch - 2 in stats_cv:
                        e.wait_ge(b.CV.h, stats_cv[nch - 2])
                xb = (nch % 2) * H
                ob = (nch % 2)
                e.scalar_tensor_tensor(ob_bf[:, ob * 2 * H: ob * 2 * H + 512],
                                       psA.ap()[:, 0:512], 1.0,
                                       sb["xstg"][:, xb: xb + 512], ALU.mult, ALU.add)
                e.scalar_tensor_tensor(ob_bf[:, ob * 2 * H + 512: ob * 2 * H + 1024],
                                       psB.ap()[:, 0:512], 1.0,
                                       sb["xstg"][:, xb + 512: xb + 1024],
                                       ALU.mult, ALU.add).then_inc(b.CV.h, 1)
            b.v(vv)
            b.CV.n += 1
            xevac[nch] = b.CV.n
            if do_stats:
                obq = (nch % 2)
                ch = ob_bf[:, obq * 2 * H: obq * 2 * H + 1024]
                vstep(b, lambda e, ch=ch, nch=nch: e.tensor_reduce(
                    sb["m_tab"][:, nch:nch + 1], ch, axis=AX, op=ALU.add))
                vstep(b, lambda e, ch=ch: e.tensor_tensor(
                    sb["sqt"][:, :], ch, ch, ALU.mult))
                vstep(b, lambda e, nch=nch: e.tensor_reduce(
                    sb["v_tab"][:, nch:nch + 1], sb["sqt"][:, :],
                    axis=AX, op=ALU.add))
                vstep(b, lambda e, ch=ch, nch=nch: e.tensor_reduce(
                    sb["xmx"][:, nch:nch + 1], ch, axis=AX, op=ALU.max))
                vstep(b, lambda e, ch=ch, nch=nch: e.tensor_reduce(
                    sb["xmn"][:, nch:nch + 1], ch, axis=AX, op=ALU.min))
                stats_cv[nch] = b.CV.n
            cvn = b.CV.n

            ssm = b.SSA if nch % 2 == 0 else b.SSB

            def st(e, cvn=cvn, nch=nch, ssm=ssm):
                e.wait_ge(b.CV.h, cvn)
                ob = (nch % 2)
                e.dma_start(out=sd["s_x3"][nch * 128:(nch + 1) * 128, :],
                            in_=sb["ostg"].ap().bitcast(BF16)[:, ob * 2 * H: ob * 2 * H + 1024]) \
                    .then_inc(ssm.h, 16)
            b.s(st)
            ssm.n += 16
            stores[nch] = (ssm, ssm.n)
            if nch + 2 < NCH:
                xload(nch + 2, xevac[nch])
        ct_q[q] = b.CT.n


def p_finish(b):
    if b.dbg_t:
        sd = b.sd

        def gdump(e):
            e.dma_start(out=b.dbg_t["d_x2"].ap(), in_=sd["s_x2"].ap()).then_inc(b.DG.h, 16)
            e.dma_start(out=b.dbg_t["d_x3"].ap(), in_=sd["s_x3"].ap()).then_inc(b.DG.h, 16)
            e.dma_start(out=b.dbg_t["d_ln2"].ap(), in_=sd["s_ln2"].ap()).then_inc(b.DG.h, 16)
            e.dma_start(out=b.dbg_t["d_xq"].ap(), in_=sd["s_xq"].ap()).then_inc(b.DG.h, 16)
            e.dma_start(out=b.dbg_t["d_h"].ap(), in_=sd["s_h"].ap()).then_inc(b.DG.h, 16)
            e.dma_start(out=b.dbg_t["d_att"].ap(), in_=sd["s_att"].ap()).then_inc(b.DG.h, 16)
            e.dma_start(out=b.dbg_t["d_trow"][0:1, 0:12], in_=b.sb["trow"][0:1, 0:12]).then_inc(b.DG.h, 16)
            e.dma_start(out=b.dbg_t["d_ctx"].ap(), in_=b.sb["ctx_sb"][:, :]).then_inc(b.DG.h, 16)
            e.dma_start(out=b.dbg_t["d_qrow"].ap(), in_=b.sb["qrow"][:, :]).then_inc(b.DG.h, 16)
        b.g(gdump)
        b.DG.n += 144
    sems = [b.D, b.DS, b.DG, b.DXA, b.DXB, b.GXA, b.GXB, b.SSA, b.SSB, b.DH, b.DHB, b.DW, b.DQ0, b.DQ1, b.GSA, b.GSB]
    vals = [s.n for s in sems]

    def y1(e):
        for sm, v in zip(sems, vals):
            e.wait_ge(sm.h, v)
    b.y(y1)



_CACHE = {}


def kernel(L0_attn_hidden_states, L0_attn_Wq, L0_attn_bq, L0_attn_Wk, L0_attn_bk,
           L0_attn_Wv, L0_attn_bv, L0_attn_Wo, L0_attn_bo, L0_attn_proj,
           L0_ffn_clip_x, L0_ffn_clip_down, L0_ffn_w_gate, L0_ffn_w_up,
           L0_ffn_w_down):
    x = np.ascontiguousarray(np.asarray(L0_attn_hidden_states, dtype=np.float32))
    Bz, N, Hd = x.shape
    assert Hd == H
    ncores = 8
    per_batch = ncores // Bz
    R = N // per_batch
    xf = x.reshape(Bz * N, Hd)

    if R not in _CACHE:
        _CACHE[R] = build(R)
    nc = _CACHE[R]

    com = {
        "wq": np.ascontiguousarray(L0_attn_Wq, dtype=np.float32),
        "wk": np.ascontiguousarray(L0_attn_Wk, dtype=np.float32),
        "wv": np.ascontiguousarray(L0_attn_Wv, dtype=np.float32),
        "wo": np.ascontiguousarray(L0_attn_Wo, dtype=np.float32),
        "bq": np.ascontiguousarray(L0_attn_bq, dtype=np.float32).reshape(1, H),
        "bk": np.ascontiguousarray(L0_attn_bk, dtype=np.float32).reshape(1, H),
        "bv": np.ascontiguousarray(L0_attn_bv, dtype=np.float32).reshape(1, H),
        "bo": np.ascontiguousarray(L0_attn_bo, dtype=np.float32).reshape(1, H),
        "proj": np.ascontiguousarray(L0_attn_proj, dtype=np.float32),
        "clip_x": np.ascontiguousarray(L0_ffn_clip_x, dtype=np.float32).reshape(1, 1),
        "clip_down": np.ascontiguousarray(L0_ffn_clip_down, dtype=np.float32).reshape(1, 1),
        "w_gate": np.ascontiguousarray(L0_ffn_w_gate, dtype=np.float32),
        "w_up": np.ascontiguousarray(L0_ffn_w_up, dtype=np.float32),
        "w_down": np.ascontiguousarray(L0_ffn_w_down, dtype=np.float32),
    }
    in_maps = []
    for c in range(ncores):
        m = dict(com)
        m["x_in"] = np.ascontiguousarray(xf[c * R:(c + 1) * R, :])
        in_maps.append(m)

    res = run_bass_kernel_spmd(nc, in_maps, core_ids=list(range(ncores)))
    out = np.concatenate([np.asarray(r["y"]) for r in res.results], axis=0)
    return out.reshape(Bz, N, Hd).astype(np.float32)


# version 15 (final)
# speedup vs baseline: 1.0172x; 1.0172x over previous
import numpy as np

import concourse.bass as bass
import concourse.mybir as mybir
from concourse.bass_utils import run_bass_kernel_spmd

F32 = mybir.dt.float32
BF16 = mybir.dt.bfloat16
I32 = mybir.dt.int32
AF = mybir.ActivationFunctionType
ALU = mybir.AluOpType
AX = mybir.AxisListType.X

H = 1024
HC = 8
ID = 2816
IC = 22
M = 256
DH = 64
HEADS = 16
NPAIR = 8
QMAX = 127.0
CSH = 16384.0
LN_EPS = 1e-5
KEPS = 1e-3
CDAT = float(DH) ** -0.25
MIS = float(M) ** -0.5


class Sem:
    def __init__(self, h):
        self.h = h
        self.n = 0


class B:
    def __init__(self, nc, block):
        self.nc = nc
        self.block = block

    def t(self, f):
        self.block.tensor(f)

    def v(self, f):
        cv = self.CV.n

        def w(e):
            e.wait_ge(self.CV.h, cv)
            f(e)
        self.block.vector(w)

    def s(self, f):
        cs = self.CS.n

        def w(e):
            e.wait_ge(self.CS.h, cs)
            f(e)
        self.block.scalar(w)

    def g(self, f):
        self.block.gpsimd(f)

    def y(self, f):
        self.block.sync(f)

    def barrier(self):
        ph = self.PH
        sems = [self.D, self.DS, self.DG, self.DXA, self.DXB,
                self.GXA, self.GXB, self.SSA, self.SSB, self.DH, self.DHB, self.DW, self.DQ0, self.DQ1, self.GSA, self.GSB]
        vals = [s.n for s in sems]
        bar = self.sb["ebar"]

        def gdrain(e):
            for sm, v in zip(sems, vals):
                e.wait_ge(sm.h, v)
            e.memset(bar[0:1, 2:3], 0.0).then_inc(ph.h, 1)
        self.block.gpsimd(gdrain)
        self.block.sync(lambda e: e.nop().then_inc(ph.h, 1))
        self.block.scalar(lambda e: e.memzero(bar[0:1, 1:2]).then_inc(ph.h, 1))
        self.block.vector(lambda e: e.memset(bar[0:1, 0:1], 0.0).then_inc(ph.h, 1))
        self.block.tensor(lambda e: e.nop().then_inc(ph.h, 1))
        ph.n += 5
        tgt = ph.n
        for name in ("sync", "scalar", "vector", "tensor", "gpsimd"):
            def mk2():
                def f(e):
                    e.wait_ge(ph.h, tgt)
                return f
            getattr(self.block, name)(mk2())


def vstep(b, fn):
    def f(e):
        fn(e).then_inc(b.CV.h, 1)
    b.v(f)
    b.CV.n += 1
    return b.CV.n


def sstep(b, fn):
    def f(e):
        fn(e).then_inc(b.CS.h, 1)
    b.s(f)
    b.CS.n += 1
    return b.CS.n


def wait(b, eng, sem, val):
    def f(e):
        e.wait_ge(sem.h, val)
    getattr(b, eng)(f)


def build(R, dbg=False):
    assert R % 1024 == 0
    NCH = R // 128
    NQ = R // 1024

    nc = bass.Bass("TRN2", target_bir_lowering=False)

    io = {}
    io["x_in"] = nc.dram_tensor("x_in", [R, H], F32, kind="ExternalInput")
    for nm, shp in (("wq", [H, H]), ("wk", [H, H]), ("wv", [H, H]), ("wo", [H, H]),
                    ("bq", [1, H]), ("bk", [1, H]), ("bv", [1, H]), ("bo", [1, H]),
                    ("proj", [M, DH]), ("clip_x", [1, 1]), ("clip_down", [1, 1]),
                    ("w_gate", [ID, H]), ("w_up", [ID, H]), ("w_down", [H, ID])):
        io[nm] = nc.dram_tensor(nm, shp, F32, kind="ExternalInput")
    y_out = nc.dram_tensor("y", [R, H], F32, kind="ExternalOutput")
    dbg_t = {}
    if dbg:
        for nm, shp, dt in (("d_x2", [R, H], BF16), ("d_x3", [R, H], BF16),
                            ("d_ln2", [R, H], BF16), ("d_xq", [R, H], BF16),
                            ("d_h", [IC * 128, R], BF16),
                            ("d_trow", [1, 16], F32),
                            ("d_ctx", [65, HEADS * 256], F32),
                            ("d_qrow", [1, 1040], F32),
                            ("d_att", [R, H], BF16)):
            dbg_t[nm] = nc.dram_tensor(nm, shp, dt, kind="ExternalOutput")

    sd = {}
    for nm, shp, dt in (("s_x2", [R, H], BF16), ("s_x3", [R, H], BF16),
                        ("s_xq", [R, H], BF16), ("s_ln2", [R, H], BF16),
                        ("s_att", [R, H], BF16), ("s_h", [IC * 128, R], BF16),
                        ("s_wgn", [ID, H], BF16), ("s_wun", [ID, H], BF16),
                        ("s_wdn", [H, ID], BF16), ("s_wqn", [H, H], BF16),
                        ("s_wkn", [H, H], BF16), ("s_wvn", [H, H], BF16),
                        ("s_won", [H, H], BF16),
                        ("arm_i", [1, 1], F32), ("arm_o", [1, 1], F32),
                        ("arc_i", [1, 65 * HEADS * 256 + 1040], F32),
                        ("arc_o", [1, 65 * HEADS * 256 + 1040], F32),
                        ("s_pj", [64, 257], BF16)):
        sd[nm] = nc.dram_tensor(nm, shp, dt)

    A = nc.alloc_sbuf_tensor
    sb = {}
    sb["biga"] = A("biga", [128, IC * 1024], BF16)
    sb["bigb"] = A("bigb", [128, IC * 1024], BF16)
    sb["bigc"] = A("bigc", [128, IC * 1024], BF16)
    sb["bigd"] = A("bigd", [128, 11264], BF16)
    sb["xstg"] = A("xstg", [128, 2 * H], F32)
    sb["ostg"] = A("ostg", [128, 2 * H], F32)
    sb["ctx_sb"] = A("ctx_sb", [65, HEADS * 256], F32)
    sb["ctxT"] = A("ctxT", [128, HEADS * 130], BF16)
    sb["qrow"] = A("qrow", [1, 1040], F32)
    sb["eps_row"] = A("eps_row", [1, HEADS * 65], BF16)
    sb["identf"] = A("identf", [128, 128], F32)
    sb["identb"] = A("identb", [128, 128], BF16)
    sb["projc2"] = A("projc2", [128, 257], BF16)
    sb["nhalf2"] = A("nhalf2", [128, 257], BF16)
    sb["projsb"] = A("projsb", [128, 256], F32)
    sb["onecb"] = A("onecb", [128, 1], BF16)
    sb["onerb"] = A("onerb", [1, 128], BF16)
    sb["brow"] = A("brow", [1, 2 * H], BF16)
    sb["b_tab"] = A("b_tab", [128, 16], F32)
    sb["swg"] = A("swg", [128, IC], F32)
    sb["swu"] = A("swu", [128, IC], F32)
    sb["sgx"] = A("sgx", [128, IC], F32)
    sb["sux"] = A("sux", [128, IC], F32)
    sb["m_tab"] = A("m_tab", [128, NCH], F32)
    sb["v_tab"] = A("v_tab", [128, NCH], F32)
    sb["r_tab"] = A("r_tab", [128, NCH], F32)
    sb["xmx"] = A("xmx", [128, NCH], F32)
    sb["xmn"] = A("xmn", [128, NCH], F32)
    sb["rr_tab"] = A("rr_tab", [128, NCH], F32)
    sb["bb_tab"] = A("bb_tab", [128, NCH], F32)
    sb["bns"] = A("bns", [128, 12], F32)
    sb["agg"] = A("agg", [128, 2], F32)
    sb["red1"] = A("red1", [128, 1], F32)
    sb["dmax"] = A("dmax", [128, 1], F32)
    sb["dcol"] = A("dcol", [128, 4], F32)
    sb["pad"] = A("pad", [128, 8], F32)
    sb["trow"] = A("trow", [1, 16], F32)
    sb["ebar"] = A("ebar", [1, 4], F32)
    sb["ti1"] = A("ti1", [1, 1], I32)
    sb["sqt"] = A("sqt", [128, 1024], BF16)

    PSn = nc.alloc_psum_tensor
    psf = [PSn(f"psf{i}", [128, 512], F32) for i in range(7)]
    psb = PSn("psb", [128, 1024], BF16)

    semg = []

    def mksem(name):
        gd = nc.semaphore(name)
        h = gd.__enter__()
        semg.append(gd)
        return Sem(h)

    with nc.Block() as block:
        b = B(nc, block)
        b.io, b.sd, b.sb, b.psf, b.psb = io, sd, sb, psf, psb
        b.y_out = y_out
        b.R, b.NCH, b.NQ = R, NCH, NQ
        b.D = mksem("d_sync")
        b.DS = mksem("d_scal")
        b.DG = mksem("d_gps")
        b.DXA = mksem("d_xa")
        b.DXB = mksem("d_xb")
        b.GXA = mksem("d_ga")
        b.GXB = mksem("d_gb")
        b.SSA = mksem("d_sa")
        b.SSB = mksem("d_sb")
        b.DH = mksem("d_hl")
        b.DHB = mksem("d_hlb")
        b.DW = mksem("d_wc")
        b.DQ0 = mksem("d_q0")
        b.DQ1 = mksem("d_q1")
        b.GSA = mksem("d_gsa")
        b.GSB = mksem("d_gsb")
        b.dbg_t = dbg_t
        b.CV = mksem("c_vec")
        b.CS = mksem("c_sca")
        b.CT = mksem("c_ten")
        b.CG = mksem("c_gps")
        b.CC = mksem("c_cc")
        b.PH = mksem("c_ph")
        emit(b)

    for gd in reversed(semg):
        gd.__exit__(None, None, None)
    return nc



def colmax_to_trow(b, slot):
    cvn = b.CV.n
    ps = b.psf[6]

    def t(e):
        e.wait_ge(b.CV.h, cvn)
        e.transpose(ps.ap()[0:1, 0:128], b.sb["red1"][:, :], b.sb["identf"][:, :]) \
            .then_inc(b.CT.h, 1)
    b.t(t)
    b.CT.n += 1
    ctn = b.CT.n

    def v(e):
        e.wait_ge(b.CT.h, ctn)
        e.tensor_reduce(b.sb["trow"][0:1, slot:slot + 1], ps.ap()[0:1, 0:128],
                        axis=AX, op=ALU.max).then_inc(b.CV.h, 1)
    b.v(v)
    b.CV.n += 1


def armax(b, slot_in, slot_out):
    cvn = b.CV.n
    dg = b.DG

    def g1(e):
        e.wait_ge(b.CV.h, cvn)
        e.dma_start(out=b.sd["arm_i"].ap(),
                    in_=b.sb["trow"][0:1, slot_in:slot_in + 1]).then_inc(dg.h, 16)
    b.g(g1)
    dg.n += 16
    dgn = dg.n

    def g2(e):
        e.wait_ge(dg.h, dgn)
        e.collective_compute("AllReduce", ALU.max,
                             replica_groups=[[0, 1, 2, 3, 4, 5, 6, 7]],
                             ins=[b.sd["arm_i"].ap()], outs=[b.sd["arm_o"].ap()]) \
            .then_inc(b.CC.h, 1)
    b.g(g2)
    b.CC.n += 1
    ccn = b.CC.n

    def g3(e):
        e.wait_ge(b.CC.h, ccn)
        e.dma_start(out=b.sb["trow"][0:1, slot_out:slot_out + 1],
                    in_=b.sd["arm_o"].ap()).then_inc(dg.h, 16)
    b.g(g3)
    dg.n += 16
    dgn2 = dg.n

    def g4(e):
        e.wait_ge(dg.h, dgn2)
        e.nop().then_inc(b.CG.h, 1)
    b.g(g4)
    b.CG.n += 1


def bcast_exact(b, src_col, dst, dst_col, rows=128):
    dg = b.DG
    cvn = b.CV.n

    def g1(e):
        e.wait_ge(b.CV.h, cvn)
        e.dma_start(out=b.sd["arm_i"].ap(),
                    in_=b.sb["trow"][0:1, src_col:src_col + 1]).then_inc(dg.h, 16)
    b.g(g1)
    dg.n += 16
    dgn = dg.n

    def g2(e):
        e.wait_ge(dg.h, dgn)
        src = b.sd["arm_i"].ap()
        srcb = bass.AP(src.tensor, src.offset, [[0, rows], [1, 1]])
        e.dma_start(out=dst[0:rows, dst_col:dst_col + 1], in_=srcb).then_inc(dg.h, 16)
    b.g(g2)
    dg.n += 16
    dgn2 = dg.n

    def g3(e):
        e.wait_ge(dg.h, dgn2)
        e.nop().then_inc(b.CG.h, 1)
    b.g(g3)
    b.CG.n += 1



def emit(b):
    p_consts(b)
    p_startup(b)
    p_scales_x(b)
    p_xq(b, b.io["x_in"], "gpsimd")
    b.barrier()
    p_gateup(b)
    colmax_to_trow(b, 4)
    armax(b, 4, 5)
    p_scales_h(b)
    b.barrier()
    p_down(b, b.io["x_in"], b.sd["s_x2"], final=False, do_stats=True)
    p_ln_fin(b, amax=False)
    p_ln_apply(b, b.sd["s_x2"], b.sd["s_ln2"])
    b.barrier()
    p_sweep1(b)
    colmax_to_trow(b, 4)
    p_ar_ctx(b)
    b.barrier()
    p_ctx_fix(b)
    b.barrier()
    p_sweep2(b)
    b.barrier()
    p_wo(b, do_stats=True)
    p_ln_fin(b, amax=True)
    colmax_to_trow(b, 4)
    armax(b, 4, 5)
    p_scales_x(b)
    p_xq(b, b.sd["s_x3"], "gpsimd")
    b.barrier()
    p_gateup(b)
    colmax_to_trow(b, 4)
    armax(b, 4, 5)
    p_scales_h(b)
    b.barrier()
    p_down(b, b.sd["s_x3"], None, final=True)
    p_finish(b)


def p_consts(b):
    io, sb, sd = b.io, b.sb, b.sd

    def g0(e):
        e.memset(sb["identf"][:, :], 0.0)
        e.memset(sb["identb"][:, :], 0.0)
        e.memset(sb["onecb"][:, :], 1.0)
        e.memset(sb["onerb"][:, :], 1.0)
        e.memset(sb["nhalf2"][:, :], -0.5).then_inc(b.CG.h, 1)
    b.g(g0)
    b.CG.n += 1
    cg0 = b.CG.n

    def g(e):
        e.wait_ge(b.CG.h, cg0)
        e.affine_select(out=sb["identf"][:, :], in_=sb["identf"][:, :],
                        compare_op=ALU.not_equal, fill=1.0, base=0,
                        pattern=[[-1, 128]], channel_multiplier=1)
        e.affine_select(out=sb["identb"][:, :], in_=sb["identb"][:, :],
                        compare_op=ALU.not_equal, fill=1.0, base=0,
                        pattern=[[-1, 128]], channel_multiplier=1) \
            .then_inc(b.CG.h, 1)
    b.g(g)
    b.CG.n += 1
    cgn = b.CG.n

    def g2(e):
        e.dma_start(out=sd["s_wqn"].ap(), in_=io["wq"].ap()).then_inc(b.DW.h, 16)
        e.dma_start(out=sd["s_wkn"].ap(), in_=io["wk"].ap()).then_inc(b.DW.h, 16)
        e.dma_start(out=sd["s_wvn"].ap(), in_=io["wv"].ap()).then_inc(b.DW.h, 16)
        e.dma_start(out=sd["s_won"].ap(), in_=io["wo"].ap()).then_inc(b.DW.h, 16)
        e.dma_start(out=sb["brow"][0:1, 0:H], in_=io["bo"].ap()).then_inc(b.DW.h, 16)
        e.dma_start(out=sb["brow"][0:1, H:2 * H], in_=io["bv"].ap()).then_inc(b.DW.h, 16)
    b.g(g2)
    b.DW.n += 96
    b.dg_wqkv = b.DW.n

    def y1(e):
        e.dma_start(out=sb["projsb"][:, 0:64], in_=io["proj"][0:128, :]).then_inc(b.D.h, 16)
        e.dma_start(out=sb["projsb"][:, 64:128], in_=io["proj"][128:256, :]).then_inc(b.D.h, 16)
        e.dma_start(out=sb["b_tab"][:, 0:8],
                    in_=io["bq"].ap().rearrange("o (c p) -> (o p) c", p=128),
                    allow_slow_non_contiguous=True).then_inc(b.D.h, 16)
        e.dma_start(out=sb["b_tab"][:, 8:16],
                    in_=io["bk"].ap().rearrange("o (c p) -> (o p) c", p=128),
                    allow_slow_non_contiguous=True).then_inc(b.D.h, 16)
        e.dma_start(out=sb["trow"][0:1, 0:1], in_=io["clip_x"].ap()).then_inc(b.D.h, 16)
        e.dma_start(out=sb["trow"][0:1, 1:2], in_=io["clip_down"].ap()).then_inc(b.D.h, 16)
    b.y(y1)
    b.D.n += 96
    dn = b.D.n

    def t1(e):
        e.wait_ge(b.D.h, dn)
        e.wait_ge(b.CG.h, cgn)
        e.transpose(b.psf[6].ap()[0:64, 0:128], sb["projsb"][:, 0:64], sb["identf"][:, :])
        e.transpose(b.psf[6].ap()[0:64, 128:256], sb["projsb"][:, 64:128], sb["identf"][:, :]) \
            .then_inc(b.CT.h, 1)
    b.t(t1)
    b.CT.n += 1
    ctn = b.CT.n

    def v1(e):
        e.wait_ge(b.CT.h, ctn)
        e.tensor_scalar(sb["projc2"][0:64, 0:256], b.psf[6].ap()[0:64, 0:256],
                        CDAT, None, ALU.mult)
        e.memset(sb["projc2"][0:64, 256:257], 0.0).then_inc(b.CV.h, 1)
    b.v(v1)
    b.CV.n += 1
    cvp = b.CV.n

    def y2(e):
        e.wait_ge(b.CV.h, cvp)
        e.dma_start(out=sd["s_pj"].ap(), in_=sb["projc2"][0:64, 0:257]).then_inc(b.D.h, 16)
        e.wait_ge(b.D.h, dn + 16)
        e.dma_start(out=sb["projc2"][64:128, 0:257], in_=sd["s_pj"].ap()).then_inc(b.D.h, 16)
    b.y(y2)
    b.D.n += 32

    def s1(e):
        e.wait_ge(b.D.h, dn)
        e.activation(sb["trow"][0:1, 2:3], sb["trow"][0:1, 0:1], AF.Sigmoid)
        e.activation(sb["trow"][0:1, 3:4], sb["trow"][0:1, 1:2], AF.Sigmoid) \
            .then_inc(b.CS.h, 1)
    b.s(s1)
    b.CS.n += 1

    vstep(b, lambda e: e.memset(sb["trow"][0:1, 12:13], 0.7))
    vstep(b, lambda e: e.tensor_copy(sb["ti1"][0:1, 0:1], sb["trow"][0:1, 12:13]))
    vstep(b, lambda e: e.tensor_copy(sb["trow"][0:1, 13:14], sb["ti1"][0:1, 0:1]))
    vstep(b, lambda e: e.tensor_scalar(sb["trow"][0:1, 14:15], sb["trow"][0:1, 13:14],
                                       -0.5, CSH + 0.5, ALU.mult, ALU.add))
    bcast_exact(b, 14, sb["pad"], 5)


def p_wquant(b):
    io, sb, sd = b.io, b.sb, b.sd
    wait(b, "v", b.CG, b.CG.n)
    stgA = sb["bigd"].ap().bitcast(F32)
    stgB = sb["bigc"].ap().bitcast(F32)
    tmp = sb["biga"].ap().bitcast(F32)
    ti = sb["biga"].ap().bitcast(I32)
    out = sb["bigb"]

    jobs = []
    for w_in, w_out, wd, deq, tab in (
            (io["w_gate"], sd["s_wgn"], H, False, sb["swg"]),
            (io["w_up"], sd["s_wun"], H, False, sb["swu"]),
            (io["w_down"], sd["s_wdn"], ID, True, None)):
        for r0 in range(0, w_in.shape[0], 256):
            jobs.append((w_in, w_out, wd, deq, tab, r0))

    cons = {}
    stores = {}
    cv0 = b.CV.n
    for j, (w_in, w_out, wd, deq, tab, r0) in enumerate(jobs):
        stg = stgA if j % 2 == 0 else stgB
        dx = b.DXA if j % 2 == 0 else b.DXB

        def ld(e, j=j, w_in=w_in, wd=wd, r0=r0, stg=stg, dx=dx):
            e.wait_ge(b.CV.h, cons[j - 2] if j >= 2 else cv0)
            e.dma_start(out=stg[:, 0:2 * wd].rearrange("p (c h) -> p c h", c=2),
                        in_=w_in[r0:r0 + 256, 0:wd].rearrange(
                            "(c p) h -> p c h", p=128)).then_inc(dx.h, 16)
        b.y(ld)
        dx.n += 16
        dj = dx.n
        ci = r0 // 128

        def v1(e, dj=dj, stg=stg, wd=wd, dx=dx):
            e.wait_ge(dx.h, dj)
            e.tensor_reduce(sb["agg"][:, 0:2],
                            stg[:, 0:2 * wd].rearrange("p (c h) -> p c h", c=2),
                            axis=AX, op=ALU.max, apply_absolute_value=True) \
                .then_inc(b.CV.h, 1)
        b.v(v1)
        b.CV.n += 1
        if tab is not None:
            vstep(b, lambda e, tab=tab, ci=ci: e.tensor_scalar(
                tab[:, ci:ci + 2], sb["agg"][:, 0:2], 1.0 / QMAX, None, ALU.mult))
        else:
            vstep(b, lambda e: e.tensor_scalar(
                sb["bns"][:, 0:2], sb["agg"][:, 0:2], 1.0 / QMAX, None, ALU.mult))
        vstep(b, lambda e: e.reciprocal(sb["bns"][:, 2:4], sb["agg"][:, 0:2]))
        vstep(b, lambda e: e.tensor_scalar(sb["bns"][:, 2:4], sb["bns"][:, 2:4],
                                           QMAX, None, ALU.mult))
        cvw = b.CV.n

        def saff(e, stg=stg, wd=wd, cvw=cvw):
            e.wait_ge(b.CV.h, cvw)
            for c in range(2):
                ins = e.activation(tmp[:, c * wd:(c + 1) * wd],
                                   stg[:, c * wd:(c + 1) * wd], AF.Identity,
                                   bias=sb["pad"][:, 5:6], scale=sb["bns"][:, 2 + c:3 + c])
            ins.then_inc(b.CS.h, 1)
        b.s(saff)
        b.CS.n += 1
        wait(b, "v", b.CS, b.CS.n)
        vstep(b, lambda e, wd=wd: e.tensor_scalar(
            tmp[:, 0:2 * wd], tmp[:, 0:2 * wd], CSH - 128.0, CSH + 127.49,
            ALU.max, ALU.min))
        vstep(b, lambda e, wd=wd: e.tensor_copy(ti[:, 5632:5632 + 2 * wd],
                                                tmp[:, 0:2 * wd]))
        if j >= 2:
            ss2, sv2 = stores[j - 2]
            wait(b, "v", ss2, sv2)
        ob = (j % 2) * 11264
        if not deq:
            vstep(b, lambda e, ob=ob, wd=wd: e.tensor_scalar(
                out[:, ob: ob + 2 * wd], ti[:, 5632:5632 + 2 * wd],
                -CSH, None, ALU.add))
        else:
            vstep(b, lambda e, ob=ob, wd=wd: e.tensor_scalar(
                out[:, ob: ob + wd], ti[:, 5632:5632 + wd],
                -CSH, sb["bns"][:, 0:1], ALU.add, ALU.mult))
            vstep(b, lambda e, ob=ob, wd=wd: e.tensor_scalar(
                out[:, ob + wd: ob + 2 * wd], ti[:, 5632 + wd:5632 + 2 * wd],
                -CSH, sb["bns"][:, 1:2], ALU.add, ALU.mult))
        cons[j] = b.CV.n
        cvj = b.CV.n
        ssm = b.SSA if j % 2 == 0 else b.SSB

        def st(e, cvj=cvj, w_out=w_out, r0=r0, wd=wd, ob=ob, ssm=ssm):
            e.wait_ge(b.CV.h, cvj)
            e.dma_start(out=w_out[r0:r0 + 256, 0:wd].rearrange(
                            "(c p) h -> p c h", p=128),
                        in_=out[:, ob: ob + 2 * wd].rearrange(
                            "p (c h) -> p c h", c=2)).then_inc(ssm.h, 16)
        b.s(st)
        ssm.n += 16
        stores[j] = (ssm, ssm.n)
    b.wq_store_done = (b.SSA.n, b.SSB.n)


def p_ln_fin(b, amax):
    sb = b.sb
    NCH = b.NCH
    vstep(b, lambda e: e.tensor_scalar(sb["m_tab"][:, 0:NCH], sb["m_tab"][:, 0:NCH],
                                       1.0 / H, None, ALU.mult))
    vstep(b, lambda e: e.tensor_tensor(sb["rr_tab"][:, 0:NCH], sb["m_tab"][:, 0:NCH],
                                       sb["m_tab"][:, 0:NCH], ALU.mult))
    vstep(b, lambda e: e.tensor_scalar(sb["v_tab"][:, 0:NCH], sb["v_tab"][:, 0:NCH],
                                       1.0 / H, LN_EPS, ALU.mult, ALU.add))
    vstep(b, lambda e: e.tensor_tensor(sb["v_tab"][:, 0:NCH], sb["v_tab"][:, 0:NCH],
                                       sb["rr_tab"][:, 0:NCH], ALU.subtract))
    cvn = b.CV.n

    def s1(e):
        e.wait_ge(b.CV.h, cvn)
        e.activation(sb["r_tab"][:, 0:NCH], sb["v_tab"][:, 0:NCH], AF.Sqrt) \
            .then_inc(b.CS.h, 1)
    b.s(s1)
    b.CS.n += 1
    wait(b, "v", b.CS, b.CS.n)
    vstep(b, lambda e: e.reciprocal(sb["r_tab"][:, 0:NCH], sb["r_tab"][:, 0:NCH]))
    if amax:
        vstep(b, lambda e: e.tensor_tensor(sb["xmx"][:, 0:NCH], sb["xmx"][:, 0:NCH],
                                           sb["m_tab"][:, 0:NCH], ALU.subtract))
        vstep(b, lambda e: e.tensor_tensor(sb["xmn"][:, 0:NCH], sb["m_tab"][:, 0:NCH],
                                           sb["xmn"][:, 0:NCH], ALU.subtract))
        vstep(b, lambda e: e.tensor_tensor(sb["xmx"][:, 0:NCH], sb["xmx"][:, 0:NCH],
                                           sb["xmn"][:, 0:NCH], ALU.max))
        vstep(b, lambda e: e.tensor_tensor(sb["xmx"][:, 0:NCH], sb["xmx"][:, 0:NCH],
                                           sb["r_tab"][:, 0:NCH], ALU.mult))
        vstep(b, lambda e: e.tensor_reduce(sb["red1"][:, :], sb["xmx"][:, 0:NCH],
                                           axis=AX, op=ALU.max))


def p_startup(b):
    io, sb, sd = b.io, b.sb, b.sd
    xsrc = io["x_in"]
    NCH = b.NCH
    NB = NCH // 2
    wait(b, "v", b.CG, b.CG.n)

    sq = sb["bigb"].ap().bitcast(F32)[:, 2816:4864]
    ln_cons = {}
    ln_cv0 = b.CV.n

    def emit_ln_batch(t):
        buf = sb["xstg"] if t % 2 == 0 else sb["ostg"]
        dx = (b.GXA, b.GXB)[t % 2]

        def ld(e, t=t, buf=buf, dx=dx):
            e.wait_ge(b.CV.h, ln_cons[t - 2] if t >= 2 else ln_cv0)
            e.dma_start(out=buf[:, 0:2048].rearrange("p (c h) -> p c h", c=2),
                        in_=xsrc[t * 256:(t + 1) * 256, :].rearrange(
                            "(c p) h -> p c h", p=128)).then_inc(dx.h, 16)
        b.g(ld)
        dx.n += 16
        di = dx.n

        def v1(e, di=di, buf=buf, dx=dx, t=t):
            e.wait_ge(dx.h, di)
            e.tensor_reduce(sb["m_tab"][:, 2 * t:2 * t + 2],
                            buf[:, 0:2048].rearrange("p (c h) -> p c h", c=2),
                            axis=AX, op=ALU.add).then_inc(b.CV.h, 1)
        b.v(v1)
        b.CV.n += 1
        vstep(b, lambda e, buf=buf: e.tensor_tensor(sq[:, 0:2048], buf[:, 0:2048],
                                                    buf[:, 0:2048], ALU.mult))
        vstep(b, lambda e, t=t: e.tensor_reduce(
            sb["v_tab"][:, 2 * t:2 * t + 2],
            sq[:, 0:2048].rearrange("p (c h) -> p c h", c=2), axis=AX, op=ALU.add))
        vstep(b, lambda e, t=t, buf=buf: e.tensor_reduce(
            sb["xmx"][:, 2 * t:2 * t + 2],
            buf[:, 0:2048].rearrange("p (c h) -> p c h", c=2), axis=AX, op=ALU.max))
        vstep(b, lambda e, t=t, buf=buf: e.tensor_reduce(
            sb["xmn"][:, 2 * t:2 * t + 2],
            buf[:, 0:2048].rearrange("p (c h) -> p c h", c=2), axis=AX, op=ALU.min))
        ln_cons[t] = b.CV.n

    stgA = sb["bigd"].ap().bitcast(F32)
    stgB = sb["bigc"].ap().bitcast(F32)
    tmp = sb["biga"].ap().bitcast(F32)
    ti = sb["biga"].ap().bitcast(I32)
    out = sb["bigb"]
    jobs = []
    for w_in, w_out, wd, deq, tab in (
            (io["w_gate"], sd["s_wgn"], H, False, sb["swg"]),
            (io["w_up"], sd["s_wun"], H, False, sb["swu"]),
            (io["w_down"], sd["s_wdn"], ID, True, None)):
        for r0 in range(0, w_in.shape[0], 256):
            jobs.append((w_in, w_out, wd, deq, tab, r0))
    wq_cons = {}
    wq_stores = {}
    wq_cv0 = b.CV.n

    def emit_wq_job(j):
        (w_in, w_out, wd, deq, tab, r0) = jobs[j]
        stg = stgA if j % 2 == 0 else stgB
        dx = (b.DXA, b.DXB)[j % 2]

        def ld(e, j=j, w_in=w_in, wd=wd, r0=r0, stg=stg, dx=dx):
            e.wait_ge(b.CV.h, wq_cons[j - 2] if j >= 2 else wq_cv0)
            e.dma_start(out=stg[:, 0:2 * wd].rearrange("p (c h) -> p c h", c=2),
                        in_=w_in[r0:r0 + 256, 0:wd].rearrange(
                            "(c p) h -> p c h", p=128)).then_inc(dx.h, 16)
        b.y(ld)
        dx.n += 16
        dj = dx.n
        ci = r0 // 128

        def v1(e, dj=dj, stg=stg, wd=wd, dx=dx):
            e.wait_ge(dx.h, dj)
            e.tensor_reduce(sb["agg"][:, 0:2],
                            stg[:, 0:2 * wd].rearrange("p (c h) -> p c h", c=2),
                            axis=AX, op=ALU.max, apply_absolute_value=True) \
                .then_inc(b.CV.h, 1)
        b.v(v1)
        b.CV.n += 1
        if tab is not None:
            vstep(b, lambda e, tab=tab, ci=ci: e.tensor_scalar(
                tab[:, ci:ci + 2], sb["agg"][:, 0:2], 1.0 / QMAX, None, ALU.mult))
        else:
            vstep(b, lambda e: e.tensor_scalar(
                sb["bns"][:, 0:2], sb["agg"][:, 0:2], 1.0 / QMAX, None, ALU.mult))
        vstep(b, lambda e: e.reciprocal(sb["bns"][:, 2:4], sb["agg"][:, 0:2]))
        vstep(b, lambda e: e.tensor_scalar(sb["bns"][:, 2:4], sb["bns"][:, 2:4],
                                           QMAX, None, ALU.mult))
        cvw = b.CV.n

        def saff(e, stg=stg, wd=wd, cvw=cvw):
            e.wait_ge(b.CV.h, cvw)
            for c in range(2):
                ins = e.activation(tmp[:, c * wd:(c + 1) * wd],
                                   stg[:, c * wd:(c + 1) * wd], AF.Identity,
                                   bias=sb["pad"][:, 5:6], scale=sb["bns"][:, 2 + c:3 + c])
            ins.then_inc(b.CS.h, 1)
        b.s(saff)
        b.CS.n += 1
        wait(b, "v", b.CS, b.CS.n)
        vstep(b, lambda e, wd=wd: e.tensor_scalar(
            tmp[:, 0:2 * wd], tmp[:, 0:2 * wd], CSH - 128.0, CSH + 127.49,
            ALU.max, ALU.min))
        vstep(b, lambda e, wd=wd: e.tensor_copy(ti[:, 5632:5632 + 2 * wd],
                                                tmp[:, 0:2 * wd]))
        if j >= 2:
            ss2, sv2 = wq_stores[j - 2]
            wait(b, "v", ss2, sv2)
        ob = (j % 2) * 11264
        if not deq:
            vstep(b, lambda e, ob=ob, wd=wd: e.tensor_scalar(
                out[:, ob: ob + 2 * wd], ti[:, 5632:5632 + 2 * wd],
                -CSH, None, ALU.add))
        else:
            vstep(b, lambda e, ob=ob, wd=wd: e.tensor_scalar(
                out[:, ob: ob + wd], ti[:, 5632:5632 + wd],
                -CSH, sb["bns"][:, 0:1], ALU.add, ALU.mult))
            vstep(b, lambda e, ob=ob, wd=wd: e.tensor_scalar(
                out[:, ob + wd: ob + 2 * wd], ti[:, 5632 + wd:5632 + 2 * wd],
                -CSH, sb["bns"][:, 1:2], ALU.add, ALU.mult))
        wq_cons[j] = b.CV.n
        cvj = b.CV.n
        ssm = b.SSA if j % 2 == 0 else b.SSB

        def st(e, cvj=cvj, w_out=w_out, r0=r0, wd=wd, ob=ob, ssm=ssm):
            e.wait_ge(b.CV.h, cvj)
            e.dma_start(out=w_out[r0:r0 + 256, 0:wd].rearrange(
                            "(c p) h -> p c h", p=128),
                        in_=out[:, ob: ob + 2 * wd].rearrange(
                            "p (c h) -> p c h", c=2)).then_inc(ssm.h, 16)
        b.s(st)
        ssm.n += 16
        wq_stores[j] = (ssm, ssm.n)

    NJ = len(jobs)
    for k in range(NB):
        emit_ln_batch(k)
        if k < NJ:
            emit_wq_job(k)

    vstep(b, lambda e: e.tensor_scalar(sb["m_tab"][:, 0:NCH], sb["m_tab"][:, 0:NCH],
                                       1.0 / H, None, ALU.mult))
    vstep(b, lambda e: e.tensor_tensor(sb["rr_tab"][:, 0:NCH], sb["m_tab"][:, 0:NCH],
                                       sb["m_tab"][:, 0:NCH], ALU.mult))
    vstep(b, lambda e: e.tensor_scalar(sb["v_tab"][:, 0:NCH], sb["v_tab"][:, 0:NCH],
                                       1.0 / H, LN_EPS, ALU.mult, ALU.add))
    vstep(b, lambda e: e.tensor_tensor(sb["v_tab"][:, 0:NCH], sb["v_tab"][:, 0:NCH],
                                       sb["rr_tab"][:, 0:NCH], ALU.subtract))
    cvn = b.CV.n

    def s1(e):
        e.wait_ge(b.CV.h, cvn)
        e.activation(sb["r_tab"][:, 0:NCH], sb["v_tab"][:, 0:NCH], AF.Sqrt) \
            .then_inc(b.CS.h, 1)
    b.s(s1)
    b.CS.n += 1
    wait(b, "v", b.CS, b.CS.n)
    vstep(b, lambda e: e.reciprocal(sb["r_tab"][:, 0:NCH], sb["r_tab"][:, 0:NCH]))
    vstep(b, lambda e: e.tensor_tensor(sb["xmx"][:, 0:NCH], sb["xmx"][:, 0:NCH],
                                       sb["m_tab"][:, 0:NCH], ALU.subtract))
    vstep(b, lambda e: e.tensor_tensor(sb["xmn"][:, 0:NCH], sb["m_tab"][:, 0:NCH],
                                       sb["xmn"][:, 0:NCH], ALU.subtract))
    vstep(b, lambda e: e.tensor_tensor(sb["xmx"][:, 0:NCH], sb["xmx"][:, 0:NCH],
                                       sb["xmn"][:, 0:NCH], ALU.max))
    vstep(b, lambda e: e.tensor_tensor(sb["xmx"][:, 0:NCH], sb["xmx"][:, 0:NCH],
                                       sb["r_tab"][:, 0:NCH], ALU.mult))
    vstep(b, lambda e: e.tensor_reduce(sb["red1"][:, :], sb["xmx"][:, 0:NCH],
                                       axis=AX, op=ALU.max))
    colmax_to_trow(b, 4)
    armax(b, 4, 5)

    for j in range(NB, NJ):
        emit_wq_job(j)
    b.wq_store_done = (b.SSA.n, b.SSB.n)


def p_ln_stats(b, xsrc, queue, amax):
    sb = b.sb
    NCH = b.NCH
    NB = NCH // 2
    emitq = b.y if queue == "sync" else b.g
    slots = (b.DXA, b.DXB) if queue == "sync" else (b.GXA, b.GXB)
    sq = sb["bigd"].ap().bitcast(F32)
    cons = {}
    cv0 = b.CV.n
    for t in range(NB):
        buf = sb["xstg"] if t % 2 == 0 else sb["ostg"]
        dx = slots[t % 2]

        def ld(e, t=t, buf=buf, dx=dx):
            e.wait_ge(b.CV.h, cons[t - 2] if t >= 2 else cv0)
            if t < 2:
                e.wait_ge(b.SSA.h, ssa0)
                e.wait_ge(b.SSB.h, ssb0)
            e.dma_start(out=buf[:, 0:2048].rearrange("p (c h) -> p c h", c=2),
                        in_=xsrc[t * 256:(t + 1) * 256, :].rearrange(
                            "(c p) h -> p c h", p=128)).then_inc(dx.h, 16)
        emitq(ld)
        dx.n += 16
        di = dx.n

        def v1(e, di=di, buf=buf, dx=dx, t=t):
            e.wait_ge(dx.h, di)
            e.tensor_reduce(sb["m_tab"][:, 2 * t:2 * t + 2],
                            buf[:, 0:2048].rearrange("p (c h) -> p c h", c=2),
                            axis=AX, op=ALU.add).then_inc(b.CV.h, 1)
        b.v(v1)
        b.CV.n += 1
        vstep(b, lambda e, buf=buf: e.tensor_tensor(sq[:, 0:2048], buf[:, 0:2048],
                                                    buf[:, 0:2048], ALU.mult))
        vstep(b, lambda e, t=t: e.tensor_reduce(
            sb["v_tab"][:, 2 * t:2 * t + 2],
            sq[:, 0:2048].rearrange("p (c h) -> p c h", c=2), axis=AX, op=ALU.add))
        if amax:
            vstep(b, lambda e, t=t, buf=buf: e.tensor_reduce(
                sb["xmx"][:, 2 * t:2 * t + 2],
                buf[:, 0:2048].rearrange("p (c h) -> p c h", c=2), axis=AX, op=ALU.max))
            vstep(b, lambda e, t=t, buf=buf: e.tensor_reduce(
                sb["xmn"][:, 2 * t:2 * t + 2],
                buf[:, 0:2048].rearrange("p (c h) -> p c h", c=2), axis=AX, op=ALU.min))
        cons[t] = b.CV.n

    vstep(b, lambda e: e.tensor_scalar(sb["m_tab"][:, 0:NCH], sb["m_tab"][:, 0:NCH],
                                       1.0 / H, None, ALU.mult))
    vstep(b, lambda e: e.tensor_tensor(sb["rr_tab"][:, 0:NCH], sb["m_tab"][:, 0:NCH],
                                       sb["m_tab"][:, 0:NCH], ALU.mult))
    vstep(b, lambda e: e.tensor_scalar(sb["v_tab"][:, 0:NCH], sb["v_tab"][:, 0:NCH],
                                       1.0 / H, LN_EPS, ALU.mult, ALU.add))
    vstep(b, lambda e: e.tensor_tensor(sb["v_tab"][:, 0:NCH], sb["v_tab"][:, 0:NCH],
                                       sb["rr_tab"][:, 0:NCH], ALU.subtract))
    cvn = b.CV.n

    def s1(e):
        e.wait_ge(b.CV.h, cvn)
        e.activation(sb["r_tab"][:, 0:NCH], sb["v_tab"][:, 0:NCH], AF.Sqrt) \
            .then_inc(b.CS.h, 1)
    b.s(s1)
    b.CS.n += 1
    csn = b.CS.n
    wait(b, "v", b.CS, csn)
    vstep(b, lambda e: e.reciprocal(sb["r_tab"][:, 0:NCH], sb["r_tab"][:, 0:NCH]))

    if amax:
        vstep(b, lambda e: e.tensor_tensor(sb["xmx"][:, 0:NCH], sb["xmx"][:, 0:NCH],
                                           sb["m_tab"][:, 0:NCH], ALU.subtract))
        vstep(b, lambda e: e.tensor_tensor(sb["xmn"][:, 0:NCH], sb["m_tab"][:, 0:NCH],
                                           sb["xmn"][:, 0:NCH], ALU.subtract))
        vstep(b, lambda e: e.tensor_tensor(sb["xmx"][:, 0:NCH], sb["xmx"][:, 0:NCH],
                                           sb["xmn"][:, 0:NCH], ALU.max))
        vstep(b, lambda e: e.tensor_tensor(sb["xmx"][:, 0:NCH], sb["xmx"][:, 0:NCH],
                                           sb["r_tab"][:, 0:NCH], ALU.mult))
        vstep(b, lambda e: e.tensor_reduce(sb["red1"][:, :], sb["xmx"][:, 0:NCH],
                                           axis=AX, op=ALU.max))


def p_scales_x(b):
    sb = b.sb
    NCH = b.NCH
    wait(b, "v", b.CG, b.CG.n)
    wait(b, "v", b.CS, b.CS.n)
    vstep(b, lambda e: e.tensor_tensor(sb["trow"][0:1, 6:7], sb["trow"][0:1, 5:6],
                                       sb["trow"][0:1, 2:3], ALU.mult))
    vstep(b, lambda e: e.tensor_scalar(sb["trow"][0:1, 6:7], sb["trow"][0:1, 6:7],
                                       1.0 / QMAX, None, ALU.mult))
    vstep(b, lambda e: e.reciprocal(sb["trow"][0:1, 7:8], sb["trow"][0:1, 6:7]))
    bcast_exact(b, 7, sb["pad"], 0)
    bcast_exact(b, 6, sb["pad"], 1)
    wait(b, "v", b.CG, b.CG.n)
    vstep(b, lambda e: e.tensor_scalar(sb["rr_tab"][:, 0:NCH], sb["r_tab"][:, 0:NCH],
                                       sb["pad"][:, 0:1], None, ALU.mult))
    vstep(b, lambda e: e.tensor_tensor(sb["bb_tab"][:, 0:NCH], sb["m_tab"][:, 0:NCH],
                                       sb["rr_tab"][:, 0:NCH], ALU.mult))
    vstep(b, lambda e: e.tensor_scalar(sb["bb_tab"][:, 0:NCH], sb["bb_tab"][:, 0:NCH],
                                       -1.0, sb["pad"][:, 5:6], ALU.mult, ALU.add))
    vstep(b, lambda e: e.tensor_scalar(sb["sgx"][:, 0:IC], sb["swg"][:, 0:IC],
                                       sb["pad"][:, 1:2], None, ALU.mult))
    vstep(b, lambda e: e.tensor_scalar(sb["sux"][:, 0:IC], sb["swu"][:, 0:IC],
                                       sb["pad"][:, 1:2], None, ALU.mult))


def p_scales_h(b):
    sb = b.sb
    wait(b, "v", b.CG, b.CG.n)
    vstep(b, lambda e: e.tensor_tensor(sb["trow"][0:1, 8:9], sb["trow"][0:1, 5:6],
                                       sb["trow"][0:1, 3:4], ALU.mult))
    vstep(b, lambda e: e.tensor_scalar(sb["trow"][0:1, 8:9], sb["trow"][0:1, 8:9],
                                       1.0 / QMAX, None, ALU.mult))
    vstep(b, lambda e: e.reciprocal(sb["trow"][0:1, 9:10], sb["trow"][0:1, 8:9]))
    vstep(b, lambda e: e.tensor_scalar(sb["trow"][0:1, 10:11], sb["trow"][0:1, 8:9],
                                       0.5, None, ALU.mult))
    bcast_exact(b, 9, sb["pad"], 2)
    bcast_exact(b, 10, sb["pad"], 3)


def p_xq(b, xsrc, queue):
    sb = b.sb
    NCH = b.NCH
    NB = NCH // 2
    emitq = b.y if queue == "sync" else b.g
    slots = (b.DXA, b.DXB) if queue == "sync" else (b.GXA, b.GXB)
    tmp = sb["biga"].ap().bitcast(F32)[:, 2048:4096]
    ti = sb["biga"].ap().bitcast(I32)[:, 4096:6144]
    obf = sb["biga"]
    cons = {}
    stores = {}
    cv0 = b.CV.n
    ssa0, ssb0 = b.SSA.n, b.SSB.n
    for t in range(NB):
        buf = sb["xstg"] if t % 2 == 0 else sb["ostg"]
        dx = slots[t % 2]

        def ld(e, t=t, buf=buf, dx=dx):
            e.wait_ge(b.CV.h, cons[t - 2] if t >= 2 else cv0)
            if t < 2:
                e.wait_ge(b.SSA.h, ssa0)
                e.wait_ge(b.SSB.h, ssb0)
            e.dma_start(out=buf[:, 0:2048].rearrange("p (c h) -> p c h", c=2),
                        in_=xsrc[t * 256:(t + 1) * 256, :].rearrange(
                            "(c p) h -> p c h", p=128)).then_inc(dx.h, 16)
        emitq(ld)
        dx.n += 16
        di = dx.n
        cvw = b.CV.n

        def s1a(e, di=di, buf=buf, t=t, dx=dx, cvw=cvw):
            e.wait_ge(dx.h, di)
            e.wait_ge(b.CV.h, cvw)
            for c in range(2):
                i = 2 * t + c
                ins = e.activation(tmp[:, c * 1024:(c + 1) * 1024],
                                   buf[:, c * 1024:(c + 1) * 1024],
                                   AF.Identity, bias=sb["bb_tab"][:, i:i + 1],
                                   scale=sb["rr_tab"][:, i:i + 1])
            ins.then_inc(b.CS.h, 1)
        b.s(s1a)
        b.CS.n += 1

        def v1(e, t=t, csw=b.CS.n):
            e.wait_ge(b.CS.h, csw)
            if t >= 2:
                ss2, sv2 = stores[t - 2]
                e.wait_ge(ss2.h, sv2)
            e.tensor_scalar(tmp[:, 0:2048], tmp[:, 0:2048],
                            CSH - 128.0, CSH + 127.49, ALU.max, ALU.min) \
                .then_inc(b.CV.h, 1)
        b.v(v1)
        b.CV.n += 1
        vstep(b, lambda e: e.tensor_copy(ti[:, 0:2048], tmp[:, 0:2048]))
        ob = t % 2
        vstep(b, lambda e, ob=ob: e.tensor_scalar(obf[:, ob * 2048:(ob + 1) * 2048],
                                                  ti[:, 0:2048], -CSH, None, ALU.add))
        cons[t] = b.CV.n
        cvi = b.CV.n
        ssm = b.SSA if ob == 0 else b.SSB

        def st(e, cvi=cvi, t=t, ob=ob, ssm=ssm, ssa0=ssa0, ssb0=ssb0):
            e.wait_ge(b.CV.h, cvi)
            if t < 2:
                e.wait_ge(ssm.h, ssa0 if t % 2 == 0 else ssb0)
            e.dma_start(out=b.sd["s_xq"][t * 256:(t + 1) * 256, :].rearrange(
                            "(c p) h -> p c h", p=128),
                        in_=obf[:, ob * 2048:(ob + 1) * 2048].rearrange(
                            "p (c h) -> p c h", c=2)).then_inc(ssm.h, 16)
        b.s(st)
        ssm.n += 16
        stores[t] = (ssm, ssm.n)


def p_gateup(b):
    sb, sd = b.sb, b.sd
    NQ = b.NQ

    wq_ssa, wq_ssb = getattr(b, "wq_store_done", (b.SSA.n, b.SSB.n))
    ssa_n, ssb_n = b.SSA.n, b.SSB.n
    for hc in range(HC):
        def ld(e, hc=hc, wq_ssa=wq_ssa, wq_ssb=wq_ssb):
            if hc == 0:
                e.wait_ge(b.SSA.h, wq_ssa)
                e.wait_ge(b.SSB.h, wq_ssb)
            e.dma_start(out=sb["biga"][:, hc * ID:(hc + 1) * ID],
                        in_=sd["s_wgn"][0:ID, hc * 128:(hc + 1) * 128],
                        transpose=True).then_inc(b.D.h, 16)
        b.y(ld)
        b.D.n += 16
    for hc in range(HC):
        def ld2(e, hc=hc):
            e.dma_start(out=sb["bigb"][:, hc * ID:(hc + 1) * ID],
                        in_=sd["s_wun"][0:ID, hc * 128:(hc + 1) * 128],
                        transpose=True).then_inc(b.D.h, 16)
        b.y(ld2)
        b.D.n += 16
    d_w = b.D.n

    vstep(b, lambda e: e.memset(sb["dmax"][:, :], 0.0))
    cv_init = b.CV.n

    ct_q = {}
    for q in range(NQ):
        qb = sb["bigd"] if q % 2 == 0 else sb["bigc"]
        dq = b.DQ0 if q % 2 == 0 else b.DQ1
        for hc in range(HC):
            def xl(e, hc=hc, q=q, qb=qb, dq=dq, ssa_n=ssa_n, ssb_n=ssb_n):
                if hc == 0 and q < 2:
                    e.wait_ge(b.SSA.h, ssa_n)
                    e.wait_ge(b.SSB.h, ssb_n)
                if hc == 0 and q >= 2:
                    e.wait_ge(b.CT.h, ct_q[q - 2])
                e.dma_start(out=qb[:, hc * 1024:(hc + 1) * 1024],
                            in_=sd["s_xq"][q * 1024:(q + 1) * 1024,
                                           hc * 128:(hc + 1) * 128],
                            transpose=True).then_inc(dq.h, 16)
            b.y(xl)
            dq.n += 16
        d_x = dq.n

        evacs = {}
        spills = {}
        cv_qprev = cv_init if q == 0 else gu_prev_last
        for g in range(2):
            ctj = {}
            cv_g0 = b.CV.n

            def epred(j):
                return evacs[j] if j in evacs else None
            for ic in range(IC):
                j = g * IC + ic
                pg = b.psf[0 + 2 * (j % 3)]
                pu = b.psf[1 + 2 * (j % 3)]
                evacs[j] = cv_g0 + 3 * (ic + 1) if False else None

                def tm(e, j=j, g=g, ic=ic, pg=pg, pu=pu, d_x=d_x, cv_qprev=cv_qprev,
                       qb=qb, dq=dq, ep=evacs.get(j - 3)):
                    if j == 0:
                        e.wait_ge(b.D.h, d_w)
                        e.wait_ge(dq.h, d_x)
                        e.wait_ge(b.CV.h, cv_qprev)
                    if j >= 3:
                        e.wait_ge(b.CV.h, ep)
                    col = g * 512
                    for hc in range(HC):
                        e.matmul(pg.ap()[:, 0:512],
                                 sb["biga"][:, hc * ID + ic * 128: hc * ID + (ic + 1) * 128],
                                 qb[:, hc * 1024 + col: hc * 1024 + col + 512],
                                 start=(hc == 0), stop=(hc == HC - 1))
                    for hc in range(HC):
                        ins = e.matmul(pu.ap()[:, 0:512],
                                       sb["bigb"][:, hc * ID + ic * 128: hc * ID + (ic + 1) * 128],
                                       qb[:, hc * 1024 + col: hc * 1024 + col + 512],
                                       start=(hc == 0), stop=(hc == HC - 1))
                    ins.then_inc(b.CT.h, 1)
                b.t(tm)
                b.CT.n += 1
                ctj[ic] = b.CT.n
                evacs[j] = cv_g0 + 3 * (ic + 1)

            def sg_sec(e, g=g, ctj=dict(ctj), cv_g0=cv_g0):
                for ic in range(IC):
                    j = g * IC + ic
                    e.wait_ge(b.CT.h, ctj[ic])
                    if ic >= 3:
                        e.wait_ge(b.CV.h, cv_g0 + 3 * (ic - 2))
                    else:
                        e.wait_ge(b.CV.h, cv_g0)
                    pg = b.psf[0 + 2 * (j % 3)]
                    e.activation(sb["xstg"][:, (j % 3) * 512:(j % 3 + 1) * 512],
                                 pg.ap()[:, 0:512], AF.Silu, bias=0.0,
                                 scale=sb["sgx"][:, ic:ic + 1]).then_inc(b.CS.h, 1)
            b.s(sg_sec)
            cs_g0 = b.CS.n
            b.CS.n += IC

            for ic in range(IC):
                j = g * IC + ic
                pu = b.psf[1 + 2 * (j % 3)]
                csj = cs_g0 + ic + 1

                def vv(e, csj=csj, ic=ic, j=j, pu=pu, q=q):
                    e.wait_ge(b.CS.h, csj)
                    if j >= 2:
                        ss2, sv2 = spills[j - 2]
                        e.wait_ge(ss2.h, sv2)
                    elif q > 0:
                        ss2, sv2 = gu_prev_spills[j % 2]
                        e.wait_ge(ss2.h, sv2)
                    e.scalar_tensor_tensor(
                        sb["ostg"].ap().bitcast(BF16)[:, (j % 2) * 512:(j % 2 + 1) * 512],
                        pu.ap()[:, 0:512], sb["sux"][:, ic:ic + 1],
                        sb["xstg"][:, (j % 3) * 512:(j % 3 + 1) * 512],
                        ALU.mult, ALU.mult).then_inc(b.CV.h, 1)
                b.v(vv)
                b.CV.n += 1
                vstep(b, lambda e, j=j: e.tensor_reduce(
                    sb["red1"][:, :],
                    sb["ostg"].ap().bitcast(BF16)[:, (j % 2) * 512:(j % 2 + 1) * 512],
                    axis=AX, op=ALU.max, apply_absolute_value=True))
                vstep(b, lambda e: e.tensor_tensor(sb["dmax"][:, :], sb["dmax"][:, :],
                                                   sb["red1"][:, :], ALU.max))
                assert b.CV.n == evacs[j], (b.CV.n, evacs[j])
                cvj = b.CV.n
                ssm = b.GSA if j % 2 == 0 else b.GSB

                def sp(e, cvj=cvj, ic=ic, q=q, g=g, j=j, ssm=ssm):
                    e.wait_ge(b.CV.h, cvj)
                    e.dma_start(out=sd["s_h"][ic * 128:(ic + 1) * 128,
                                              q * 1024 + g * 512: q * 1024 + g * 512 + 512],
                                in_=sb["ostg"].ap().bitcast(BF16)[:, (j % 2) * 512:(j % 2 + 1) * 512]) \
                        .then_inc(ssm.h, 16)
                b.g(sp)
                ssm.n += 16
                spills[j] = (ssm, ssm.n)
        ct_q[q] = b.CT.n
        gu_prev_last = b.CV.n
        gu_prev_spills = (spills[2 * IC - 2], spills[2 * IC - 1])

    vstep(b, lambda e: e.tensor_copy(sb["red1"][:, :], sb["dmax"][:, :]))


def p_down(b, xold, xdst, final, do_stats=False):
    sb, sd = b.sb, b.sd
    R = b.R
    NG = R // 512
    NCH = b.NCH
    tmp = sb["biga"].ap().bitcast(F32)
    ti = sb["biga"].ap().bitcast(I32)
    ostg = sb["ostg"]
    ob_bf = ostg.ap().bitcast(BF16)

    ssa_n, ssb_n = b.GSA.n, b.GSB.n
    for icc in range(IC):
        def ld(e, icc=icc, ssa_n=ssa_n, ssb_n=ssb_n):
            if icc == 0:
                e.wait_ge(b.GSA.h, ssa_n)
                e.wait_ge(b.GSB.h, ssb_n)
            e.dma_start(out=sb["bigc"][:, icc * H:(icc + 1) * H],
                        in_=sd["s_wdn"][0:H, icc * 128:(icc + 1) * 128],
                        transpose=True).then_inc(b.D.h, 16)
        b.y(ld)
        b.D.n += 16
    d_w = b.D.n

    dg0 = b.DG.n
    xevac = {}

    xdn = {}

    def xload(i, wait_cv):
        gx = b.GXA if i % 2 == 0 else b.GXB

        def xle(e, i=i, wait_cv=wait_cv, gx=gx):
            if wait_cv:
                e.wait_ge(b.CV.h, wait_cv)
            e.dma_start(out=sb["xstg"][:, (i % 2) * H:(i % 2 + 1) * H],
                        in_=xold[i * 128:(i + 1) * 128, :]).then_inc(gx.h, 16)
        b.g(xle)
        gx.n += 16
        xdn[i] = (gx, gx.n)

    xload(0, 0)
    xload(1, 0)

    hq = {}
    ct_g = {}
    ctA = {}
    ctB = {}
    dn_store = {}
    stats_cv = {}
    for g in range(NG):
        def hlA(e, g=g):
            if g >= 1:
                e.wait_ge(b.CT.h, ctA[g - 1])
            for icc in range(11):
                e.dma_start(out=sb["bigd"][:, icc * 512:(icc + 1) * 512],
                            in_=sd["s_h"][icc * 128:(icc + 1) * 128,
                                          g * 512:(g + 1) * 512]).then_inc(b.DH.h, 16)
        b.y(hlA)
        b.DH.n += 16 * 11
        d_ha = b.DH.n

        def hlB(e, g=g):
            if g >= 1:
                e.wait_ge(b.CT.h, ctB[g - 1])
            for icc in range(11, IC):
                e.dma_start(out=sb["bigd"][:, icc * 512:(icc + 1) * 512],
                            in_=sd["s_h"][icc * 128:(icc + 1) * 128,
                                          g * 512:(g + 1) * 512]).then_inc(b.DHB.h, 16)
        b.y(hlB)
        b.DHB.n += 16 * 11
        d_hb = b.DHB.n

        qcv = {}
        for hf, (dsm, dvl) in ((0, (b.DH, d_ha)), (1, (b.DHB, d_hb))):
            w = 11 * 512
            base = hf * w
            cvw = b.CV.n

            def sq(e, dsm=dsm, dvl=dvl, base=base, w=w, cvw=cvw):
                e.wait_ge(dsm.h, dvl)
                e.wait_ge(b.CV.h, cvw)
                e.activation(tmp[:, 0:w], sb["bigd"][:, base:base + w], AF.Identity,
                             bias=sb["pad"][:, 5:6], scale=sb["pad"][:, 2:3]) \
                    .then_inc(b.CS.h, 1)
            b.s(sq)
            b.CS.n += 1
            wait(b, "v", b.CS, b.CS.n)
            vstep(b, lambda e, w=w: e.tensor_scalar(
                tmp[:, 0:w], tmp[:, 0:w], CSH - 128.0, CSH + 127.49, ALU.max, ALU.min))
            vstep(b, lambda e, w=w: e.tensor_copy(ti[:, 5632:5632 + w], tmp[:, 0:w]))
            vstep(b, lambda e, base=base, w=w: e.tensor_scalar(
                sb["bigd"][:, base:base + w], ti[:, 5632:5632 + w], -CSH, None, ALU.add))
            qcv[hf] = b.CV.n
        hq[g] = b.CV.n

        for pair in range(2):
            for hf in range(2):
                for ii in range(2):
                    sub = 2 * pair + ii
                    nch = g * 4 + sub
                    par = sub % 2
                    psA = b.psf[0 + 2 * par]
                    psB = b.psf[1 + 2 * par]

                    def tdm(e, hf=hf, sub=sub, nch=nch, psA=psA, psB=psB, g=g,
                            qv=qcv[hf], d_w=d_w, pair=pair, ii=ii):
                        if pair == 0 and ii == 0:
                            e.wait_ge(b.CV.h, qv)
                        if hf == 0:
                            if g == 0 and sub == 0:
                                e.wait_ge(b.D.h, d_w)
                            if nch >= 2:
                                e.wait_ge(b.CV.h, xevac[nch - 2])
                        r0, r1 = (0, 11) if hf == 0 else (11, IC)
                        for icc in range(r0, r1):
                            e.matmul(psA.ap()[:, 0:512],
                                     sb["bigd"][:, icc * 512 + sub * 128: icc * 512 + sub * 128 + 128],
                                     sb["bigc"][:, icc * H: icc * H + 512],
                                     start=(icc == 0), stop=(icc == IC - 1))
                        for icc in range(r0, r1):
                            ins = e.matmul(psB.ap()[:, 0:512],
                                           sb["bigd"][:, icc * 512 + sub * 128: icc * 512 + sub * 128 + 128],
                                           sb["bigc"][:, icc * H + 512: icc * H + 1024],
                                           start=(icc == 0), stop=(icc == IC - 1))
                        ins.then_inc(b.CT.h, 1)
                    b.t(tdm)
                    b.CT.n += 1
                    if hf == 0 and pair == 1:
                        ctA[g] = b.CT.n
                    if hf == 1:
                        if pair == 1:
                            ctB[g] = b.CT.n
                        ctn = b.CT.n

                        def vev(e, ctn=ctn, nch=nch, psA=psA, psB=psB):
                            e.wait_ge(b.CT.h, ctn)
                            gx2, gv2 = xdn[nch]
                            e.wait_ge(gx2.h, gv2)
                            if nch >= 2:
                                ss2, sv2 = dn_store[nch - 2]
                                e.wait_ge(ss2.h, sv2)
                                if do_stats:
                                    e.wait_ge(b.CV.h, stats_cv[nch - 2])
                            xb = (nch % 2) * H
                            ob = (nch % 2)
                            if final:
                                dA = ostg[:, ob * H: ob * H + 512]
                                dB = ostg[:, ob * H + 512: ob * H + 1024]
                            else:
                                dA = ob_bf[:, ob * 2 * H: ob * 2 * H + 512]
                                dB = ob_bf[:, ob * 2 * H + 512: ob * 2 * H + 1024]
                            e.scalar_tensor_tensor(dA, psA.ap()[:, 0:512], sb["pad"][:, 3:4],
                                                   sb["xstg"][:, xb: xb + 512], ALU.mult, ALU.add)
                            e.scalar_tensor_tensor(dB, psB.ap()[:, 0:512], sb["pad"][:, 3:4],
                                                   sb["xstg"][:, xb + 512: xb + 1024],
                                                   ALU.mult, ALU.add).then_inc(b.CV.h, 1)
                        b.v(vev)
                        b.CV.n += 1
                        xevac[nch] = b.CV.n
                        if do_stats:
                            obq = (nch % 2)
                            ch = ob_bf[:, obq * 2 * H: obq * 2 * H + 1024]
                            vstep(b, lambda e, ch=ch, nch=nch: e.tensor_reduce(
                                sb["m_tab"][:, nch:nch + 1], ch, axis=AX, op=ALU.add))
                            vstep(b, lambda e, ch=ch: e.tensor_tensor(
                                sb["sqt"][:, :], ch, ch, ALU.mult))
                            vstep(b, lambda e, nch=nch: e.tensor_reduce(
                                sb["v_tab"][:, nch:nch + 1], sb["sqt"][:, :],
                                axis=AX, op=ALU.add))
                            stats_cv[nch] = b.CV.n
                        cvn = b.CV.n
                        ssm = b.SSA if nch % 2 == 0 else b.SSB

                        def stv(e, cvn=cvn, nch=nch, ssm=ssm):
                            e.wait_ge(b.CV.h, cvn)
                            ob = (nch % 2)
                            if final:
                                e.dma_start(out=b.y_out[nch * 128:(nch + 1) * 128, :],
                                            in_=ostg[:, ob * H:(ob + 1) * H]).then_inc(ssm.h, 16)
                            else:
                                e.dma_start(out=xdst[nch * 128:(nch + 1) * 128, :],
                                            in_=ob_bf[:, ob * 2 * H: ob * 2 * H + 1024]) \
                                    .then_inc(ssm.h, 16)
                        b.s(stv)
                        ssm.n += 16
                        dn_store[nch] = (ssm, ssm.n)
                        if nch + 2 < NCH:
                            xload(nch + 2, xevac[nch])
        ct_g[g] = b.CT.n


def p_ln_apply(b, xsrc, dst):
    sb = b.sb
    NCH = b.NCH
    cons = {}
    stores = {}
    cv0 = b.CV.n
    ssa0, ssb0 = b.SSA.n, b.SSB.n
    ob_bf = sb["ostg"].ap().bitcast(BF16)
    for i in range(NCH):
        gx = b.GXA if i % 2 == 0 else b.GXB

        def ld(e, i=i, gx=gx):
            e.wait_ge(b.CV.h, cons[i - 2] if i >= 2 else cv0)
            e.dma_start(out=sb["xstg"][:, (i % 2) * H:(i % 2 + 1) * H],
                        in_=xsrc[i * 128:(i + 1) * 128, :]).then_inc(gx.h, 16)
        b.g(ld)
        gx.n += 16
        dgi = gx.n

        def v1(e, dgi=dgi, i=i, gx=gx):
            e.wait_ge(gx.h, dgi)
            if i >= 2:
                ss2, sv2 = stores[i - 2]
                e.wait_ge(ss2.h, sv2)
            else:
                e.wait_ge(b.SSA.h, ssa0)
                e.wait_ge(b.SSB.h, ssb0)
            e.tensor_scalar(ob_bf[:, (i % 2) * 2 * H: (i % 2) * 2 * H + H],
                            sb["xstg"][:, (i % 2) * H:(i % 2 + 1) * H],
                            sb["m_tab"][:, i:i + 1], sb["r_tab"][:, i:i + 1],
                            ALU.subtract, ALU.mult).then_inc(b.CV.h, 1)
        b.v(v1)
        b.CV.n += 1
        cons[i] = b.CV.n
        cvi = b.CV.n
        ssm = b.SSA if i % 2 == 0 else b.SSB

        def st(e, cvi=cvi, i=i, ssm=ssm):
            e.wait_ge(b.CV.h, cvi)
            e.dma_start(out=dst[i * 128:(i + 1) * 128, :],
                        in_=ob_bf[:, (i % 2) * 2 * H: (i % 2) * 2 * H + H]) \
                .then_inc(ssm.h, 16)
        b.s(st)
        ssm.n += 16
        stores[i] = (ssm, ssm.n)



VE = 0
KP = 8 * 1040
KQ = KP + 2 * 1024
KF = KQ + 2 * 1024


def p_sweep1(b):
    sb, sd = b.sb, b.sd
    NQ = b.NQ

    dgn = b.dg_wqkv
    for hc in range(HC):
        def ld(e, hc=hc, dgn=dgn):
            if hc == 0:
                e.wait_ge(b.DW.h, dgn)
            e.dma_start(out=sb["biga"][:, hc * 1024:(hc + 1) * 1024],
                        in_=sd["s_wkn"][0:H, hc * 128:(hc + 1) * 128],
                        transpose=True).then_inc(b.D.h, 16)
        b.y(ld)
        b.D.n += 16
    for hc in range(HC):
        def ld2(e, hc=hc):
            e.dma_start(out=sb["biga"][:, 8192 + hc * 1024: 9216 + hc * 1024],
                        in_=sd["s_wvn"][0:H, hc * 128:(hc + 1) * 128],
                        transpose=True).then_inc(b.D.h, 16)
        b.y(ld2)
        b.D.n += 16
    d_w = b.D.n

    vstep(b, lambda e: e.memset(sb["ctx_sb"][:, :], 0.0))
    vstep(b, lambda e: e.memset(sb["dmax"][:, :], 0.0))
    vstep(b, lambda e: e.memset(sb["qrow"][:, :], 0.0))
    b.ctx_evac_cv = b.CV.n
    b.ps45_cs = b.CS.n
    b.ps45_cv = b.CV.n
    cv_init = b.CV.n
    ct_q = {}

    for q in range(NQ):
        qb = sb["bigd"] if q % 2 == 0 else sb["bigc"]
        dq = b.DQ0 if q % 2 == 0 else b.DQ1
        for hc in range(HC):
            def xl(e, hc=hc, q=q, qb=qb, dq=dq):
                if hc == 0 and q >= 2:
                    e.wait_ge(b.CT.h, ct_q[q - 2])
                e.dma_start(out=qb[:, hc * 1024:(hc + 1) * 1024],
                            in_=sd["s_ln2"][q * 1024:(q + 1) * 1024,
                                            hc * 128:(hc + 1) * 128],
                            transpose=True).then_inc(dq.h, 16)
            b.y(xl)
            dq.n += 16
        d_x = dq.n

        vproj = {}
        for i in range(8):
            ph = b.psf[0 + (i % 2)]
            pl = b.psf[2 + (i % 2)]

            def tv(e, i=i, ph=ph, pl=pl, d_x=d_x, q=q, qb=qb, dq=dq):
                if i == 0:
                    e.wait_ge(b.D.h, d_w)
                    e.wait_ge(dq.h, d_x)
                    e.wait_ge(b.CV.h, cv_init if q == 0 else b.s1_vext_free[q])
                if i >= 2:
                    e.wait_ge(b.CV.h, vproj[i - 2])
                for half, ps in ((0, ph), (1, pl)):
                    for hc in range(HC):
                        e.matmul(ps.ap()[:, 0:512],
                                 qb[:, hc * 1024 + i * 128: hc * 1024 + (i + 1) * 128],
                                 sb["biga"][:, 8192 + hc * 1024 + half * 512:
                                            8192 + hc * 1024 + half * 512 + 512],
                                 start=(hc == 0), stop=False)
                    ins = e.matmul(ps.ap()[:, 0:512], sb["onerb"][0:1, 0:128],
                                   sb["brow"][0:1, H + half * 512: H + half * 512 + 512],
                                   start=False, stop=True)
                ins.then_inc(b.CT.h, 1)
            b.t(tv)
            b.CT.n += 1
            ctn = b.CT.n

            def vv(e, ctn=ctn, i=i, ph=ph, pl=pl):
                e.wait_ge(b.CT.h, ctn)
                for half, ps in ((0, ph), (1, pl)):
                    dst = sb["bigb"][:, VE + i * 1040 + half * 520:
                                     VE + i * 1040 + half * 520 + 520]
                    d3 = dst.rearrange("p (h e) -> p h e", e=65)[:, :, 0:64]
                    s3 = ps.ap()[:, 0:512].rearrange("p (h e) -> p h e", e=64)
                    ins = e.tensor_copy(d3, s3)
                ins.then_inc(b.CV.h, 1)
            b.v(vv)
            b.CV.n += 1
            vproj[i] = b.CV.n

        def vo(e):
            vex = sb["bigb"][:, VE: VE + 8 * 1040].rearrange(
                "p (i h e) -> p i h e", i=8, e=65)
            e.memset(vex[:, :, :, 64:65], 1.0).then_inc(b.CV.h, 1)
        b.v(vo)
        b.CV.n += 1
        cv_vext = b.CV.n

        kdone = {}
        for pr in range(NPAIR):
            kb = pr % 2

            def tk(e, pr=pr, kb=kb, cv_vext=cv_vext, pcs=b.ps45_cs, pcv=b.ps45_cv,
                   qb=qb):
                if pr == 0:
                    e.wait_ge(b.CV.h, cv_vext)
                e.wait_ge(b.CS.h, pcs)
                e.wait_ge(b.CV.h, pcv)
                for g in range(2):
                    ps = b.psf[4 + g]
                    for hc in range(HC):
                        ins = e.matmul(ps.ap()[:, 0:512],
                                       sb["biga"][:, hc * 1024 + pr * 128: hc * 1024 + (pr + 1) * 128],
                                       qb[:, hc * 1024 + g * 512: hc * 1024 + g * 512 + 512],
                                       start=(hc == 0), stop=(hc == HC - 1))
                ins.then_inc(b.CT.h, 1)
            b.t(tk)
            b.CT.n += 1
            ctk = b.CT.n

            def sk(e, ctk=ctk, pr=pr, kb=kb):
                e.wait_ge(b.CT.h, ctk)
                for g in range(2):
                    ps = b.psf[4 + g]
                    e.tensor_scalar(sb["bigb"][:, KP + kb * 1024 + g * 512:
                                               KP + kb * 1024 + g * 512 + 512],
                                    ps.ap()[:, 0:512],
                                    sb["b_tab"][:, 8 + pr: 9 + pr], None, ALU.add) \
                        .then_inc(b.CV.h, 1)
                    e.wait_ge(b.CV.h, b.CV.n + 2 * g + 1)
                    e.tensor_tensor(sb["bigb"][:, KQ + kb * 1024 + g * 512:
                                               KQ + kb * 1024 + g * 512 + 512],
                                    sb["bigb"][:, KP + kb * 1024 + g * 512:
                                               KP + kb * 1024 + g * 512 + 512],
                                    sb["bigb"][:, KP + kb * 1024 + g * 512:
                                               KP + kb * 1024 + g * 512 + 512],
                                    ALU.mult).then_inc(b.CV.h, 1)
            b.v(sk)
            b.CV.n += 4
            kdone[pr] = b.CV.n

            for hh in range(2):
                head = 2 * pr + hh
                p_head_sweep1(b, head, kb, hh * 64, head % 2, kdone[pr])

        for colh, w in ((0, 512), (1, 512), (2, 16)):
            cvn = b.CV.n

            def tvs(e, colh=colh, w=w, cvn=cvn):
                e.wait_ge(b.CV.h, cvn)
                ps = b.psf[4]
                for i in range(8):
                    ins = e.matmul(ps.ap()[0:1, 0:w], sb["onecb"][:, 0:1],
                                   sb["bigb"][:, VE + i * 1040 + colh * 512:
                                              VE + i * 1040 + colh * 512 + w],
                                   start=(i == 0), stop=(i == 7))
                ins.then_inc(b.CT.h, 1)
            b.t(tvs)
            b.CT.n += 1
            ctn = b.CT.n

            def vvs(e, ctn=ctn, colh=colh, w=w):
                e.wait_ge(b.CT.h, ctn)
                e.tensor_tensor(sb["qrow"][0:1, colh * 512: colh * 512 + w],
                                sb["qrow"][0:1, colh * 512: colh * 512 + w],
                                b.psf[4].ap()[0:1, 0:w], ALU.add).then_inc(b.CV.h, 1)
            b.v(vvs)
            b.CV.n += 1
        if not hasattr(b, "s1_vext_free"):
            b.s1_vext_free = {}
        b.s1_vext_free[q + 1] = b.CV.n
        b.ps45_cv = b.CV.n
        ct_q[q] = b.CT.n

    vstep(b, lambda e: e.tensor_copy(sb["red1"][:, :], sb["dmax"][:, :]))


def p_head_sweep1(b, head, kb, base, kfb, cs_k):
    sb = b.sb
    KFW = 257
    ctd = {}
    cs0 = b.CS.n
    for i in range(8):
        ps = b.psf[4 + (i % 2)]

        def td(e, i=i, ps=ps, cs_k=cs_k, kb=kb, base=base,
               pcs=b.ps45_cs, pcv=b.ps45_cv, cs0=cs0):
            if i == 0:
                e.wait_ge(b.CV.h, cs_k)
            if i < 2:
                e.wait_ge(b.CS.h, pcs)
                e.wait_ge(b.CV.h, pcv)
            if i >= 2:
                e.wait_ge(b.CS.h, cs0 + i - 1)
            e.matmul(ps.ap()[:, 0:257],
                     sb["bigb"][base:base + 64,
                                KP + kb * 1024 + i * 128: KP + kb * 1024 + (i + 1) * 128],
                     sb["projc2"][base:base + 64, 0:257],
                     start=True, stop=False)
            e.matmul(ps.ap()[:, 0:257],
                     sb["bigb"][base:base + 64,
                                KQ + kb * 1024 + i * 128: KQ + kb * 1024 + (i + 1) * 128],
                     sb["nhalf2"][base:base + 64, 0:257],
                     start=False, stop=True).then_inc(b.CT.h, 1)
        b.t(td)
        b.CT.n += 1
        ctd[i] = b.CT.n

    def se(e, ctd=dict(ctd), kfb=kfb):
        for i in range(8):
            e.wait_ge(b.CT.h, ctd[i])
            ps = b.psf[4 + (i % 2)]
            e.activation(sb["bigb"][:, KF + kfb * 2056 + i * KFW:
                                    KF + kfb * 2056 + i * KFW + KFW],
                         ps.ap()[:, 0:257], AF.Exp).then_inc(b.CS.h, 1)
    b.s(se)
    b.CS.n += 8
    cs_last = b.CS.n
    b.ps45_cs = cs_last

    wait(b, "v", b.CS, cs_last)
    vstep(b, lambda e, kfb=kfb: e.tensor_reduce(
        sb["red1"][:, :],
        sb["bigb"][:, KF + kfb * 2056: KF + kfb * 2056 + 2056]
        .rearrange("p (i o) -> p i o", o=KFW)[:, :, 256:257].transpose([0, 2, 1]),
        axis=AX, op=ALU.max))
    vstep(b, lambda e: e.tensor_tensor(sb["dmax"][:, :], sb["dmax"][:, :],
                                       sb["red1"][:, :], ALU.max))

    cv_prev = b.ctx_evac_cv

    def tc(e, cs_last=cs_last, head=head, kfb=kfb, cv_prev=cv_prev):
        e.wait_ge(b.CS.h, cs_last)
        e.wait_ge(b.CV.h, cv_prev)
        for i in range(8):
            ins = e.matmul(b.psf[6].ap()[0:65, 0:256],
                           sb["bigb"][:, VE + i * 1040 + head * 65: VE + i * 1040 + head * 65 + 65],
                           sb["bigb"][:, KF + kfb * 2056 + i * KFW: KF + kfb * 2056 + i * KFW + 256],
                           start=(i == 0), stop=(i == 7))
        ins.then_inc(b.CT.h, 1)
    b.t(tc)
    b.CT.n += 1
    ctc = b.CT.n

    def vc(e, ctc=ctc, head=head):
        e.wait_ge(b.CT.h, ctc)
        e.tensor_tensor(sb["ctx_sb"][0:65, head * 256:(head + 1) * 256],
                        sb["ctx_sb"][0:65, head * 256:(head + 1) * 256],
                        b.psf[6].ap()[0:65, 0:256], ALU.add).then_inc(b.CV.h, 1)
    b.v(vc)
    b.CV.n += 1
    b.ctx_evac_cv = b.CV.n


def p_ar_ctx(b):
    sb, sd = b.sb, b.sd
    cvn = b.CV.n
    CW = 65 * HEADS * 256

    def g1(e, cvn=cvn):
        e.wait_ge(b.CV.h, cvn)
        e.dma_start(out=sd["arm_i"].ap(),
                    in_=sb["trow"][0:1, 4:5]).then_inc(b.DG.h, 16)
        e.dma_start(out=sd["arc_i"][0:1, 0:CW].rearrange("o (p w) -> (o p) w", p=65),
                    in_=sb["ctx_sb"][:, :]).then_inc(b.DG.h, 16)
        e.dma_start(out=sd["arc_i"][0:1, CW:CW + 1040],
                    in_=sb["qrow"][:, :]).then_inc(b.DG.h, 16)
    b.g(g1)
    b.DG.n += 48
    dgn = b.DG.n

    def g2(e, dgn=dgn):
        e.wait_ge(b.DG.h, dgn)
        e.collective_compute("AllReduce", ALU.max,
                             replica_groups=[[0, 1, 2, 3, 4, 5, 6, 7]],
                             ins=[sd["arm_i"].ap()], outs=[sd["arm_o"].ap()]) \
            .then_inc(b.CC.h, 1)
        e.collective_compute("AllReduce", ALU.add,
                             replica_groups=[[0, 1, 2, 3], [4, 5, 6, 7]],
                             ins=[sd["arc_i"].ap()],
                             outs=[sd["arc_o"].ap()]) \
            .then_inc(b.CC.h, 1)
    b.g(g2)
    b.CC.n += 2
    ccn = b.CC.n

    def g3(e, ccn=ccn):
        e.wait_ge(b.CC.h, ccn)
        e.dma_start(out=sb["trow"][0:1, 5:6], in_=sd["arm_o"].ap()).then_inc(b.DG.h, 16)
        e.dma_start(out=sb["ctx_sb"][:, :],
                    in_=sd["arc_o"][0:1, 0:CW].rearrange("o (p w) -> (o p) w", p=65)).then_inc(b.DG.h, 16)
        e.dma_start(out=sb["qrow"][:, :],
                    in_=sd["arc_o"][0:1, CW:CW + 1040]).then_inc(b.DG.h, 16)
    b.g(g3)
    b.DG.n += 48
    dgn2 = b.DG.n

    def g4(e, dgn2=dgn2):
        e.wait_ge(b.DG.h, dgn2)
        e.nop().then_inc(b.CG.h, 1)
    b.g(g4)
    b.CG.n += 1


def p_ctx_fix(b):
    sb = b.sb
    wait(b, "v", b.CG, b.CG.n)
    vstep(b, lambda e: e.reciprocal(sb["trow"][0:1, 11:12], sb["trow"][0:1, 5:6]))
    vstep(b, lambda e: e.tensor_scalar(sb["trow"][0:1, 11:12], sb["trow"][0:1, 11:12],
                                       MIS, None, ALU.mult))
    bcast_exact(b, 11, sb["pad"], 4, rows=65)
    wait(b, "v", b.CG, b.CG.n)

    cv0 = b.CV.n
    v1_cv, v3_cv, t1_ct, t2_ct, t3_ct = {}, {}, {}, {}, {}

    def emit_t1(head):
        pk = 5 - (head % 2)
        cw = b.CV.n

        def t1(e, head=head, pk=pk, cw=cw):
            e.wait_ge(b.CV.h, cw)
            e.transpose(b.psf[pk].ap()[0:65, 0:1],
                        sb["qrow"][0:1, head * 65:(head + 1) * 65],
                        sb["identf"][0:1, 0:1]).then_inc(b.CT.h, 1)
        b.t(t1)
        b.CT.n += 1
        t1_ct[head] = b.CT.n

    def emit_v1_ts_stage(head):
        pk = 5 - (head % 2)
        ctn = t1_ct[head]

        def v1(e, ctn=ctn, pk=pk):
            e.wait_ge(b.CT.h, ctn)
            e.tensor_scalar(sb["dcol"][0:65, 0:1], b.psf[pk].ap()[0:65, 0:1],
                            MIS * KEPS, None, ALU.mult).then_inc(b.CV.h, 1)
        b.v(v1)
        b.CV.n += 1
        v1_cv[head] = b.CV.n
        vstep(b, lambda e, head=head: e.tensor_scalar(
            sb["ctx_sb"][0:65, head * 256:(head + 1) * 256],
            sb["ctx_sb"][0:65, head * 256:(head + 1) * 256],
            sb["pad"][0:65, 4:5], sb["dcol"][0:65, 0:1], ALU.mult, ALU.add))

    def emit_t2(head):
        pq = 3 - (head % 2)
        cvn2 = b.CV.n

        def t2(e, cvn2=cvn2, head=head, pq=pq):
            e.wait_ge(b.CV.h, cvn2)
            e.transpose(b.psf[pq].ap()[0:128, 0:65],
                        sb["ctx_sb"][0:65, head * 256:head * 256 + 128],
                        sb["identf"][0:65, 0:65])
            e.transpose(b.psf[pq].ap()[0:128, 128:193],
                        sb["ctx_sb"][0:65, head * 256 + 128:head * 256 + 256],
                        sb["identf"][0:65, 0:65]).then_inc(b.CT.h, 1)
        b.t(t2)
        b.CT.n += 1
        t2_ct[head] = b.CT.n

    def emit_v3(head):
        pq = 3 - (head % 2)
        ctn2 = t2_ct[head]

        def v3(e, ctn2=ctn2, head=head, pq=pq):
            e.wait_ge(b.CT.h, ctn2)
            e.tensor_copy(sb["ctxT"][:, head * 130: head * 130 + 65],
                          b.psf[pq].ap()[0:128, 0:65])
            e.tensor_copy(sb["ctxT"][:, head * 130 + 65: head * 130 + 130],
                          b.psf[pq].ap()[0:128, 128:193]).then_inc(b.CV.h, 1)
        b.v(v3)
        b.CV.n += 1
        v3_cv[head] = b.CV.n

    def emit_t3(head):
        pk = 5 - (head % 2)
        cvn3 = b.CV.n

        def t3(e, cvn3=cvn3, head=head, pk=pk):
            e.wait_ge(b.CV.h, cvn3)
            e.matmul(b.psf[pk].ap()[0:1, 256:321], sb["onecb"][:, 0:1],
                     sb["ctxT"][:, head * 130: head * 130 + 65],
                     start=True, stop=False)
            e.matmul(b.psf[pk].ap()[0:1, 256:321], sb["onecb"][:, 0:1],
                     sb["ctxT"][:, head * 130 + 65: head * 130 + 130],
                     start=False, stop=True).then_inc(b.CT.h, 1)
        b.t(t3)
        b.CT.n += 1
        t3_ct[head] = b.CT.n

    def emit_v4(head):
        pk = 5 - (head % 2)
        ctn3 = t3_ct[head]

        def v4(e, ctn3=ctn3, head=head, pk=pk):
            e.wait_ge(b.CT.h, ctn3)
            e.tensor_scalar(sb["eps_row"][0:1, head * 65:(head + 1) * 65],
                            b.psf[pk].ap()[0:1, 256:321], KEPS, None, ALU.mult) \
                .then_inc(b.CV.h, 1)
        b.v(v4)
        b.CV.n += 1

    for head in range(HEADS):
        emit_t1(head)
        emit_v1_ts_stage(head)
        emit_t2(head)
        if head >= 1:
            emit_v3(head - 1)
            emit_t3(head - 1)
            emit_v4(head - 1)
    emit_v3(HEADS - 1)
    emit_t3(HEADS - 1)
    emit_v4(HEADS - 1)


def p_sweep2(b):
    sb, sd = b.sb, b.sd
    NQ = b.NQ
    QP = 0
    QF = 2 * 1024
    AT = QF + 2 * 2048

    for hc in range(HC):
        def ld(e, hc=hc):
            e.dma_start(out=sb["biga"][:, hc * 1024:(hc + 1) * 1024],
                        in_=sd["s_wqn"][0:H, hc * 128:(hc + 1) * 128],
                        transpose=True).then_inc(b.D.h, 16)
        b.y(ld)
        b.D.n += 16
    for hc in range(HC):
        def ld2(e, hc=hc):
            e.dma_start(out=sb["biga"][:, 8192 + hc * 1024: 9216 + hc * 1024],
                        in_=sd["s_won"][0:H, hc * 128:(hc + 1) * 128],
                        transpose=True).then_inc(b.D.h, 16)
        b.y(ld2)
        b.D.n += 16
    d_w = b.D.n

    ct_q = {}
    at_store = {}
    b.s2_prev_csqf = b.CS.n
    b.s2_prev_atev = b.CV.n
    for q in range(NQ):
        qb = sb["bigd"] if q % 2 == 0 else sb["bigc"]
        dq = b.DQ0 if q % 2 == 0 else b.DQ1
        for hc in range(HC):
            def xl(e, hc=hc, q=q, qb=qb, dq=dq):
                if hc == 0 and q >= 2:
                    e.wait_ge(b.CT.h, ct_q[q - 2])
                e.dma_start(out=qb[:, hc * 1024:(hc + 1) * 1024],
                            in_=sd["s_ln2"][q * 1024:(q + 1) * 1024,
                                            hc * 128:(hc + 1) * 128],
                            transpose=True).then_inc(dq.h, 16)
            b.y(xl)
            dq.n += 16
        d_x = dq.n

        qdone = {}
        at_evac = {}
        for pr in range(NPAIR):
            kb = pr % 2

            def tq(e, pr=pr, kb=kb, d_x=d_x, q=q, patev=b.s2_prev_atev, dq=dq):
                if pr == 0:
                    e.wait_ge(b.D.h, d_w)
                    e.wait_ge(dq.h, d_x)
                    if q >= 1:
                        e.wait_ge(b.DS.h, at_store[q - 1])
                if pr >= 2:
                    e.wait_ge(b.CV.h, qdone[pr - 2])
                e.wait_ge(b.CV.h, patev)
                for g in range(2):
                    ps = b.psf[0 + g]
                    for hc in range(HC):
                        ins = e.matmul(ps.ap()[:, 0:512],
                                       sb["biga"][:, hc * 1024 + pr * 128: hc * 1024 + (pr + 1) * 128],
                                       qb[:, hc * 1024 + g * 512: hc * 1024 + g * 512 + 512],
                                       start=(hc == 0), stop=(hc == HC - 1))
                ins.then_inc(b.CT.h, 1)
            b.t(tq)
            b.CT.n += 1
            ctq = b.CT.n

            def sq(e, ctq=ctq, pr=pr, kb=kb):
                e.wait_ge(b.CT.h, ctq)
                for g in range(2):
                    ps = b.psf[0 + g]
                    ins = e.tensor_scalar(sb["bigb"][:, QP + kb * 1024 + g * 512:
                                                     QP + kb * 1024 + g * 512 + 512],
                                          ps.ap()[:, 0:512],
                                          sb["b_tab"][:, pr: pr + 1], None, ALU.add)
                ins.then_inc(b.CV.h, 1)
            b.v(sq)
            b.CV.n += 1
            qdone[pr] = b.CV.n

            for hh in range(2):
                head = 2 * pr + hh
                p_head_sweep2(b, q, head, kb, hh * 64, head % 2,
                              qdone[pr], QP, QF, AT, at_evac)

        cv_at = b.CV.n

        def sat(e, cv_at=cv_at, q=q):
            e.wait_ge(b.CV.h, cv_at)
            for i in range(8):
                e.dma_start(out=sd["s_att"][q * 1024 + i * 128: q * 1024 + (i + 1) * 128, :],
                            in_=sb["bigb"][:, AT + i * 1024: AT + (i + 1) * 1024]) \
                    .then_inc(b.DS.h, 16)
        b.s(sat)
        b.DS.n += 128
        at_store[q] = b.DS.n
        ct_q[q] = b.CT.n
    b.s2_at_store = at_store


def p_head_sweep2(b, q, head, kb, base, qfb, cs_q, QP, QF, AT, at_evac):
    sb = b.sb
    ctd = {}
    cs0 = b.CS.n
    for mc in range(2):
        for g in range(2):
            ps = b.psf[2 + 2 * mc + g]

            def td(e, mc=mc, g=g, ps=ps, cs_q=cs_q, kb=kb, base=base, head=head,
                   pcs=b.s2_prev_csqf):
                if mc == 0 and g == 0:
                    e.wait_ge(b.CV.h, cs_q)
                    e.wait_ge(b.CS.h, pcs)
                e.matmul(ps.ap()[:, 0:512],
                         sb["projc2"][base:base + 64, mc * 128:(mc + 1) * 128],
                         sb["bigb"][base:base + 64,
                                    QP + kb * 1024 + g * 512: QP + kb * 1024 + g * 512 + 512],
                         start=True, stop=True).then_inc(b.CT.h, 1)
            b.t(td)
            b.CT.n += 1
            ctd[(mc, g)] = b.CT.n

    def se(e, ctd=dict(ctd), qfb=qfb):
        for mc in range(2):
            for g in range(2):
                e.wait_ge(b.CT.h, ctd[(mc, g)])
                ps = b.psf[2 + 2 * mc + g]
                e.activation(sb["bigb"][:, QF + qfb * 2048 + mc * 1024 + g * 512:
                                        QF + qfb * 2048 + mc * 1024 + g * 512 + 512],
                             ps.ap()[:, 0:512], AF.Exp).then_inc(b.CS.h, 1)
    b.s(se)
    b.CS.n += 4
    cs_qf = b.CS.n
    b.s2_prev_csqf = cs_qf

    for i in range(8):
        bank = b.psf[6] if i < 4 else b.psf[1]
        off = (i % 4) * 128

        def to(e, i=i, bank=bank, off=off, cs_qf=cs_qf, qfb=qfb, head=head,
               patev=b.s2_prev_atev):
            if i == 0:
                e.wait_ge(b.CS.h, cs_qf)
                e.wait_ge(b.CV.h, patev)
            if i == 4:
                e.wait_ge(b.CV.h, at_evac.get((head, 7), patev))
            g = i // 4
            loc = (i % 4) * 128
            for mc in range(2):
                e.matmul(bank.ap()[:, off: off + 65],
                         sb["bigb"][:, QF + qfb * 2048 + mc * 1024 + g * 512 + loc:
                                    QF + qfb * 2048 + mc * 1024 + g * 512 + loc + 128],
                         sb["ctxT"][:, head * 130 + mc * 65: head * 130 + (mc + 1) * 65],
                         start=(mc == 0), stop=False)
            e.matmul(bank.ap()[:, off: off + 65], sb["onerb"][0:1, 0:128],
                     sb["eps_row"][0:1, head * 65:(head + 1) * 65],
                     start=False, stop=True).then_inc(b.CT.h, 1)
        b.t(to)
        b.CT.n += 1
        if i % 4 == 3:
            grp = i // 4
            bank2 = bank
            cto = b.CT.n

            def ve(e, cto=cto, bank2=bank2):
                e.wait_ge(b.CT.h, cto)
                e.reciprocal(sb["dcol"][:, 0:1], bank2.ap()[:, 64:65])
                e.reciprocal(sb["dcol"][:, 1:2], bank2.ap()[:, 192:193])
                e.reciprocal(sb["dcol"][:, 2:3], bank2.ap()[:, 320:321])
                e.reciprocal(sb["dcol"][:, 3:4], bank2.ap()[:, 448:449]) \
                    .then_inc(b.CV.h, 1)
            b.v(ve)
            b.CV.n += 1
            for k in range(4):
                i2 = grp * 4 + k
                vstep(b, lambda e, i2=i2, bank2=bank2, k=k, head=head: e.tensor_scalar(
                    sb["bigb"][:, AT + i2 * 1024 + head * 64: AT + i2 * 1024 + head * 64 + 64],
                    bank2.ap()[:, k * 128: k * 128 + 64], sb["dcol"][:, k:k + 1],
                    None, ALU.mult))
                at_evac[(head, i2)] = b.CV.n
    b.s2_prev_atev = b.CV.n


def p_wo(b, do_stats=True):
    sb, sd = b.sb, b.sd
    NQ = b.NQ
    NCH = b.NCH

    dg0 = b.DG.n
    xevac = {}
    stats_cv = {}

    xdn = {}

    def xload(i, wait_cv):
        gx = b.GXA if i % 2 == 0 else b.GXB

        def xle(e, i=i, wait_cv=wait_cv, gx=gx):
            if wait_cv:
                e.wait_ge(b.CV.h, wait_cv)
            e.dma_start(out=sb["xstg"][:, (i % 2) * H:(i % 2 + 1) * H],
                        in_=sd["s_x2"][i * 128:(i + 1) * 128, :]).then_inc(gx.h, 16)
        b.g(xle)
        gx.n += 16
        xdn[i] = (gx, gx.n)

    xload(0, 0)
    xload(1, 0)

    stores = {}
    ct_q = {}
    for q in range(NQ):
        qb = sb["bigd"] if q % 2 == 0 else sb["bigc"]
        dq = b.DQ0 if q % 2 == 0 else b.DQ1
        for hc in range(HC):
            def xl(e, hc=hc, q=q, qb=qb, dq=dq):
                if hc == 0:
                    if q == 0:
                        e.wait_ge(b.DS.h, b.s2_at_store[b.NQ - 1])
                    elif q >= 2:
                        e.wait_ge(b.CT.h, ct_q[q - 2])
                e.dma_start(out=qb[:, hc * 1024:(hc + 1) * 1024],
                            in_=sd["s_att"][q * 1024:(q + 1) * 1024,
                                            hc * 128:(hc + 1) * 128],
                            transpose=True).then_inc(dq.h, 16)
            b.y(xl)
            dq.n += 16
        d_x = dq.n

        for i in range(8):
            nch = q * 8 + i
            psA = b.psf[0 + 2 * (i % 2)]
            psB = b.psf[1 + 2 * (i % 2)]

            def tw(e, i=i, nch=nch, psA=psA, psB=psB, d_x=d_x, qb=qb, dq=dq):
                if i == 0:
                    e.wait_ge(dq.h, d_x)
                if nch >= 2:
                    e.wait_ge(b.CV.h, stats_cv.get(nch - 2, xevac[nch - 2]))
                for half, ps in ((0, psA), (1, psB)):
                    for oc in range(HC):
                        e.matmul(ps.ap()[:, 0:512],
                                 qb[:, oc * 1024 + i * 128: oc * 1024 + (i + 1) * 128],
                                 sb["biga"][:, 8192 + oc * 1024 + half * 512:
                                            8192 + oc * 1024 + half * 512 + 512],
                                 start=(oc == 0), stop=False)
                    ins = e.matmul(ps.ap()[:, 0:512], sb["onerb"][0:1, 0:128],
                                   sb["brow"][0:1, half * 512: half * 512 + 512],
                                   start=False, stop=True)
                ins.then_inc(b.CT.h, 1)
            b.t(tw)
            b.CT.n += 1
            ctn = b.CT.n
            ob_bf = sb["ostg"].ap().bitcast(BF16)

            def vv(e, ctn=ctn, nch=nch, psA=psA, psB=psB):
                e.wait_ge(b.CT.h, ctn)
                gx2, gv2 = xdn[nch]
                e.wait_ge(gx2.h, gv2)
                if nch >= 2:
                    ss2, sv2 = stores[nch - 2]
                    e.wait_ge(ss2.h, sv2)
                    if nch - 2 in stats_cv:
                        e.wait_ge(b.CV.h, stats_cv[nch - 2])
                xb = (nch % 2) * H
                ob = (nch % 2)
                e.scalar_tensor_tensor(ob_bf[:, ob * 2 * H: ob * 2 * H + 512],
                                       psA.ap()[:, 0:512], 1.0,
                                       sb["xstg"][:, xb: xb + 512], ALU.mult, ALU.add)
                e.scalar_tensor_tensor(ob_bf[:, ob * 2 * H + 512: ob * 2 * H + 1024],
                                       psB.ap()[:, 0:512], 1.0,
                                       sb["xstg"][:, xb + 512: xb + 1024],
                                       ALU.mult, ALU.add).then_inc(b.CV.h, 1)
            b.v(vv)
            b.CV.n += 1
            xevac[nch] = b.CV.n
            if do_stats:
                obq = (nch % 2)
                ch = ob_bf[:, obq * 2 * H: obq * 2 * H + 1024]
                vstep(b, lambda e, ch=ch, nch=nch: e.tensor_reduce(
                    sb["m_tab"][:, nch:nch + 1], ch, axis=AX, op=ALU.add))
                vstep(b, lambda e, ch=ch: e.tensor_tensor(
                    sb["sqt"][:, :], ch, ch, ALU.mult))
                vstep(b, lambda e, nch=nch: e.tensor_reduce(
                    sb["v_tab"][:, nch:nch + 1], sb["sqt"][:, :],
                    axis=AX, op=ALU.add))
                vstep(b, lambda e, ch=ch, nch=nch: e.tensor_reduce(
                    sb["xmx"][:, nch:nch + 1], ch, axis=AX, op=ALU.max))
                vstep(b, lambda e, ch=ch, nch=nch: e.tensor_reduce(
                    sb["xmn"][:, nch:nch + 1], ch, axis=AX, op=ALU.min))
                stats_cv[nch] = b.CV.n
            cvn = b.CV.n

            ssm = b.SSA if nch % 2 == 0 else b.SSB

            def st(e, cvn=cvn, nch=nch, ssm=ssm):
                e.wait_ge(b.CV.h, cvn)
                ob = (nch % 2)
                e.dma_start(out=sd["s_x3"][nch * 128:(nch + 1) * 128, :],
                            in_=sb["ostg"].ap().bitcast(BF16)[:, ob * 2 * H: ob * 2 * H + 1024]) \
                    .then_inc(ssm.h, 16)
            b.s(st)
            ssm.n += 16
            stores[nch] = (ssm, ssm.n)
            if nch + 2 < NCH:
                xload(nch + 2, xevac[nch])
        ct_q[q] = b.CT.n


def p_finish(b):
    if b.dbg_t:
        sd = b.sd

        def gdump(e):
            e.dma_start(out=b.dbg_t["d_x2"].ap(), in_=sd["s_x2"].ap()).then_inc(b.DG.h, 16)
            e.dma_start(out=b.dbg_t["d_x3"].ap(), in_=sd["s_x3"].ap()).then_inc(b.DG.h, 16)
            e.dma_start(out=b.dbg_t["d_ln2"].ap(), in_=sd["s_ln2"].ap()).then_inc(b.DG.h, 16)
            e.dma_start(out=b.dbg_t["d_xq"].ap(), in_=sd["s_xq"].ap()).then_inc(b.DG.h, 16)
            e.dma_start(out=b.dbg_t["d_h"].ap(), in_=sd["s_h"].ap()).then_inc(b.DG.h, 16)
            e.dma_start(out=b.dbg_t["d_att"].ap(), in_=sd["s_att"].ap()).then_inc(b.DG.h, 16)
            e.dma_start(out=b.dbg_t["d_trow"][0:1, 0:12], in_=b.sb["trow"][0:1, 0:12]).then_inc(b.DG.h, 16)
            e.dma_start(out=b.dbg_t["d_ctx"].ap(), in_=b.sb["ctx_sb"][:, :]).then_inc(b.DG.h, 16)
            e.dma_start(out=b.dbg_t["d_qrow"].ap(), in_=b.sb["qrow"][:, :]).then_inc(b.DG.h, 16)
        b.g(gdump)
        b.DG.n += 144
    sems = [b.D, b.DS, b.DG, b.DXA, b.DXB, b.GXA, b.GXB, b.SSA, b.SSB, b.DH, b.DHB, b.DW, b.DQ0, b.DQ1, b.GSA, b.GSB]
    vals = [s.n for s in sems]

    def y1(e):
        for sm, v in zip(sems, vals):
            e.wait_ge(sm.h, v)
    b.y(y1)



_CACHE = {}


def kernel(L0_attn_hidden_states, L0_attn_Wq, L0_attn_bq, L0_attn_Wk, L0_attn_bk,
           L0_attn_Wv, L0_attn_bv, L0_attn_Wo, L0_attn_bo, L0_attn_proj,
           L0_ffn_clip_x, L0_ffn_clip_down, L0_ffn_w_gate, L0_ffn_w_up,
           L0_ffn_w_down):
    x = np.ascontiguousarray(np.asarray(L0_attn_hidden_states, dtype=np.float32))
    Bz, N, Hd = x.shape
    assert Hd == H
    ncores = 8
    per_batch = ncores // Bz
    R = N // per_batch
    xf = x.reshape(Bz * N, Hd)

    if R not in _CACHE:
        _CACHE[R] = build(R)
    nc = _CACHE[R]

    com = {
        "wq": np.ascontiguousarray(L0_attn_Wq, dtype=np.float32),
        "wk": np.ascontiguousarray(L0_attn_Wk, dtype=np.float32),
        "wv": np.ascontiguousarray(L0_attn_Wv, dtype=np.float32),
        "wo": np.ascontiguousarray(L0_attn_Wo, dtype=np.float32),
        "bq": np.ascontiguousarray(L0_attn_bq, dtype=np.float32).reshape(1, H),
        "bk": np.ascontiguousarray(L0_attn_bk, dtype=np.float32).reshape(1, H),
        "bv": np.ascontiguousarray(L0_attn_bv, dtype=np.float32).reshape(1, H),
        "bo": np.ascontiguousarray(L0_attn_bo, dtype=np.float32).reshape(1, H),
        "proj": np.ascontiguousarray(L0_attn_proj, dtype=np.float32),
        "clip_x": np.ascontiguousarray(L0_ffn_clip_x, dtype=np.float32).reshape(1, 1),
        "clip_down": np.ascontiguousarray(L0_ffn_clip_down, dtype=np.float32).reshape(1, 1),
        "w_gate": np.ascontiguousarray(L0_ffn_w_gate, dtype=np.float32),
        "w_up": np.ascontiguousarray(L0_ffn_w_up, dtype=np.float32),
        "w_down": np.ascontiguousarray(L0_ffn_w_down, dtype=np.float32),
    }
    in_maps = []
    for c in range(ncores):
        m = dict(com)
        m["x_in"] = np.ascontiguousarray(xf[c * R:(c + 1) * R, :])
        in_maps.append(m)

    res = run_bass_kernel_spmd(nc, in_maps, core_ids=list(range(ncores)))
    out = np.concatenate([np.asarray(r["y"]) for r in res.results], axis=0)
    return out.reshape(Bz, N, Hd).astype(np.float32)
